# version 07
# baseline (speedup 1.0000x reference)
import numpy as np
import concourse.bass as bass
import concourse.mybir as mybir
from concourse.bass_utils import run_bass_kernel_spmd
from contextlib import ExitStack

F32 = mybir.dt.float32
BF16 = mybir.dt.bfloat16
I32 = mybir.dt.int32
U8 = mybir.dt.uint8
AF = mybir.ActivationFunctionType
ALU = mybir.AluOpType
AX = mybir.AxisListType

D = 2048
T = 2048
NH = 16
NKV = 4
HD = 128
NSLAB = 16
NTB = 4
SCALE = float(1.0 / np.sqrt(128.0))
WIN = 512
FRONT = 256
NEGINF = -1.0e9

_STAGES = 4
_STAGE_LIST = None
_DBG = None



def _deint(n_heads):
    p = []
    for h in range(n_heads):
        base = h * HD
        p.extend(base + 2 * f for f in range(64))
        p.extend(base + 2 * f + 1 for f in range(64))
    return np.asarray(p, dtype=np.int64)


def _rope_tables(theta):
    inv_freq = 1.0 / theta ** (np.arange(0, HD, 2)[: HD // 2].astype(np.float32) / HD)
    t = np.arange(T, dtype=np.float32)
    ang = np.outer(t, inv_freq)
    cos = np.cos(ang).T.astype(np.float32)
    sin = np.sin(ang).T.astype(np.float32)
    C2 = np.concatenate([cos, cos], axis=0)
    S2 = np.concatenate([-sin, sin], axis=0)
    return C2, S2


def host_prep(inp):
    g = {}
    pq = _deint(NH)
    pk = _deint(NKV)
    d1 = _deint(1)
    g["wq_a"] = np.ascontiguousarray(inp["L0_attn_a_wq"][:, pq])
    g["wk_a"] = np.ascontiguousarray(inp["L0_attn_a_wk"][:, pk])
    g["wv_a"] = np.ascontiguousarray(inp["L0_attn_a_wv"])
    g["wo_a"] = np.ascontiguousarray(inp["L0_attn_a_wo"])
    wq = np.asarray(inp["L0_attn_b_w_qkv"]).T
    g["wqkv_b"] = np.ascontiguousarray(np.concatenate(
        [wq[:, 0:2048][:, pq], wq[:, 2048:4096],
         wq[:, 4096:4608][:, pk], wq[:, 4608:5120]], axis=1))
    g["wo_b"] = np.ascontiguousarray(np.asarray(inp["L0_attn_b_w_o"]).T)
    g["qnw"] = np.ascontiguousarray(np.asarray(inp["L0_attn_b_q_norm_w"])[d1].reshape(128, 1))
    g["qnb"] = np.ascontiguousarray(np.asarray(inp["L0_attn_b_q_norm_b"])[d1].reshape(128, 1))
    g["knw"] = np.ascontiguousarray(np.asarray(inp["L0_attn_b_k_norm_w"])[d1].reshape(128, 1))
    g["knb"] = np.ascontiguousarray(np.asarray(inp["L0_attn_b_k_norm_b"])[d1].reshape(128, 1))
    def _u8(a):
        a = np.ascontiguousarray(np.asarray(a, dtype=np.int32))
        return a.view(np.uint8).reshape(a.shape + (4,))[..., 0]
    wt = _u8(inp["L0_ffn_gate_w"]).T.reshape(16, 128, 64, 128)
    g["gw_blk"] = np.ascontiguousarray(wt.transpose(2, 1, 0, 3))
    wt = _u8(inp["L0_ffn_up_w"]).T.reshape(16, 128, 64, 128)
    g["uw_blk"] = np.ascontiguousarray(wt.transpose(2, 1, 0, 3))
    wt = _u8(inp["L0_ffn_down_w"]).T.reshape(64, 128, 16, 128)
    g["dw_blk"] = np.ascontiguousarray(wt.transpose(2, 1, 0, 3))
    g["gsc"] = np.ascontiguousarray(np.asarray(inp["L0_ffn_gate_scale"]).reshape(64, 128).T)
    g["usc"] = np.ascontiguousarray(np.asarray(inp["L0_ffn_up_scale"]).reshape(64, 128).T)
    g["dsc"] = np.ascontiguousarray(np.asarray(inp["L0_ffn_down_scale"]).reshape(16, 128).T)
    C2a, S2a = _rope_tables(10000.0)
    C2b, S2b = _rope_tables(1000000.0)
    g["tab_a"] = np.ascontiguousarray(np.concatenate([C2a, S2a], axis=1))
    g["tab_b"] = np.ascontiguousarray(np.concatenate([C2b, S2b], axis=1))
    r = np.arange(128)
    P1 = np.where(r[None, :] >= r[:, None], 0.0, NEGINF).astype(np.float32)
    P2 = np.where(r[None, :] < r[:, None], 0.0, NEGINF).astype(np.float32)
    g["masks"] = np.ascontiguousarray(np.concatenate([P1, P2], axis=1))
    return g


def kv_plan(qb, windowed):
    out = []
    for jt in range(T // 128):
        subs = []
        for a in range(4):
            dd = qb * 512 + a * 128 - jt * 128
            if dd < 0:
                st = None
            elif not windowed or jt * 128 < FRONT:
                st = 1 if dd == 0 else 0
            else:
                if dd == 0:
                    st = 1
                elif dd <= WIN - 128:
                    st = 0
                elif dd == WIN:
                    st = 2
                else:
                    st = None
            subs.append(st)
        alive = [a for a in range(4) if subs[a] is not None]
        if not alive:
            continue
        a_lo, a_hi = min(alive), max(alive)
        assert alive == list(range(a_lo, a_hi + 1))
        out.append((jt, a_lo, a_hi, [(a, subs[a]) for a in alive if subs[a]]))
    return out



ENGS = ("tensor", "vector", "scalar", "sync", "gpsimd")


class Prog:
    def __init__(self, nc, es):
        self.nc = nc
        self.es = es
        self.ops = {e: [] for e in ENGS}
        self.engsem = {}
        self.cnt = {}
        self.streams = []
        for e in ("tensor", "vector", "scalar"):
            self.engsem[e] = self.new_sem("pg_" + e)
            self.cnt[e] = 0
        self.lastwait = {e: {} for e in ENGS}

    def new_sem(self, name):
        return self.es.enter_context(self.nc.semaphore(name))

    def _expand(self, waits):
        out = []
        for w in waits:
            if w is None:
                continue
            if isinstance(w, tuple) and len(w) == 2 and w[0] == "MULTI":
                out.extend(x for x in w[1] if x is not None)
            else:
                out.append(w)
        return out

    def do(self, eng, fn, waits=(), tick=False):
        ws = []
        lw = self.lastwait[eng]
        for (sem, val, producer) in self._expand(waits):
            if producer == eng:
                continue
            k = id(sem)
            if lw.get(k, -1) >= val:
                continue
            lw[k] = val
            ws.append((sem, val))
        tok = None
        esem = self.engsem.get(eng)
        if tick:
            self.cnt[eng] += 1
            tok = (esem, self.cnt[eng], eng)

        def f(e, ws=ws, fn=fn, tick=tick, esem=esem):
            for (s, v) in ws:
                e.wait_ge(s, v)
            ins = fn(e)
            if tick:
                ins.then_inc(esem, 1)
        self.ops[eng].append(f)
        return tok

    def now(self, eng):
        return (self.engsem[eng], self.cnt[eng], eng)

    def barrier(self):
        toks = [self.now(e) for e in ("tensor", "vector", "scalar")]
        for st in self.streams:
            toks.append(st.wait_all())
        for e in ENGS:
            self.do(e, lambda ee: None, waits=toks)

    def replay(self, block):
        def mk(eng):
            def body(e):
                for f in self.ops[eng]:
                    f(e)
            return body
        block.tensor(mk("tensor"))
        block.vector(mk("vector"))
        block.scalar(mk("scalar"))
        block.sync(mk("sync"))
        block.gpsimd(mk("gpsimd"))


class Stream:

    def __init__(self, P, queue, name, R=2):
        self.P = P
        self.queue = queue
        self.R = R
        self.sems = [P.new_sem(f"{name}{i}") for i in range(R)]
        self.counts = [0] * R
        self.n = 0
        P.streams.append(self)

    def dma(self, out, in_, waits=(), **kw):
        i = self.n % self.R
        self.n += 1
        self.counts[i] += 16
        sem = self.sems[i]
        val = self.counts[i]

        def fn(e, out=out, in_=in_, kw=kw, sem=sem):
            return e.dma_start(out=out, in_=in_, **kw).then_inc(sem, 16)
        self.P.do(self.queue, fn, waits=waits)
        return (sem, val, "dma")

    def wait_all(self):
        return ("MULTI", [(self.sems[i], self.counts[i], "dma")
                          for i in range(self.R) if self.counts[i] > 0])



def build_program():
    nc = bass.Bass("TRN2", target_bir_lowering=False)

    def din(name, shape, dt):
        return nc.dram_tensor(name, shape, dt, kind="ExternalInput").ap()

    x_dm = din("x_dm", [D, T], F32)
    wq_a = din("wq_a", [D, 2048], F32)
    wk_a = din("wk_a", [D, 512], F32)
    wv_a = din("wv_a", [D, 512], F32)
    wo_a = din("wo_a", [2048, D], F32)
    wqkv_b = din("wqkv_b", [D, 5120], F32)
    wo_b = din("wo_b", [2048, D], F32)
    qnw_d = din("qnw", [128, 1], F32)
    qnb_d = din("qnb", [128, 1], F32)
    knw_d = din("knw", [128, 1], F32)
    knb_d = din("knb", [128, 1], F32)
    gw_blk = din("gw_blk", [64, 128, 16, 128], U8)
    uw_blk = din("uw_blk", [64, 128, 16, 128], U8)
    dw_blk = din("dw_blk", [16, 128, 64, 128], U8)
    gsc_d = din("gsc", [128, 64], F32)
    usc_d = din("usc", [128, 64], F32)
    dsc_d = din("dsc", [128, 16], F32)
    tab_a = din("tab_a", [128, 2 * T], F32)
    tab_b = din("tab_b", [128, 2 * T], F32)
    masks_d = din("masks", [128, 256], F32)

    R_dm = nc.dram_tensor("out_dm", [D, T], F32, kind="ExternalOutput").ap()

    h_scr = nc.dram_tensor("h_scr", [64, 128, 4, 512], BF16, kind="Internal").ap()
    gate_scr = nc.dram_tensor("gate_scr", [16, 128, 4, 512], BF16, kind="Internal").ap()

    es = ExitStack()
    with es:
        A1 = es.enter_context(nc.sbuf_tensor("A1", [128, 32768], BF16))
        A2 = es.enter_context(nc.sbuf_tensor("A2", [128, 32768], BF16))
        A3 = es.enter_context(nc.sbuf_tensor("A3", [128, 16384], BF16))
        A4 = es.enter_context(nc.sbuf_tensor("A4", [128, 16384], BF16))
        m_row = es.enter_context(nc.sbuf_tensor("m_row", [1, T], BF16))
        r_row = es.enter_context(nc.sbuf_tensor("r_row", [1, T], F32))

        tiny = es.enter_context(nc.sbuf_tensor("tiny", [1, 64], F32))
        negB = es.enter_context(nc.sbuf_tensor("negB", [128, 16], F32))
        masks_sb = es.enter_context(nc.sbuf_tensor("masks_sb", [128, 256], BF16))
        gsc_sb = es.enter_context(nc.sbuf_tensor("gsc_sb", [128, 64], F32))
        usc_sb = es.enter_context(nc.sbuf_tensor("usc_sb", [128, 64], F32))
        dsc_sb = es.enter_context(nc.sbuf_tensor("dsc_sb", [128, 16], F32))
        qnw_sb = es.enter_context(nc.sbuf_tensor("qnw_sb", [128, 1], F32))
        qnb_sb = es.enter_context(nc.sbuf_tensor("qnb_sb", [128, 1], F32))
        knw_sb = es.enter_context(nc.sbuf_tensor("knw_sb", [128, 1], F32))
        knb_sb = es.enter_context(nc.sbuf_tensor("knb_sb", [128, 1], F32))
        epsc = es.enter_context(nc.sbuf_tensor("epsc", [128, 2], F32))
        ones_c = es.enter_context(nc.sbuf_tensor("ones_c", [128, 1], BF16))
        ones_r = es.enter_context(nc.sbuf_tensor("ones_r", [1, 128], BF16))
        onesf_r = es.enter_context(nc.sbuf_tensor("onesf_r", [1, 128], F32))
        oneshk_r = es.enter_context(nc.sbuf_tensor("oneshk_r", [1, 128], F32))
        oneshk_c = es.enter_context(nc.sbuf_tensor("oneshk_c", [128, 1], F32))

        psh = [es.enter_context(nc.psum_tensor(f"ps{i}", [128, 512], F32)) for i in range(8)]
        bank_free = [None] * 8

        P = Prog(nc, es)
        bcast_free = [None, None, None, None]

        s_ld = Stream(P, "sync", "ld", R=2)
        s_ld2 = Stream(P, "scalar", "ld2", R=2)
        s_cast = Stream(P, "gpsimd", "cast", R=4)
        s_acc = Stream(P, "gpsimd", "acc", R=2)
        s_st = Stream(P, "sync", "st", R=2)
        s_swp = Stream(P, "gpsimd", "swp", R=2)

        scrA2 = [A4[:, 0:2048], A4[:, 2048:4096]]
        scrB2 = [A4[:, 4096:6144], A4[:, 6144:8192]]
        xsq0 = A4[:, 0:2048]
        xsq1 = A4[:, 2048:4096]
        tabs = A4[:, 8192:12288]
        attnout = A4[:, 0:8192]
        phat = [A4[:, 8192 + i * 512: 8704 + i * 512] for i in range(4)]
        gstage = [A4[:, 10240 + i * 512: 10752 + i * 512] for i in range(2)]
        ostage = [A4[:, 11264 + i * 1024: 12288 + i * 1024].bitcast(F32)
                  for i in range(2)]
        rzstage = [A4[:, 13312:14336].bitcast(F32),
                   A4[:, 14336:15360].bitcast(F32)]
        z_row = A4[0:1, 15360:16384].bitcast(F32)
        z16 = A4[0:1, 9728:10240]

        wgu = [A4[:, 8192 + i * 2048: 10240 + i * 2048] for i in range(4)]
        wgu_u8 = [A4[:, 3584 + i * 1024: 4608 + i * 1024].bitcast(U8)
                  for i in range(4)]
        wd_u8 = [A4[:, 0 + i * 4096: 4096 + i * 4096].bitcast(U8)
                 for i in range(2)]
        hstage = [A4[:, 2048 + i * 512: 2560 + i * 512] for i in range(3)]
        fstage = [A4[:, 0 + i * 1024: 1024 + i * 1024].bitcast(F32) for i in range(2)]
        wq_stage2 = [A4[:, 12288:14336], A4[:, 14336:16384]]
        bcast_sb = [A4[:, 12288 + i * 512: 12800 + i * 512] for i in range(4)]
        wv_stage = [A4[:, 12288 + i * 512: 12800 + i * 512] for i in range(2)]

        xn = A1
        wo_sb = A1
        init_stage = [A1[:, i * 4096:(i + 1) * 4096].bitcast(F32) for i in range(2)]
        qhat = A2
        h_sb = A2

        kT = A3[:, 0:8192]
        vT = A3[:, 8192:16384]
        wd_buf = [A3[:, i * 8192:(i + 1) * 8192] for i in range(2)]

        TE, VE, SE = "tensor", "vector", "scalar"

        def t_mm(out, lhsT, rhs, start, stop, waits=(), tick=False):
            return P.do(TE, lambda e: e.matmul(out, lhsT, rhs, start=start, stop=stop),
                        waits=waits, tick=tick)

        s_cast.dma(masks_sb[:], masks_d)
        for dst, src in ((gsc_sb, gsc_d), (usc_sb, usc_d), (dsc_sb, dsc_d),
                         (qnw_sb, qnw_d), (qnb_sb, qnb_d), (knw_sb, knw_d),
                         (knb_sb, knb_d)):
            s_ld.dma(dst[:], src)
        t_consts = s_ld.wait_all()
        P.do(VE, lambda e: e.memset(epsc[:, 0:1], 1e-5), tick=True)
        P.do(VE, lambda e: e.memset(epsc[:, 1:2], 1e-6), tick=True)
        P.do(VE, lambda e: e.memset(ones_c[:], 1.0), tick=True)
        P.do(VE, lambda e: e.memset(ones_r[:], 1.0), tick=True)
        P.do(VE, lambda e: e.memset(onesf_r[:], 1.0), tick=True)
        P.do(VE, lambda e: e.memset(oneshk_r[:], 1.0 / 128.0), tick=True)
        tok_ones = P.do(VE, lambda e: e.memset(oneshk_c[:], 1.0 / 128.0), tick=True)

        init_st_tok = [None, None]
        for k in range(NSLAB):
            tok = s_ld2.dma(init_stage[k % 2][:, 0:2048], x_dm[k * 128:(k + 1) * 128, :],
                            waits=[init_st_tok[k % 2]])
            init_st_tok[k % 2] = s_st.dma(R_dm[k * 128:(k + 1) * 128, :],
                                          init_stage[k % 2][:, 0:2048], waits=[tok])

        t_Rinit = s_st.wait_all()

        def dbg_dump(region, nslab=16):
            P.barrier()
            for k in range(nslab):
                s_cast.dma(R_dm[k * 128:(k + 1) * 128, :],
                           region[:, k * 2048:(k + 1) * 2048])
            P.barrier()

        def emit_ln(src_dm, extra_wait=()):
            for k in range(NSLAB):
                s_cast.dma(xn[:, k * 2048:(k + 1) * 2048],
                           src_dm[k * 128:(k + 1) * 128, :],
                           waits=list(extra_wait) if k == 0 else [])
            all_ld = s_cast.wait_all()
            mm_tok = None
            sq_prev = [None, None]
            for k in range(NSLAB):
                xk = xn[:, k * 2048:(k + 1) * 2048]
                xq = xsq0 if k % 2 == 0 else xsq1
                tok_s = P.do(SE, lambda e, xq=xq, xk=xk: e.activation(
                    out=xq, in_=xk, func=AF.Square),
                    waits=[all_ld, sq_prev[k % 2]], tick=True)
                for tb in range(NTB):
                    t_mm(psh[tb][0:1, :], ones_c[:], xk[:, tb * 512:(tb + 1) * 512],
                         start=(k == 0), stop=(k == NSLAB - 1),
                         waits=[all_ld, tok_ones] +
                               ([bank_free[tb]] if k == 0 else []))
                for tb in range(NTB):
                    mm_tok = t_mm(psh[4 + tb][0:1, :], ones_c[:],
                                  xq[:, tb * 512:(tb + 1) * 512],
                                  start=(k == 0), stop=(k == NSLAB - 1),
                                  waits=[tok_s] + ([bank_free[4 + tb]] if k == 0 else []),
                                  tick=(tb == NTB - 1))
                sq_prev[k % 2] = mm_tok
            row_toks = []
            for tb in range(NTB):
                row_toks.append(P.do(SE, lambda e, tb=tb: e.activation(
                    out=m_row[:, tb * 512:(tb + 1) * 512], in_=psh[tb][0:1, :],
                    func=AF.Copy, scale=1.0 / 2048.0), waits=[mm_tok], tick=True))
                row_toks.append(P.do(SE, lambda e, tb=tb: e.activation(
                    out=r_row[:, tb * 512:(tb + 1) * 512], in_=psh[4 + tb][0:1, :],
                    func=AF.Copy, scale=1.0 / 2048.0), waits=[mm_tok], tick=True))
            var_toks = []
            for tb in range(NTB):
                ta = P.do(VE, lambda e, tb=tb: e.tensor_mul(
                    psh[tb][0:1, :], m_row[:, tb * 512:(tb + 1) * 512],
                    m_row[:, tb * 512:(tb + 1) * 512]),
                    waits=row_toks, tick=True)
                tv = P.do(VE, lambda e, tb=tb: e.tensor_sub(
                    r_row[:, tb * 512:(tb + 1) * 512],
                    r_row[:, tb * 512:(tb + 1) * 512], psh[tb][0:1, :]),
                    waits=[ta], tick=True)
                var_toks.append(tv)
                bank_free[tb] = tv
            tok_sd = P.do(SE, lambda e: e.activation(out=r_row[:], in_=r_row[:],
                                                     func=AF.Ln, bias=epsc[0:1, 0:1]),
                          waits=var_toks, tick=True)
            tok_r = P.do(SE, lambda e: e.activation(out=r_row[:], in_=r_row[:],
                                                    func=AF.Exp, scale=-0.5),
                         waits=[tok_sd], tick=True)
            tok_mr = P.do(VE, lambda e: e.tensor_mul(m_row[:], m_row[:], r_row[:]),
                          waits=[tok_r], tick=True)
            bank_free[4] = tok_mr
            last = None
            xn_tb = []
            for tb in range(NTB):
                br, bm = (6, 7) if tb % 2 == 0 else (2, 3)
                t_r = t_mm(psh[br][:, :], onesf_r[:], r_row[:, tb * 512:(tb + 1) * 512],
                           start=True, stop=True, waits=[tok_r, bank_free[br]], tick=True)
                t_m = t_mm(psh[bm][:, :], ones_r[:], m_row[:, tb * 512:(tb + 1) * 512],
                           start=True, stop=True, waits=[tok_mr, bank_free[bm]], tick=True)
                rb = bcast_sb[tb % 2]
                mb = bcast_sb[2 + tb % 2]
                t_re = P.do(SE, lambda e, br=br, rb=rb: e.activation(
                    out=rb, in_=psh[br][:, :], func=AF.Copy),
                    waits=[t_r, bcast_free[tb % 2]], tick=True)
                bank_free[br] = t_re
                t_me = P.do(SE, lambda e, bm=bm, mb=mb: e.activation(
                    out=mb, in_=psh[bm][:, :], func=AF.Copy),
                    waits=[t_m, bcast_free[2 + tb % 2]], tick=True)
                bank_free[bm] = t_me
                for k in range(NSLAB):
                    xkb = xn[:, k * 2048 + tb * 512: k * 2048 + (tb + 1) * 512]
                    P.do(VE, lambda e, xkb=xkb, rb=rb: e.tensor_mul(xkb, xkb, rb),
                         waits=[t_re], tick=False)
                    last = P.do(VE, lambda e, xkb=xkb, mb=mb: e.tensor_sub(xkb, xkb, mb),
                                waits=[t_me], tick=(k == NSLAB - 1))
                bcast_free[tb % 2] = last
                bcast_free[2 + tb % 2] = last
                xn_tb.append(last)
            return last, xn_tb


        def emit_attn(which, src_dm, extra_wait=()):
            is_b = which == "b"
            tab_src = tab_b if is_b else tab_a
            wq_src = (wqkv_b[:, 0:2048] if is_b else wq_a)
            wk_src = (wqkv_b[:, 4096:4608] if is_b else wk_a)
            wv_src = (wqkv_b[:, 4608:5120] if is_b else wv_a)
            wo_src = wo_b if is_b else wo_a

            t1 = s_cast.dma(tabs[:, 0:2048], tab_src[:, 0:T])
            t2 = s_cast.dma(tabs[:, 2048:4096], tab_src[:, T:2 * T])
            tok_tabs = ("MULTI", [t1, t2])

            tok_xn, xn_tb = emit_ln(src_dm, extra_wait=extra_wait)
            if _DBG == "xn":
                dbg_dump(xn)
                return

            ev_tok = None
            wv_free = [tok_xn, tok_xn]
            for half in range(2):
                for k in range(NSLAB):
                    b = k % 2
                    tok_w = s_cast.dma(wv_stage[b][:, 0:512],
                                       wv_src[k * 128:(k + 1) * 128, :],
                                       waits=[wv_free[b]])
                    for j, tt in enumerate(range(half * 8, half * 8 + 8)):
                        last_mm = t_mm(
                            psh[j][:, :],
                            xn[:, k * 2048 + tt * 128: k * 2048 + tt * 128 + 128],
                            wv_stage[b][:, 0:512],
                            start=(k == 0), stop=(k == NSLAB - 1),
                            waits=[tok_w, xn_tb[tt // 4]] +
                                  ([bank_free[j]] if k == 0 else []),
                            tick=(j == 7))
                    wv_free[b] = last_mm
                    if k == NSLAB - 1:
                        for j, tt in enumerate(range(half * 8, half * 8 + 8)):
                            ev_tok = P.do(SE, lambda e, j=j, tt=tt: e.activation(
                                out=vT[:, tt * 512:(tt + 1) * 512], in_=psh[j][:, :],
                                func=AF.Copy), waits=[P.now(TE)], tick=True)
                            bank_free[j] = ev_tok
            tok_v = ev_tok
            wv_guard = ev_tok
            if _DBG == "v":
                dbg_dump(vT, nslab=4)
                return

            wq_free = [wv_guard, None]
            wq_i = [0]
            sfreeA = [[None] * NTB, [None] * NTB]
            sfreeB = [[None] * NTB, [None] * NTB]
            rows_free_m = [None] * NTB
            rows_free_r = [None] * NTB

            rope_par = [0]
            gstage_free = [None, None]

            def emit_proj(st):
                wb = wq_i[0] % 2
                wq_i[0] += 1
                is_gate = st.get("gate") is not None
                if not is_gate:
                    par = rope_par[0] % 2
                    rope_par[0] += 1
                    st["par"] = par
                    st["sA"] = scrA2[par]
                    st["sB"] = scrB2[par]
                wq_stage = wq_stage2[wb]
                st["wb"] = wb
                tok_w = s_cast.dma(wq_stage[:].rearrange("p (k c) -> p k c", k=16),
                                   st["w_cols"].rearrange("(k p) c -> p k c", p=128),
                                   waits=[wq_free[wb]])
                st["evac"] = []
                for tb in range(NTB):
                    sec = slice(tb * 512, (tb + 1) * 512)
                    j = tb % 2
                    last_mm = None
                    for k in range(NSLAB):
                        last_mm = t_mm(psh[j][:, :], wq_stage[:, k * 128:(k + 1) * 128],
                                       xn[:, k * 2048 + tb * 512: k * 2048 + (tb + 1) * 512],
                                       start=(k == 0), stop=(k == NSLAB - 1),
                                       waits=[tok_w, xn_tb[tb]] +
                                             ([bank_free[j]] if k == 0 else []),
                                       tick=(k == NSLAB - 1))
                    if is_gate:
                        gs = st["gate"]
                        gb = tb % 2
                        tok_e = P.do(SE, lambda e, j=j, gb=gb: e.activation(
                            out=gstage[gb], in_=psh[j][:, :], func=AF.Copy),
                            waits=[last_mm, gstage_free[gb]], tick=True)
                        gstage_free[gb] = s_st.dma(gate_scr[gs, :, tb, :], gstage[gb],
                                                   waits=[tok_e])
                    else:
                        tok_e = P.do(SE, lambda e, j=j, sA=st["sA"], sec=sec: e.activation(
                            out=sA[:, sec], in_=psh[j][:, :], func=AF.Copy),
                            waits=[last_mm, sfreeA[st["par"]][tb]], tick=True)
                    bank_free[j] = tok_e
                    st["evac"].append(tok_e)
                wq_free[wb] = st["evac"][-1]

            def emit_post(st):
                par = st["par"]
                sA, sB = st["sA"], st["sB"]
                ln_w, ln_b, dst, bound_idx = st["ln_w"], st["ln_b"], st["dst"], st["bidx"]
                tok_o = None
                tmax = None
                for tb in range(NTB):
                    sec = slice(tb * 512, (tb + 1) * 512)
                    tabC = tabs[:, tb * 512:(tb + 1) * 512]
                    tabS = tabs[:, 2048 + tb * 512: 2048 + (tb + 1) * 512]
                    tok_e = st["evac"][tb]
                    tok_q = tok_e
                    tok_sumsq = None
                    if ln_w is not None:
                        tok_sq = P.do(SE, lambda e, sA=sA, sB=sB, sec=sec: e.activation(
                            out=sB[:, sec], in_=sA[:, sec], func=AF.Square),
                            waits=[tok_e, sfreeB[par][tb]], tick=True)
                        tma = t_mm(psh[2][0:1, :], ones_c[:], sA[:, sec],
                                   start=True, stop=True,
                                   waits=[tok_e, bank_free[2]], tick=True)
                        te1 = P.do(SE, lambda e, sec=sec: e.activation(
                            out=m_row[0:1, sec], in_=psh[2][0:1, :],
                            func=AF.Copy, scale=1.0 / 128.0),
                            waits=[tma, rows_free_m[tb]], tick=True)
                        bank_free[2] = te1
                        tmb = t_mm(psh[3][0:1, :], ones_c[:], sB[:, sec],
                                   start=True, stop=True,
                                   waits=[tok_sq, bank_free[3]], tick=True)
                        tok_sumsq = tmb
                        te2 = P.do(SE, lambda e, sec=sec: e.activation(
                            out=r_row[0:1, sec], in_=psh[3][0:1, :],
                            func=AF.Copy, scale=1.0 / 128.0),
                            waits=[tmb, rows_free_r[tb]], tick=True)
                        bank_free[3] = te2
                        ta = P.do(VE, lambda e, sec=sec: e.tensor_mul(
                            psh[2][0:1, :], m_row[0:1, sec], m_row[0:1, sec]),
                            waits=[te1, bank_free[2]], tick=True)
                        tv = P.do(VE, lambda e, sec=sec: e.tensor_sub(
                            r_row[0:1, sec], r_row[0:1, sec], psh[2][0:1, :]),
                            waits=[ta, te2], tick=True)
                        bank_free[2] = tv
                        tsd = P.do(SE, lambda e, sec=sec: e.activation(
                            out=r_row[0:1, sec], in_=r_row[0:1, sec], func=AF.Ln,
                            bias=epsc[0:1, 1:2]), waits=[tv], tick=True)
                        tr = P.do(SE, lambda e, sec=sec: e.activation(
                            out=r_row[0:1, sec], in_=r_row[0:1, sec], func=AF.Exp,
                            scale=-0.5), waits=[tsd], tick=True)
                        tmr = P.do(VE, lambda e, sec=sec: e.tensor_mul(
                            m_row[0:1, sec], m_row[0:1, sec], r_row[0:1, sec]),
                            waits=[tr], tick=True)
                        t_r = t_mm(psh[6][:, :], onesf_r[:], r_row[0:1, sec],
                                   start=True, stop=True,
                                   waits=[tr, bank_free[6]], tick=True)
                        t_m = t_mm(psh[7][:, :], ones_r[:], m_row[0:1, sec],
                                   start=True, stop=True,
                                   waits=[tmr, bank_free[7]], tick=True)
                        rows_free_r[tb] = ("MULTI", [t_r, tmr])
                        rows_free_m[tb] = t_m
                        tn1 = P.do(VE, lambda e, sA=sA, sec=sec: e.tensor_mul(
                            sA[:, sec], sA[:, sec], psh[6][:, :]),
                            waits=[t_r], tick=True)
                        tn2 = P.do(VE, lambda e, sA=sA, sec=sec: e.tensor_sub(
                            sA[:, sec], sA[:, sec], psh[7][:, :]),
                            waits=[t_m, tn1], tick=True)
                        bank_free[6] = tn2
                        bank_free[7] = tn2
                        tok_q = P.do(SE, lambda e, sA=sA, sec=sec, ln_w=ln_w, ln_b=ln_b:
                                     e.activation(out=sA[:, sec], in_=sA[:, sec],
                                                  func=AF.Identity,
                                                  scale=ln_w[:, 0:1], bias=ln_b[:, 0:1]),
                                     waits=[tn2], tick=True)
                    swp_w = [tok_q]
                    if tok_sumsq is not None:
                        swp_w.append(tok_sumsq)
                    else:
                        swp_w.append(sfreeB[par][tb])
                    sw1 = s_swp.dma(sB[0:64, sec], sA[64:128, sec], waits=swp_w)
                    sw2 = s_swp.dma(sB[64:128, sec], sA[0:64, sec], waits=[tok_q])
                    tm1 = P.do(VE, lambda e, sA=sA, sec=sec, tabC=tabC: e.tensor_mul(
                        sA[:, sec], sA[:, sec], tabC),
                        waits=[tok_tabs, tok_q, sw1, sw2], tick=True)
                    tm2 = P.do(VE, lambda e, sB=sB, sec=sec, tabS=tabS: e.tensor_mul(
                        sB[:, sec], sB[:, sec], tabS),
                        waits=[sw1, sw2], tick=True)
                    tok_o = P.do(VE, lambda e, dst=dst, sA=sA, sB=sB, sec=sec:
                                 e.tensor_add(dst[:, sec], sA[:, sec], sB[:, sec]),
                                 waits=[tm1, tm2], tick=True)
                    sfreeA[par][tb] = tok_o
                    tsq2 = P.do(SE, lambda e, dst=dst, sB=sB, sec=sec: e.activation(
                        out=sB[:, sec], in_=dst[:, sec], func=AF.Square),
                        waits=[tok_o], tick=True)
                    tmb2 = t_mm(psh[3][0:1, :], ones_c[:], sB[:, sec],
                                start=True, stop=True,
                                waits=[tsq2, bank_free[3]], tick=True)
                    tred = P.do(VE, lambda e: e.reduce_max(
                        tiny[0:1, 32:33], psh[3][0:1, :], axis=AX.X),
                        waits=[tmb2], tick=True)
                    bank_free[3] = tred
                    sfreeB[par][tb] = tred
                    if tb == 0:
                        tmax = P.do(VE, lambda e, bi=bound_idx: e.tensor_copy(
                            tiny[0:1, bi:bi + 1], tiny[0:1, 32:33]), waits=[tred],
                            tick=True)
                    else:
                        tmax = P.do(VE, lambda e, bi=bound_idx: e.tensor_max(
                            tiny[0:1, bi:bi + 1], tiny[0:1, bi:bi + 1],
                            tiny[0:1, 32:33]), waits=[tred], tick=True)
                st["tok_o"] = tok_o
                st["tmax"] = tmax

            slabs = []
            kst, qst = [], []
            for g in range(NKV):
                st = {"w_cols": wk_src[:, g * 128:(g + 1) * 128],
                      "dst": kT[:, g * 2048:(g + 1) * 2048],
                      "ln_w": knw_sb if is_b else None,
                      "ln_b": knb_sb if is_b else None, "bidx": 16 + g}
                slabs.append(st)
                kst.append(st)
            for h in range(NH):
                st = {"w_cols": wq_src[:, h * 128:(h + 1) * 128],
                      "dst": qhat[:, h * 2048:(h + 1) * 2048],
                      "ln_w": qnw_sb if is_b else None,
                      "ln_b": qnb_sb if is_b else None, "bidx": h}
                slabs.append(st)
                qst.append(st)
                if is_b:
                    slabs.append({"w_cols": wqkv_b[:, 2048 + h * 128: 2048 + (h + 1) * 128],
                                  "gate": h})
            prev = None
            for st in slabs:
                emit_proj(st)
                if prev is not None:
                    emit_post(prev)
                prev = None if st.get("gate") is not None else st
            if prev is not None:
                emit_post(prev)
            ktoks = [(st["tok_o"], st["tmax"]) for st in kst]
            qtoks = [(st["tok_o"], st["tmax"]) for st in qst]
            if _DBG == "k":
                dbg_dump(kT, nslab=4)
                return
            if _DBG == "q":
                dbg_dump(qhat)
                return

            gate_toks = [gstage_free[0], gstage_free[1]]

            bound_toks = [t for (_, t) in qtoks] + [t for (_, t) in ktoks]
            nkv_view = tiny[0:1, 16:20].unsqueeze(2).broadcast_to([1, 4, 4])
            nq_view = tiny[0:1, 0:16].rearrange("a (b c) -> a b c", b=4)
            out_view = tiny[0:1, 40:56].rearrange("a (b c) -> a b c", b=4)
            t_b1 = P.do(VE, lambda e: e.tensor_mul(out_view, nq_view, nkv_view),
                        waits=bound_toks, tick=True)
            t_b2 = P.do(SE, lambda e: e.activation(out=tiny[0:1, 40:56],
                                                   in_=tiny[0:1, 40:56], func=AF.Sqrt),
                        waits=[t_b1], tick=True)
            t_b3 = P.do(VE, lambda e: e.tensor_scalar_mul(tiny[0:1, 40:56],
                                                          tiny[0:1, 40:56], -SCALE),
                        waits=[t_b2], tick=True)
            t_b4 = t_mm(psh[2][:, 0:16], onesf_r[:], tiny[0:1, 40:56],
                        start=True, stop=True, waits=[t_b3, bank_free[2]], tick=True)
            tok_negB = P.do(SE, lambda e: e.activation(out=negB[:], in_=psh[2][:, 0:16],
                                                       func=AF.Copy),
                            waits=[t_b4], tick=True)
            bank_free[2] = tok_negB

            P.barrier()
            wo_toks = []
            for s4 in range(4):
                wo_toks.append(s_cast.dma(
                    wo_sb[:, s4 * 8192:(s4 + 1) * 8192].rearrange("p (s d) -> p s d", s=4),
                    wo_src[s4 * 512:(s4 + 1) * 512, :].rearrange("(s p) d -> p s d", p=128)))
            tok_wo = ("MULTI", wo_toks)

            plan = [kv_plan(qb, windowed=(not is_b)) for qb in range(NTB)]
            gate_ld_free = [None, None]
            ostage_free = [None, None]
            qb_list = [1] if _DBG == "attnout" else list(range(NTB))
            phat_free = [None, None, None]
            z16_free = [None]
            for qb in qb_list:
                att_toks = []
                np_count = 0

                def head_main(h):
                    nonlocal np_count
                    g = h // 4
                    av_bank = 4 + (h % 2)
                    zb = 6 if h % 2 == 0 else 3
                    ents = plan[qb]
                    nkv = len(ents)

                    def stage_a(kvi):
                        jt, a_lo, a_hi, mks = ents[kvi]
                        nonlocal np_count
                        cols = slice(a_lo * 128, (a_hi + 1) * 128)
                        sbank = kvi % 3
                        tok_s = t_mm(psh[sbank][:, cols],
                                     kT[:, g * 2048 + jt * 128: g * 2048 + (jt + 1) * 128],
                                     qhat[:, h * 2048 + qb * 512 + a_lo * 128:
                                          h * 2048 + qb * 512 + (a_hi + 1) * 128],
                                     start=True, stop=True,
                                     waits=[qtoks[h][0], ktoks[g][0], bank_free[sbank]],
                                     tick=True)
                        for (a, mid) in mks:
                            mc = slice((mid - 1) * 128, mid * 128)
                            tok_s = P.do(VE, lambda e, sbank=sbank, a=a, mc=mc: e.tensor_add(
                                psh[sbank][:, a * 128:(a + 1) * 128],
                                psh[sbank][:, a * 128:(a + 1) * 128], masks_sb[:, mc]),
                                waits=[tok_s, t_consts], tick=True)
                        pi = np_count % 3
                        np_count += 1
                        pt = phat[pi]
                        tok_e = P.do(SE, lambda e, sbank=sbank, pt=pt, cols=cols, h=h:
                                     e.activation(out=pt[:, cols], in_=psh[sbank][:, cols],
                                                  func=AF.Exp, scale=SCALE,
                                                  bias=negB[:, h:h + 1]),
                                     waits=[tok_s, tok_negB, phat_free[pi]], tick=True)
                        bank_free[sbank] = tok_e
                        return (kvi, pi, pt, cols, tok_e)

                    def stage_b(sa):
                        kvi, pi, pt, cols, tok_e = sa
                        jt = ents[kvi][0]
                        first = kvi == 0
                        is_last = kvi == nkv - 1
                        t_mm(psh[zb][0:1, cols], ones_c[:], pt[:, cols],
                             start=first, stop=is_last,
                             waits=[tok_e] + ([bank_free[zb]] if first else []), tick=False)
                        tok_av = t_mm(psh[av_bank][:, cols],
                                      vT[:, jt * 512 + g * 128: jt * 512 + (g + 1) * 128],
                                      pt[:, cols],
                                      start=first, stop=is_last,
                                      waits=[tok_e, tok_v] +
                                            ([bank_free[av_bank]] if first else []),
                                      tick=True)
                        phat_free[pi] = tok_av

                    from collections import deque
                    pend = deque()
                    for kvi in range(nkv):
                        pend.append(stage_a(kvi))
                        if len(pend) > 2:
                            stage_b(pend.popleft())
                    while pend:
                        stage_b(pend.popleft())
                    return (h, av_bank, zb, P.now(TE))

                def head_epi(hs):
                    h, av_bank, zb, tok_avdone = hs
                    tok_ln = P.do(SE, lambda e, zb=zb: e.activation(
                        out=z_row[:], in_=psh[zb][0:1, :], func=AF.Ln),
                        waits=[tok_avdone, z16_free[0]], tick=True)
                    bank_free[zb] = tok_ln
                    tok_zr = P.do(SE, lambda e: e.activation(
                        out=z16[:], in_=z_row[:], func=AF.Exp, scale=-1.0),
                        waits=[tok_ln], tick=True)
                    t_bz = t_mm(psh[7][:, :], ones_r[:], z16[:],
                                start=True, stop=True, waits=[tok_zr, bank_free[7]],
                                tick=True)
                    z16_free[0] = t_bz
                    rz = rzstage[h % 2]
                    tok_rz = P.do(SE, lambda e, rz=rz: e.activation(
                        out=rz, in_=psh[7][:, :], func=AF.Copy), waits=[t_bz], tick=True)
                    bank_free[7] = tok_rz
                    tok_ao = P.do(VE, lambda e, av_bank=av_bank, h=h, rz=rz: e.tensor_mul(
                        attnout[:, h * 512:(h + 1) * 512], psh[av_bank][:, :], rz),
                        waits=[tok_rz, tok_avdone], tick=True)
                    bank_free[av_bank] = tok_ao
                    att_toks.append(tok_ao)

                pend_h = None
                for h in range(NH):
                    hs = head_main(h)
                    if pend_h is not None:
                        head_epi(pend_h)
                    pend_h = hs
                head_epi(pend_h)

                if _DBG == "attnout":
                    dbg_dump(attnout, nslab=4)
                    return
                if is_b:
                    for s in range(NSLAB):
                        b = s % 2
                        tok_g = s_ld2.dma(gstage[b], gate_scr[s, :, qb, :],
                                          waits=[gate_ld_free[b]] + gate_toks)
                        tok_sg = P.do(SE, lambda e, b=b: e.activation(
                            out=gstage[b], in_=gstage[b], func=AF.Sigmoid),
                            waits=[tok_g, att_toks[s] if s < len(att_toks) else None],
                            tick=True)
                        tok_mulg = P.do(VE, lambda e, s=s, b=b: e.tensor_mul(
                            attnout[:, s * 512:(s + 1) * 512],
                            attnout[:, s * 512:(s + 1) * 512], gstage[b]),
                            waits=[tok_sg] + att_toks, tick=True)
                        gate_ld_free[b] = tok_mulg

                ao_ready = P.now(VE)
                for kd in range(NSLAB):
                    j = kd % 2
                    last_mm = None
                    for s in range(NSLAB):
                        last_mm = t_mm(psh[j][:, :],
                                       wo_sb[:, s * 2048 + kd * 128: s * 2048 + (kd + 1) * 128],
                                       attnout[:, s * 512:(s + 1) * 512],
                                       start=(s == 0), stop=(s == NSLAB - 1),
                                       waits=[tok_wo, ao_ready] +
                                             ([bank_free[j]] if s == 0 else []),
                                       tick=(s == NSLAB - 1))
                    ot = ostage[j]
                    tok_oe = P.do(SE, lambda e, j=j, ot=ot: e.activation(
                        out=ot, in_=psh[j][:, :], func=AF.Copy),
                        waits=[last_mm, ostage_free[j]], tick=True)
                    bank_free[j] = tok_oe
                    extra_acc = getattr(P, "first_acc_extra", None)
                    P.first_acc_extra = None
                    ostage_free[j] = s_acc.dma(
                        R_dm[kd * 128:(kd + 1) * 128, qb * 512:(qb + 1) * 512],
                        ot, waits=[tok_oe, extra_acc], accum_op=ALU.add)
            P.barrier()

        def emit_ffn(src_dm, extra_wait=()):
            tok_xn, xn_tb = emit_ln(src_dm, extra_wait=extra_wait)
            wgu_free = [None, None, None, None]
            wgu_u8_free = [tok_xn, tok_xn, tok_xn, tok_xn]
            h_free = [None, None, None]
            hb_pre = []
            nt = 0
            for it in range(64):
                bg, bu = (it % 2) * 2, (it % 2) * 2 + 1
                tg8 = s_ld.dma(wgu_u8[bg][:, 0:2048], gw_blk[it].rearrange("p k i -> p (k i)"),
                               waits=[wgu_u8_free[bg]])
                tu8 = s_ld2.dma(wgu_u8[bu][:, 0:2048], uw_blk[it].rearrange("p k i -> p (k i)"),
                                waits=[wgu_u8_free[bu]])
                tok_wg = P.do(SE, lambda e, bg=bg: e.activation(
                    out=wgu[bg], in_=wgu_u8[bg][:, 0:2048], func=AF.Copy, bias=-128.0),
                    waits=[tg8, wgu_free[bg]], tick=True)
                wgu_u8_free[bg] = tok_wg
                tok_wu = P.do(SE, lambda e, bu=bu: e.activation(
                    out=wgu[bu], in_=wgu_u8[bu][:, 0:2048], func=AF.Copy, bias=-128.0),
                    waits=[tu8, wgu_free[bu]], tick=True)
                wgu_u8_free[bu] = tok_wu
                for tb in range(NTB):
                    jg = tb % 2
                    ju = 2 + tb % 2
                    mg = mu = None
                    for k in range(NSLAB):
                        rhs = xn[:, k * 2048 + tb * 512: k * 2048 + (tb + 1) * 512]
                        mg = t_mm(psh[jg][:, :], wgu[bg][:, k * 128:(k + 1) * 128], rhs,
                                  start=(k == 0), stop=(k == NSLAB - 1),
                                  waits=[tok_wg, xn_tb[tb]] + ([bank_free[jg]] if k == 0 else []),
                                  tick=(k == NSLAB - 1))
                    for k in range(NSLAB):
                        rhs = xn[:, k * 2048 + tb * 512: k * 2048 + (tb + 1) * 512]
                        mu = t_mm(psh[ju][:, :], wgu[bu][:, k * 128:(k + 1) * 128], rhs,
                                  start=(k == 0), stop=(k == NSLAB - 1),
                                  waits=[tok_wu, xn_tb[tb]] + ([bank_free[ju]] if k == 0 else []),
                                  tick=(k == NSLAB - 1))
                    if tb == NTB - 1:
                        wgu_free[bg] = mg
                        wgu_free[bu] = mu
                    fs = fstage[tb % 2]
                    tok_sg = P.do(SE, lambda e, jg=jg, fs=fs, it=it: e.activation(
                        out=fs, in_=psh[jg][:, :], func=AF.Silu,
                        scale=gsc_sb[:, it:it + 1]),
                        waits=[mg, t_consts], tick=True)
                    bank_free[jg] = tok_sg
                    ht = hstage[nt % 3]
                    tok_h = P.do(VE, lambda e, ju=ju, fs=fs, ht=ht, it=it:
                                 e.scalar_tensor_tensor(ht, psh[ju][:, :],
                                                        usc_sb[:, it:it + 1], fs,
                                                        ALU.mult, ALU.mult),
                                 waits=[tok_sg, mu, h_free[nt % 3]], tick=True)
                    bank_free[ju] = tok_h
                    h_free[nt % 3] = s_st.dma(h_scr[it, :, tb, :], ht, waits=[tok_h])
                    nt += 1
                if it % 16 == 15:
                    q4 = it // 16
                    hb_pre.append(s_ld2.dma(
                        h_sb[:, q4 * 8192:(q4 + 1) * 8192].rearrange("p (it t) -> p it t", it=16),
                        h_scr[q4 * 16:(q4 + 1) * 16, :, 0, :].rearrange("it p t -> p it t"),
                        waits=[s_st.wait_all()]))
            P.barrier()
            if _DBG == "ffng":
                pass
            fstage_free = [None, None]
            wd_free = [None, None]
            wd_u8_free = [None, None]
            h_guard = None
            for tb in range(NTB):
                if tb == 0:
                    hb_toks = hb_pre
                else:
                    hb_toks = []
                    for q4 in range(4):
                        hb_toks.append(s_ld2.dma(
                            h_sb[:, q4 * 8192:(q4 + 1) * 8192].rearrange("p (it t) -> p it t", it=16),
                            h_scr[q4 * 16:(q4 + 1) * 16, :, tb, :].rearrange("it p t -> p it t"),
                            waits=[h_guard]))
                if _DBG == "ffnh" and tb == 0:
                    dbg_dump(h_sb)
                    return
                last_mm = None
                for kd in range(16):
                    b = kd % 2
                    t8 = s_ld.dma(wd_u8[b][:, 0:8192],
                                  dw_blk[kd].rearrange("p it d -> p (it d)"),
                                  waits=[wd_u8_free[b]])
                    tok_wd = P.do(SE, lambda e, b=b: e.activation(
                        out=wd_buf[b], in_=wd_u8[b][:, 0:8192], func=AF.Copy, bias=-128.0),
                        waits=[t8, wd_free[b]], tick=True)
                    wd_u8_free[b] = tok_wd
                    j = kd % 2
                    last_mm = None
                    for it in range(64):
                        last_mm = t_mm(psh[j][:, :],
                                       wd_buf[b][:, it * 128:(it + 1) * 128],
                                       h_sb[:, it * 512:(it + 1) * 512],
                                       start=(it == 0), stop=(it == 63),
                                       waits=[tok_wd, hb_toks[it // 16]] +
                                             ([bank_free[j]] if it == 0 else []),
                                       tick=(it == 63))
                    wd_free[b] = last_mm
                    ot = ostage[j]
                    tok_oe = P.do(SE, lambda e, j=j, ot=ot, kd=kd: e.activation(
                        out=ot, in_=psh[j][:, :], func=AF.Copy,
                        scale=dsc_sb[:, kd:kd + 1]),
                        waits=[last_mm, fstage_free[j]], tick=True)
                    bank_free[j] = tok_oe
                    fstage_free[j] = s_acc.dma(
                        R_dm[kd * 128:(kd + 1) * 128, tb * 512:(tb + 1) * 512],
                        ot, waits=[tok_oe], accum_op=ALU.add)
                h_guard = last_mm
            P.barrier()

        P.first_acc_extra = t_Rinit
        stage_list = _STAGE_LIST if _STAGE_LIST is not None else list(range(1, _STAGES + 1))
        for si, s in enumerate(stage_list):
            src0 = x_dm if si == 0 else R_dm
            ew = [t_Rinit] if si == 0 else [s_acc.wait_all()]
            if s == 1:
                emit_attn("a", src0, extra_wait=ew)
            elif s == 2:
                emit_ffn(src0, extra_wait=ew)
            elif s == 3:
                emit_attn("b", src0, extra_wait=ew)
            elif s == 4:
                emit_ffn(src0, extra_wait=ew)
            if _DBG is not None:
                break

        P.barrier()
        for e in ENGS:
            P.do(e, lambda ee: None, waits=[s_acc.wait_all(), s_st.wait_all(),
                                            s_cast.wait_all()])

        block = es.enter_context(nc.Block())
        P.replay(block)
    return nc


def kernel(**inputs):
    g = host_prep(inputs)
    x = np.asarray(inputs["L0_attn_a_x"], dtype=np.float32)
    nc = build_program()
    in_maps = []
    for c in range(8):
        m = dict(g)
        m["x_dm"] = np.ascontiguousarray(x[c].T)
        in_maps.append(m)
    res = run_bass_kernel_spmd(nc, in_maps, core_ids=list(range(8)))
    out = np.stack([np.asarray(r["out_dm"]).T for r in res.results], axis=0)
    return out


# version 08
# speedup vs baseline: 1.0055x; 1.0055x over previous
import numpy as np
import concourse.bass as bass
import concourse.mybir as mybir
from concourse.bass_utils import run_bass_kernel_spmd
from contextlib import ExitStack

F32 = mybir.dt.float32
BF16 = mybir.dt.bfloat16
I32 = mybir.dt.int32
U8 = mybir.dt.uint8
AF = mybir.ActivationFunctionType
ALU = mybir.AluOpType
AX = mybir.AxisListType

D = 2048
T = 2048
NH = 16
NKV = 4
HD = 128
NSLAB = 16
NTB = 4
SCALE = float(1.0 / np.sqrt(128.0))
WIN = 512
FRONT = 256
NEGINF = -1.0e9

_STAGES = 4
_STAGE_LIST = None
_DBG = None



def _deint(n_heads):
    p = []
    for h in range(n_heads):
        base = h * HD
        p.extend(base + 2 * f for f in range(64))
        p.extend(base + 2 * f + 1 for f in range(64))
    return np.asarray(p, dtype=np.int64)


def _rope_tables(theta):
    inv_freq = 1.0 / theta ** (np.arange(0, HD, 2)[: HD // 2].astype(np.float32) / HD)
    t = np.arange(T, dtype=np.float32)
    ang = np.outer(t, inv_freq)
    cos = np.cos(ang).T.astype(np.float32)
    sin = np.sin(ang).T.astype(np.float32)
    C2 = np.concatenate([cos, cos], axis=0)
    S2 = np.concatenate([-sin, sin], axis=0)
    return C2, S2


def host_prep(inp):
    g = {}
    pq = _deint(NH)
    pk = _deint(NKV)
    d1 = _deint(1)
    g["wq_a"] = np.ascontiguousarray(inp["L0_attn_a_wq"][:, pq])
    g["wk_a"] = np.ascontiguousarray(inp["L0_attn_a_wk"][:, pk])
    g["wv_a"] = np.ascontiguousarray(inp["L0_attn_a_wv"])
    g["wo_a"] = np.ascontiguousarray(inp["L0_attn_a_wo"])
    wq = np.asarray(inp["L0_attn_b_w_qkv"]).T
    g["wqkv_b"] = np.ascontiguousarray(np.concatenate(
        [wq[:, 0:2048][:, pq], wq[:, 2048:4096],
         wq[:, 4096:4608][:, pk], wq[:, 4608:5120]], axis=1))
    g["wo_b"] = np.ascontiguousarray(np.asarray(inp["L0_attn_b_w_o"]).T)
    g["qnw"] = np.ascontiguousarray(np.asarray(inp["L0_attn_b_q_norm_w"])[d1].reshape(128, 1))
    g["qnb"] = np.ascontiguousarray(np.asarray(inp["L0_attn_b_q_norm_b"])[d1].reshape(128, 1))
    g["knw"] = np.ascontiguousarray(np.asarray(inp["L0_attn_b_k_norm_w"])[d1].reshape(128, 1))
    g["knb"] = np.ascontiguousarray(np.asarray(inp["L0_attn_b_k_norm_b"])[d1].reshape(128, 1))
    def _u8(a):
        a = np.ascontiguousarray(np.asarray(a, dtype=np.int32))
        return a.view(np.uint8).reshape(a.shape + (4,))[..., 0]
    wt = _u8(inp["L0_ffn_gate_w"]).T.reshape(16, 128, 64, 128)
    g["gw_blk"] = np.ascontiguousarray(wt.transpose(2, 1, 0, 3))
    wt = _u8(inp["L0_ffn_up_w"]).T.reshape(16, 128, 64, 128)
    g["uw_blk"] = np.ascontiguousarray(wt.transpose(2, 1, 0, 3))
    wt = _u8(inp["L0_ffn_down_w"]).T.reshape(64, 128, 16, 128)
    g["dw_blk"] = np.ascontiguousarray(wt.transpose(2, 1, 0, 3))
    g["gsc"] = np.ascontiguousarray(np.asarray(inp["L0_ffn_gate_scale"]).reshape(64, 128).T)
    g["usc"] = np.ascontiguousarray(np.asarray(inp["L0_ffn_up_scale"]).reshape(64, 128).T)
    g["dsc"] = np.ascontiguousarray(np.asarray(inp["L0_ffn_down_scale"]).reshape(16, 128).T)
    C2a, S2a = _rope_tables(10000.0)
    C2b, S2b = _rope_tables(1000000.0)
    g["tab_a"] = np.ascontiguousarray(np.concatenate([C2a, S2a], axis=1))
    g["tab_b"] = np.ascontiguousarray(np.concatenate([C2b, S2b], axis=1))
    r = np.arange(128)
    P1 = np.where(r[None, :] >= r[:, None], 0.0, NEGINF).astype(np.float32)
    P2 = np.where(r[None, :] < r[:, None], 0.0, NEGINF).astype(np.float32)
    g["masks"] = np.ascontiguousarray(np.concatenate([P1, P2], axis=1))
    return g


def kv_plan(qb, windowed):
    out = []
    for jt in range(T // 128):
        subs = []
        for a in range(4):
            dd = qb * 512 + a * 128 - jt * 128
            if dd < 0:
                st = None
            elif not windowed or jt * 128 < FRONT:
                st = 1 if dd == 0 else 0
            else:
                if dd == 0:
                    st = 1
                elif dd <= WIN - 128:
                    st = 0
                elif dd == WIN:
                    st = 2
                else:
                    st = None
            subs.append(st)
        alive = [a for a in range(4) if subs[a] is not None]
        if not alive:
            continue
        a_lo, a_hi = min(alive), max(alive)
        assert alive == list(range(a_lo, a_hi + 1))
        out.append((jt, a_lo, a_hi, [(a, subs[a]) for a in alive if subs[a]]))
    return out



ENGS = ("tensor", "vector", "scalar", "sync", "gpsimd")


class Prog:
    def __init__(self, nc, es):
        self.nc = nc
        self.es = es
        self.ops = {e: [] for e in ENGS}
        self.engsem = {}
        self.cnt = {}
        self.streams = []
        for e in ("tensor", "vector", "scalar"):
            self.engsem[e] = self.new_sem("pg_" + e)
            self.cnt[e] = 0
        self.lastwait = {e: {} for e in ENGS}

    def new_sem(self, name):
        return self.es.enter_context(self.nc.semaphore(name))

    def _expand(self, waits):
        out = []
        for w in waits:
            if w is None:
                continue
            if isinstance(w, tuple) and len(w) == 2 and w[0] == "MULTI":
                out.extend(x for x in w[1] if x is not None)
            else:
                out.append(w)
        return out

    def do(self, eng, fn, waits=(), tick=False):
        ws = []
        lw = self.lastwait[eng]
        for (sem, val, producer) in self._expand(waits):
            if producer == eng:
                continue
            k = id(sem)
            if lw.get(k, -1) >= val:
                continue
            lw[k] = val
            ws.append((sem, val))
        tok = None
        esem = self.engsem.get(eng)
        if tick:
            self.cnt[eng] += 1
            tok = (esem, self.cnt[eng], eng)

        def f(e, ws=ws, fn=fn, tick=tick, esem=esem):
            for (s, v) in ws:
                e.wait_ge(s, v)
            ins = fn(e)
            if tick:
                ins.then_inc(esem, 1)
        self.ops[eng].append(f)
        return tok

    def now(self, eng):
        return (self.engsem[eng], self.cnt[eng], eng)

    def barrier(self):
        toks = [self.now(e) for e in ("tensor", "vector", "scalar")]
        for st in self.streams:
            toks.append(st.wait_all())
        for e in ENGS:
            self.do(e, lambda ee: None, waits=toks)

    def replay(self, block):
        def mk(eng):
            def body(e):
                for f in self.ops[eng]:
                    f(e)
            return body
        block.tensor(mk("tensor"))
        block.vector(mk("vector"))
        block.scalar(mk("scalar"))
        block.sync(mk("sync"))
        block.gpsimd(mk("gpsimd"))


class Stream:

    def __init__(self, P, queue, name, R=2):
        self.P = P
        self.queue = queue
        self.R = R
        self.sems = [P.new_sem(f"{name}{i}") for i in range(R)]
        self.counts = [0] * R
        self.n = 0
        P.streams.append(self)

    def dma(self, out, in_, waits=(), **kw):
        i = self.n % self.R
        self.n += 1
        self.counts[i] += 16
        sem = self.sems[i]
        val = self.counts[i]

        def fn(e, out=out, in_=in_, kw=kw, sem=sem):
            return e.dma_start(out=out, in_=in_, **kw).then_inc(sem, 16)
        self.P.do(self.queue, fn, waits=waits)
        return (sem, val, "dma")

    def wait_all(self):
        return ("MULTI", [(self.sems[i], self.counts[i], "dma")
                          for i in range(self.R) if self.counts[i] > 0])



def build_program():
    nc = bass.Bass("TRN2", target_bir_lowering=False)

    def din(name, shape, dt):
        return nc.dram_tensor(name, shape, dt, kind="ExternalInput").ap()

    x_dm = din("x_dm", [D, T], F32)
    wq_a = din("wq_a", [D, 2048], F32)
    wk_a = din("wk_a", [D, 512], F32)
    wv_a = din("wv_a", [D, 512], F32)
    wo_a = din("wo_a", [2048, D], F32)
    wqkv_b = din("wqkv_b", [D, 5120], F32)
    wo_b = din("wo_b", [2048, D], F32)
    qnw_d = din("qnw", [128, 1], F32)
    qnb_d = din("qnb", [128, 1], F32)
    knw_d = din("knw", [128, 1], F32)
    knb_d = din("knb", [128, 1], F32)
    gw_blk = din("gw_blk", [64, 128, 16, 128], U8)
    uw_blk = din("uw_blk", [64, 128, 16, 128], U8)
    dw_blk = din("dw_blk", [16, 128, 64, 128], U8)
    gsc_d = din("gsc", [128, 64], F32)
    usc_d = din("usc", [128, 64], F32)
    dsc_d = din("dsc", [128, 16], F32)
    tab_a = din("tab_a", [128, 2 * T], F32)
    tab_b = din("tab_b", [128, 2 * T], F32)
    masks_d = din("masks", [128, 256], F32)

    R_dm = nc.dram_tensor("out_dm", [D, T], F32, kind="ExternalOutput").ap()

    h_scr = nc.dram_tensor("h_scr", [64, 128, 4, 512], BF16, kind="Internal").ap()
    gate_scr = nc.dram_tensor("gate_scr", [16, 128, 4, 512], BF16, kind="Internal").ap()

    es = ExitStack()
    with es:
        A1 = es.enter_context(nc.sbuf_tensor("A1", [128, 32768], BF16))
        A2 = es.enter_context(nc.sbuf_tensor("A2", [128, 32768], BF16))
        A3 = es.enter_context(nc.sbuf_tensor("A3", [128, 16384], BF16))
        A4 = es.enter_context(nc.sbuf_tensor("A4", [128, 16384], BF16))
        m_row = es.enter_context(nc.sbuf_tensor("m_row", [1, T], BF16))
        r_row = es.enter_context(nc.sbuf_tensor("r_row", [1, T], F32))

        tiny = es.enter_context(nc.sbuf_tensor("tiny", [1, 64], F32))
        negB = es.enter_context(nc.sbuf_tensor("negB", [128, 16], F32))
        masks_sb = es.enter_context(nc.sbuf_tensor("masks_sb", [128, 256], BF16))
        gsc_sb = es.enter_context(nc.sbuf_tensor("gsc_sb", [128, 64], F32))
        usc_sb = es.enter_context(nc.sbuf_tensor("usc_sb", [128, 64], F32))
        dsc_sb = es.enter_context(nc.sbuf_tensor("dsc_sb", [128, 16], F32))
        qnw_sb = es.enter_context(nc.sbuf_tensor("qnw_sb", [128, 1], F32))
        qnb_sb = es.enter_context(nc.sbuf_tensor("qnb_sb", [128, 1], F32))
        knw_sb = es.enter_context(nc.sbuf_tensor("knw_sb", [128, 1], F32))
        knb_sb = es.enter_context(nc.sbuf_tensor("knb_sb", [128, 1], F32))
        epsc = es.enter_context(nc.sbuf_tensor("epsc", [128, 2], F32))
        ones_c = es.enter_context(nc.sbuf_tensor("ones_c", [128, 1], BF16))
        ones_r = es.enter_context(nc.sbuf_tensor("ones_r", [1, 128], BF16))
        onesf_r = es.enter_context(nc.sbuf_tensor("onesf_r", [1, 128], F32))
        oneshk_r = es.enter_context(nc.sbuf_tensor("oneshk_r", [1, 128], F32))
        oneshk_c = es.enter_context(nc.sbuf_tensor("oneshk_c", [128, 1], F32))

        psh = [es.enter_context(nc.psum_tensor(f"ps{i}", [128, 512], F32)) for i in range(8)]
        bank_free = [None] * 8

        P = Prog(nc, es)
        bcast_free = [None, None, None, None]

        s_ld = Stream(P, "sync", "ld", R=2)
        s_ld2 = Stream(P, "scalar", "ld2", R=2)
        s_cast = Stream(P, "gpsimd", "cast", R=4)
        s_acc = Stream(P, "gpsimd", "acc", R=2)
        s_st = Stream(P, "sync", "st", R=2)
        s_swp = Stream(P, "gpsimd", "swp", R=2)

        scrA2 = [A4[:, 0:2048], A4[:, 2048:4096]]
        scrB2 = [A4[:, 4096:6144], A4[:, 6144:8192]]
        xsq0 = A4[:, 0:2048]
        xsq1 = A4[:, 2048:4096]
        tabs = A4[:, 8192:12288]
        attnout = A4[:, 0:8192]
        phat = [A4[:, 8192 + i * 512: 8704 + i * 512] for i in range(4)]
        gstage = [A4[:, 10240 + i * 512: 10752 + i * 512] for i in range(2)]
        ostage = [A4[:, 11264 + i * 1024: 12288 + i * 1024].bitcast(F32)
                  for i in range(2)]
        rzstage = [A4[:, 13312:14336].bitcast(F32),
                   A4[:, 14336:15360].bitcast(F32)]
        z_row = A4[0:1, 15360:16384].bitcast(F32)
        z16 = A4[0:1, 9728:10240]

        wgu = [A4[:, 8192 + i * 2048: 10240 + i * 2048] for i in range(4)]
        wgu_u8 = [A4[:, 3584 + i * 1024: 4608 + i * 1024].bitcast(U8)
                  for i in range(4)]
        wd_u8 = [A4[:, 0 + i * 4096: 4096 + i * 4096].bitcast(U8)
                 for i in range(2)]
        hstage = [A4[:, 2048 + i * 512: 2560 + i * 512] for i in range(3)]
        fstage = [A4[:, 0 + i * 1024: 1024 + i * 1024].bitcast(F32) for i in range(2)]
        wq_stage2 = [A4[:, 12288:14336], A4[:, 14336:16384]]
        bcast_sb = [A4[:, 12288 + i * 512: 12800 + i * 512] for i in range(4)]
        wv_stage = [A4[:, 12288 + i * 512: 12800 + i * 512] for i in range(2)]

        xn = A1
        wo_sb = A1
        init_stage = [A1[:, i * 4096:(i + 1) * 4096].bitcast(F32) for i in range(2)]
        qhat = A2
        h_sb = A2
        h_sb2 = A1

        kT = A3[:, 0:8192]
        vT = A3[:, 8192:16384]
        wd_buf = [A3[:, i * 8192:(i + 1) * 8192] for i in range(2)]

        TE, VE, SE = "tensor", "vector", "scalar"

        def t_mm(out, lhsT, rhs, start, stop, waits=(), tick=False):
            return P.do(TE, lambda e: e.matmul(out, lhsT, rhs, start=start, stop=stop),
                        waits=waits, tick=tick)

        s_cast.dma(masks_sb[:], masks_d)
        for dst, src in ((gsc_sb, gsc_d), (usc_sb, usc_d), (dsc_sb, dsc_d),
                         (qnw_sb, qnw_d), (qnb_sb, qnb_d), (knw_sb, knw_d),
                         (knb_sb, knb_d)):
            s_ld.dma(dst[:], src)
        t_consts = s_ld.wait_all()
        P.do(VE, lambda e: e.memset(epsc[:, 0:1], 1e-5), tick=True)
        P.do(VE, lambda e: e.memset(epsc[:, 1:2], 1e-6), tick=True)
        P.do(VE, lambda e: e.memset(ones_c[:], 1.0), tick=True)
        P.do(VE, lambda e: e.memset(ones_r[:], 1.0), tick=True)
        P.do(VE, lambda e: e.memset(onesf_r[:], 1.0), tick=True)
        P.do(VE, lambda e: e.memset(oneshk_r[:], 1.0 / 128.0), tick=True)
        tok_ones = P.do(VE, lambda e: e.memset(oneshk_c[:], 1.0 / 128.0), tick=True)

        init_st_tok = [None, None]
        for k in range(NSLAB):
            tok = s_ld2.dma(init_stage[k % 2][:, 0:2048], x_dm[k * 128:(k + 1) * 128, :],
                            waits=[init_st_tok[k % 2]])
            init_st_tok[k % 2] = s_st.dma(R_dm[k * 128:(k + 1) * 128, :],
                                          init_stage[k % 2][:, 0:2048], waits=[tok])

        t_Rinit = s_st.wait_all()

        def dbg_dump(region, nslab=16):
            P.barrier()
            for k in range(nslab):
                s_cast.dma(R_dm[k * 128:(k + 1) * 128, :],
                           region[:, k * 2048:(k + 1) * 2048])
            P.barrier()

        def emit_ln(src_dm, extra_wait=()):
            for k in range(NSLAB):
                s_cast.dma(xn[:, k * 2048:(k + 1) * 2048],
                           src_dm[k * 128:(k + 1) * 128, :],
                           waits=list(extra_wait) if k == 0 else [])
            all_ld = s_cast.wait_all()
            mm_tok = None
            sq_prev = [None, None]
            for k in range(NSLAB):
                xk = xn[:, k * 2048:(k + 1) * 2048]
                xq = xsq0 if k % 2 == 0 else xsq1
                tok_s = P.do(SE, lambda e, xq=xq, xk=xk: e.activation(
                    out=xq, in_=xk, func=AF.Square),
                    waits=[all_ld, sq_prev[k % 2]], tick=True)
                for tb in range(NTB):
                    t_mm(psh[tb][0:1, :], ones_c[:], xk[:, tb * 512:(tb + 1) * 512],
                         start=(k == 0), stop=(k == NSLAB - 1),
                         waits=[all_ld, tok_ones] +
                               ([bank_free[tb]] if k == 0 else []))
                for tb in range(NTB):
                    mm_tok = t_mm(psh[4 + tb][0:1, :], ones_c[:],
                                  xq[:, tb * 512:(tb + 1) * 512],
                                  start=(k == 0), stop=(k == NSLAB - 1),
                                  waits=[tok_s] + ([bank_free[4 + tb]] if k == 0 else []),
                                  tick=(tb == NTB - 1))
                sq_prev[k % 2] = mm_tok
            row_toks = []
            for tb in range(NTB):
                row_toks.append(P.do(SE, lambda e, tb=tb: e.activation(
                    out=m_row[:, tb * 512:(tb + 1) * 512], in_=psh[tb][0:1, :],
                    func=AF.Copy, scale=1.0 / 2048.0), waits=[mm_tok], tick=True))
                row_toks.append(P.do(SE, lambda e, tb=tb: e.activation(
                    out=r_row[:, tb * 512:(tb + 1) * 512], in_=psh[4 + tb][0:1, :],
                    func=AF.Copy, scale=1.0 / 2048.0), waits=[mm_tok], tick=True))
            var_toks = []
            for tb in range(NTB):
                ta = P.do(VE, lambda e, tb=tb: e.tensor_mul(
                    psh[tb][0:1, :], m_row[:, tb * 512:(tb + 1) * 512],
                    m_row[:, tb * 512:(tb + 1) * 512]),
                    waits=row_toks, tick=True)
                tv = P.do(VE, lambda e, tb=tb: e.tensor_sub(
                    r_row[:, tb * 512:(tb + 1) * 512],
                    r_row[:, tb * 512:(tb + 1) * 512], psh[tb][0:1, :]),
                    waits=[ta], tick=True)
                var_toks.append(tv)
                bank_free[tb] = tv
            tok_sd = P.do(SE, lambda e: e.activation(out=r_row[:], in_=r_row[:],
                                                     func=AF.Ln, bias=epsc[0:1, 0:1]),
                          waits=var_toks, tick=True)
            tok_r = P.do(SE, lambda e: e.activation(out=r_row[:], in_=r_row[:],
                                                    func=AF.Exp, scale=-0.5),
                         waits=[tok_sd], tick=True)
            tok_mr = P.do(VE, lambda e: e.tensor_mul(m_row[:], m_row[:], r_row[:]),
                          waits=[tok_r], tick=True)
            bank_free[4] = tok_mr
            last = None
            xn_tb = []
            for tb in range(NTB):
                br, bm = (6, 7) if tb % 2 == 0 else (2, 3)
                t_r = t_mm(psh[br][:, :], onesf_r[:], r_row[:, tb * 512:(tb + 1) * 512],
                           start=True, stop=True, waits=[tok_r, bank_free[br]], tick=True)
                t_m = t_mm(psh[bm][:, :], ones_r[:], m_row[:, tb * 512:(tb + 1) * 512],
                           start=True, stop=True, waits=[tok_mr, bank_free[bm]], tick=True)
                rb = bcast_sb[tb % 2]
                mb = bcast_sb[2 + tb % 2]
                t_re = P.do(SE, lambda e, br=br, rb=rb: e.activation(
                    out=rb, in_=psh[br][:, :], func=AF.Copy),
                    waits=[t_r, bcast_free[tb % 2]], tick=True)
                bank_free[br] = t_re
                t_me = P.do(SE, lambda e, bm=bm, mb=mb: e.activation(
                    out=mb, in_=psh[bm][:, :], func=AF.Copy),
                    waits=[t_m, bcast_free[2 + tb % 2]], tick=True)
                bank_free[bm] = t_me
                for k in range(NSLAB):
                    xkb = xn[:, k * 2048 + tb * 512: k * 2048 + (tb + 1) * 512]
                    P.do(VE, lambda e, xkb=xkb, rb=rb: e.tensor_mul(xkb, xkb, rb),
                         waits=[t_re], tick=False)
                    last = P.do(VE, lambda e, xkb=xkb, mb=mb: e.tensor_sub(xkb, xkb, mb),
                                waits=[t_me], tick=(k == NSLAB - 1))
                bcast_free[tb % 2] = last
                bcast_free[2 + tb % 2] = last
                xn_tb.append(last)
            return last, xn_tb


        def emit_attn(which, src_dm, extra_wait=()):
            is_b = which == "b"
            tab_src = tab_b if is_b else tab_a
            wq_src = (wqkv_b[:, 0:2048] if is_b else wq_a)
            wk_src = (wqkv_b[:, 4096:4608] if is_b else wk_a)
            wv_src = (wqkv_b[:, 4608:5120] if is_b else wv_a)
            wo_src = wo_b if is_b else wo_a

            t1 = s_cast.dma(tabs[:, 0:2048], tab_src[:, 0:T])
            t2 = s_cast.dma(tabs[:, 2048:4096], tab_src[:, T:2 * T])
            tok_tabs = ("MULTI", [t1, t2])

            tok_xn, xn_tb = emit_ln(src_dm, extra_wait=extra_wait)
            if _DBG == "xn":
                dbg_dump(xn)
                return

            ev_tok = None
            wv_free = [tok_xn, tok_xn]
            for half in range(2):
                for k in range(NSLAB):
                    b = k % 2
                    tok_w = s_cast.dma(wv_stage[b][:, 0:512],
                                       wv_src[k * 128:(k + 1) * 128, :],
                                       waits=[wv_free[b]])
                    for j, tt in enumerate(range(half * 8, half * 8 + 8)):
                        last_mm = t_mm(
                            psh[j][:, :],
                            xn[:, k * 2048 + tt * 128: k * 2048 + tt * 128 + 128],
                            wv_stage[b][:, 0:512],
                            start=(k == 0), stop=(k == NSLAB - 1),
                            waits=[tok_w, xn_tb[tt // 4]] +
                                  ([bank_free[j]] if k == 0 else []),
                            tick=(j == 7))
                    wv_free[b] = last_mm
                    if k == NSLAB - 1:
                        for j, tt in enumerate(range(half * 8, half * 8 + 8)):
                            ev_tok = P.do(SE, lambda e, j=j, tt=tt: e.activation(
                                out=vT[:, tt * 512:(tt + 1) * 512], in_=psh[j][:, :],
                                func=AF.Copy), waits=[P.now(TE)], tick=True)
                            bank_free[j] = ev_tok
            tok_v = ev_tok
            wv_guard = ev_tok
            if _DBG == "v":
                dbg_dump(vT, nslab=4)
                return

            wq_free = [wv_guard, None]
            wq_i = [0]
            sfreeA = [[None] * NTB, [None] * NTB]
            sfreeB = [[None] * NTB, [None] * NTB]
            rows_free_m = [None] * NTB
            rows_free_r = [None] * NTB

            rope_par = [0]
            gstage_free = [None, None]

            def emit_proj(st):
                wb = wq_i[0] % 2
                wq_i[0] += 1
                is_gate = st.get("gate") is not None
                if not is_gate:
                    par = rope_par[0] % 2
                    rope_par[0] += 1
                    st["par"] = par
                    st["sA"] = scrA2[par]
                    st["sB"] = scrB2[par]
                wq_stage = wq_stage2[wb]
                st["wb"] = wb
                tok_w = s_cast.dma(wq_stage[:].rearrange("p (k c) -> p k c", k=16),
                                   st["w_cols"].rearrange("(k p) c -> p k c", p=128),
                                   waits=[wq_free[wb]])
                st["evac"] = []
                for tb in range(NTB):
                    sec = slice(tb * 512, (tb + 1) * 512)
                    j = tb % 2
                    last_mm = None
                    for k in range(NSLAB):
                        last_mm = t_mm(psh[j][:, :], wq_stage[:, k * 128:(k + 1) * 128],
                                       xn[:, k * 2048 + tb * 512: k * 2048 + (tb + 1) * 512],
                                       start=(k == 0), stop=(k == NSLAB - 1),
                                       waits=[tok_w, xn_tb[tb]] +
                                             ([bank_free[j]] if k == 0 else []),
                                       tick=(k == NSLAB - 1))
                    if is_gate:
                        gs = st["gate"]
                        gb = tb % 2
                        tok_e = P.do(SE, lambda e, j=j, gb=gb: e.activation(
                            out=gstage[gb], in_=psh[j][:, :], func=AF.Copy),
                            waits=[last_mm, gstage_free[gb]], tick=True)
                        gstage_free[gb] = s_st.dma(gate_scr[gs, :, tb, :], gstage[gb],
                                                   waits=[tok_e])
                    else:
                        tok_e = P.do(SE, lambda e, j=j, sA=st["sA"], sec=sec: e.activation(
                            out=sA[:, sec], in_=psh[j][:, :], func=AF.Copy),
                            waits=[last_mm, sfreeA[st["par"]][tb]], tick=True)
                    bank_free[j] = tok_e
                    st["evac"].append(tok_e)
                wq_free[wb] = st["evac"][-1]

            def emit_post(st):
                par = st["par"]
                sA, sB = st["sA"], st["sB"]
                ln_w, ln_b, dst, bound_idx = st["ln_w"], st["ln_b"], st["dst"], st["bidx"]
                tok_o = None
                tmax = None
                for tb in range(NTB):
                    sec = slice(tb * 512, (tb + 1) * 512)
                    tabC = tabs[:, tb * 512:(tb + 1) * 512]
                    tabS = tabs[:, 2048 + tb * 512: 2048 + (tb + 1) * 512]
                    tok_e = st["evac"][tb]
                    tok_q = tok_e
                    tok_sumsq = None
                    S, Q, R = 2 + tb % 2, 4 + tb % 2, 6 + tb % 2
                    if ln_w is not None:
                        tok_sq = P.do(SE, lambda e, sA=sA, sB=sB, sec=sec: e.activation(
                            out=sB[:, sec], in_=sA[:, sec], func=AF.Square),
                            waits=[tok_e, sfreeB[par][tb]], tick=True)
                        tma = t_mm(psh[S][0:1, :], ones_c[:], sA[:, sec],
                                   start=True, stop=True,
                                   waits=[tok_e, bank_free[S]], tick=True)
                        te1 = P.do(SE, lambda e, sec=sec, S=S: e.activation(
                            out=m_row[0:1, sec], in_=psh[S][0:1, :],
                            func=AF.Copy, scale=1.0 / 128.0),
                            waits=[tma, rows_free_m[tb]], tick=True)
                        bank_free[S] = te1
                        tmb = t_mm(psh[Q][0:1, :], ones_c[:], sB[:, sec],
                                   start=True, stop=True,
                                   waits=[tok_sq, bank_free[Q]], tick=True)
                        tok_sumsq = tmb
                        te2 = P.do(SE, lambda e, sec=sec, Q=Q: e.activation(
                            out=r_row[0:1, sec], in_=psh[Q][0:1, :],
                            func=AF.Copy, scale=1.0 / 128.0),
                            waits=[tmb, rows_free_r[tb]], tick=True)
                        bank_free[Q] = te2
                        ta = P.do(VE, lambda e, sec=sec, S=S: e.tensor_mul(
                            psh[S][0:1, :], m_row[0:1, sec], m_row[0:1, sec]),
                            waits=[te1, bank_free[S]], tick=True)
                        tv = P.do(VE, lambda e, sec=sec, S=S: e.tensor_sub(
                            r_row[0:1, sec], r_row[0:1, sec], psh[S][0:1, :]),
                            waits=[ta, te2], tick=True)
                        bank_free[S] = tv
                        tsd = P.do(SE, lambda e, sec=sec: e.activation(
                            out=r_row[0:1, sec], in_=r_row[0:1, sec], func=AF.Ln,
                            bias=epsc[0:1, 1:2]), waits=[tv], tick=True)
                        tr = P.do(SE, lambda e, sec=sec: e.activation(
                            out=r_row[0:1, sec], in_=r_row[0:1, sec], func=AF.Exp,
                            scale=-0.5), waits=[tsd], tick=True)
                        tmr = P.do(VE, lambda e, sec=sec: e.tensor_mul(
                            m_row[0:1, sec], m_row[0:1, sec], r_row[0:1, sec]),
                            waits=[tr], tick=True)
                        t_r = t_mm(psh[R][:, :], onesf_r[:], r_row[0:1, sec],
                                   start=True, stop=True,
                                   waits=[tr, bank_free[R]], tick=True)
                        t_m = t_mm(psh[S][:, :], ones_r[:], m_row[0:1, sec],
                                   start=True, stop=True,
                                   waits=[tmr, bank_free[S]], tick=True)
                        rows_free_r[tb] = ("MULTI", [t_r, tmr])
                        rows_free_m[tb] = t_m
                        tn1 = P.do(VE, lambda e, sA=sA, sec=sec, R=R: e.tensor_mul(
                            sA[:, sec], sA[:, sec], psh[R][:, :]),
                            waits=[t_r], tick=True)
                        tn2 = P.do(VE, lambda e, sA=sA, sec=sec, S=S: e.tensor_sub(
                            sA[:, sec], sA[:, sec], psh[S][:, :]),
                            waits=[t_m, tn1], tick=True)
                        bank_free[R] = tn2
                        bank_free[S] = tn2
                        tok_q = P.do(SE, lambda e, sA=sA, sec=sec, ln_w=ln_w, ln_b=ln_b:
                                     e.activation(out=sA[:, sec], in_=sA[:, sec],
                                                  func=AF.Identity,
                                                  scale=ln_w[:, 0:1], bias=ln_b[:, 0:1]),
                                     waits=[tn2], tick=True)
                    swp_w = [tok_q]
                    if tok_sumsq is not None:
                        swp_w.append(tok_sumsq)
                    else:
                        swp_w.append(sfreeB[par][tb])
                    sw1 = P.do(SE, lambda e, sA=sA, sB=sB, sec=sec: e.activation(
                        out=sB[0:64, sec], in_=sA[64:128, sec], func=AF.Copy),
                        waits=swp_w, tick=True)
                    sw2 = P.do(SE, lambda e, sA=sA, sB=sB, sec=sec: e.activation(
                        out=sB[64:128, sec], in_=sA[0:64, sec], func=AF.Copy),
                        waits=[tok_q], tick=True)
                    tm1 = P.do(VE, lambda e, sA=sA, sec=sec, tabC=tabC: e.tensor_mul(
                        sA[:, sec], sA[:, sec], tabC),
                        waits=[tok_tabs, tok_q, sw1, sw2], tick=True)
                    tm2 = P.do(VE, lambda e, sB=sB, sec=sec, tabS=tabS: e.tensor_mul(
                        sB[:, sec], sB[:, sec], tabS),
                        waits=[sw1, sw2], tick=True)
                    tok_o = P.do(VE, lambda e, dst=dst, sA=sA, sB=sB, sec=sec:
                                 e.tensor_add(dst[:, sec], sA[:, sec], sB[:, sec]),
                                 waits=[tm1, tm2], tick=True)
                    sfreeA[par][tb] = tok_o
                    tsq2 = P.do(SE, lambda e, dst=dst, sB=sB, sec=sec: e.activation(
                        out=sB[:, sec], in_=dst[:, sec], func=AF.Square),
                        waits=[tok_o], tick=True)
                    tmb2 = t_mm(psh[Q][0:1, :], ones_c[:], sB[:, sec],
                                start=True, stop=True,
                                waits=[tsq2, bank_free[Q]], tick=True)
                    tred = P.do(VE, lambda e, Q=Q: e.reduce_max(
                        tiny[0:1, 32:33], psh[Q][0:1, :], axis=AX.X),
                        waits=[tmb2], tick=True)
                    bank_free[Q] = tred
                    sfreeB[par][tb] = tred
                    if tb == 0:
                        tmax = P.do(VE, lambda e, bi=bound_idx: e.tensor_copy(
                            tiny[0:1, bi:bi + 1], tiny[0:1, 32:33]), waits=[tred],
                            tick=True)
                    else:
                        tmax = P.do(VE, lambda e, bi=bound_idx: e.tensor_max(
                            tiny[0:1, bi:bi + 1], tiny[0:1, bi:bi + 1],
                            tiny[0:1, 32:33]), waits=[tred], tick=True)
                st["tok_o"] = tok_o
                st["tmax"] = tmax

            slabs = []
            kst, qst = [], []
            for g in range(NKV):
                st = {"w_cols": wk_src[:, g * 128:(g + 1) * 128],
                      "dst": kT[:, g * 2048:(g + 1) * 2048],
                      "ln_w": knw_sb if is_b else None,
                      "ln_b": knb_sb if is_b else None, "bidx": 16 + g}
                slabs.append(st)
                kst.append(st)
            for h in range(NH):
                st = {"w_cols": wq_src[:, h * 128:(h + 1) * 128],
                      "dst": qhat[:, h * 2048:(h + 1) * 2048],
                      "ln_w": qnw_sb if is_b else None,
                      "ln_b": qnb_sb if is_b else None, "bidx": h}
                slabs.append(st)
                qst.append(st)
                if is_b:
                    slabs.append({"w_cols": wqkv_b[:, 2048 + h * 128: 2048 + (h + 1) * 128],
                                  "gate": h})
            prev = None
            for st in slabs:
                emit_proj(st)
                if prev is not None:
                    emit_post(prev)
                prev = None if st.get("gate") is not None else st
            if prev is not None:
                emit_post(prev)
            ktoks = [(st["tok_o"], st["tmax"]) for st in kst]
            qtoks = [(st["tok_o"], st["tmax"]) for st in qst]
            if _DBG == "k":
                dbg_dump(kT, nslab=4)
                return
            if _DBG == "q":
                dbg_dump(qhat)
                return

            gate_toks = [gstage_free[0], gstage_free[1]]

            bound_toks = [t for (_, t) in qtoks] + [t for (_, t) in ktoks]
            nkv_view = tiny[0:1, 16:20].unsqueeze(2).broadcast_to([1, 4, 4])
            nq_view = tiny[0:1, 0:16].rearrange("a (b c) -> a b c", b=4)
            out_view = tiny[0:1, 40:56].rearrange("a (b c) -> a b c", b=4)
            t_b1 = P.do(VE, lambda e: e.tensor_mul(out_view, nq_view, nkv_view),
                        waits=bound_toks, tick=True)
            t_b2 = P.do(SE, lambda e: e.activation(out=tiny[0:1, 40:56],
                                                   in_=tiny[0:1, 40:56], func=AF.Sqrt),
                        waits=[t_b1], tick=True)
            t_b3 = P.do(VE, lambda e: e.tensor_scalar_mul(tiny[0:1, 40:56],
                                                          tiny[0:1, 40:56], -SCALE),
                        waits=[t_b2], tick=True)
            t_b4 = t_mm(psh[2][:, 0:16], onesf_r[:], tiny[0:1, 40:56],
                        start=True, stop=True, waits=[t_b3, bank_free[2]], tick=True)
            tok_negB = P.do(SE, lambda e: e.activation(out=negB[:], in_=psh[2][:, 0:16],
                                                       func=AF.Copy),
                            waits=[t_b4], tick=True)
            bank_free[2] = tok_negB

            P.barrier()
            wo_toks = []
            for s4 in range(4):
                wo_toks.append(s_cast.dma(
                    wo_sb[:, s4 * 8192:(s4 + 1) * 8192].rearrange("p (s d) -> p s d", s=4),
                    wo_src[s4 * 512:(s4 + 1) * 512, :].rearrange("(s p) d -> p s d", p=128)))
            tok_wo = ("MULTI", wo_toks)

            plan = [kv_plan(qb, windowed=(not is_b)) for qb in range(NTB)]
            gate_ld_free = [None, None]
            ostage_free = [None, None]
            qb_list = [1] if _DBG == "attnout" else list(range(NTB))
            phat_free = [None, None, None]
            z16_free = [None]
            for qb in qb_list:
                att_toks = []
                np_count = 0

                def head_main(h):
                    nonlocal np_count
                    g = h // 4
                    av_bank = 4 + (h % 2)
                    zb = 6 if h % 2 == 0 else 3
                    ents = plan[qb]
                    nkv = len(ents)

                    def stage_a(kvi):
                        jt, a_lo, a_hi, mks = ents[kvi]
                        nonlocal np_count
                        cols = slice(a_lo * 128, (a_hi + 1) * 128)
                        sbank = kvi % 3
                        tok_s = t_mm(psh[sbank][:, cols],
                                     kT[:, g * 2048 + jt * 128: g * 2048 + (jt + 1) * 128],
                                     qhat[:, h * 2048 + qb * 512 + a_lo * 128:
                                          h * 2048 + qb * 512 + (a_hi + 1) * 128],
                                     start=True, stop=True,
                                     waits=[qtoks[h][0], ktoks[g][0], bank_free[sbank]],
                                     tick=True)
                        for (a, mid) in mks:
                            mc = slice((mid - 1) * 128, mid * 128)
                            tok_s = P.do(VE, lambda e, sbank=sbank, a=a, mc=mc: e.tensor_add(
                                psh[sbank][:, a * 128:(a + 1) * 128],
                                psh[sbank][:, a * 128:(a + 1) * 128], masks_sb[:, mc]),
                                waits=[tok_s, t_consts], tick=True)
                        pi = np_count % 3
                        np_count += 1
                        pt = phat[pi]
                        tok_e = P.do(SE, lambda e, sbank=sbank, pt=pt, cols=cols, h=h:
                                     e.activation(out=pt[:, cols], in_=psh[sbank][:, cols],
                                                  func=AF.Exp, scale=SCALE,
                                                  bias=negB[:, h:h + 1]),
                                     waits=[tok_s, tok_negB, phat_free[pi]], tick=True)
                        bank_free[sbank] = tok_e
                        return (kvi, pi, pt, cols, tok_e)

                    def stage_b(sa):
                        kvi, pi, pt, cols, tok_e = sa
                        jt = ents[kvi][0]
                        first = kvi == 0
                        is_last = kvi == nkv - 1
                        t_mm(psh[zb][0:1, cols], ones_c[:], pt[:, cols],
                             start=first, stop=is_last,
                             waits=[tok_e] + ([bank_free[zb]] if first else []), tick=False)
                        tok_av = t_mm(psh[av_bank][:, cols],
                                      vT[:, jt * 512 + g * 128: jt * 512 + (g + 1) * 128],
                                      pt[:, cols],
                                      start=first, stop=is_last,
                                      waits=[tok_e, tok_v] +
                                            ([bank_free[av_bank]] if first else []),
                                      tick=True)
                        phat_free[pi] = tok_av

                    from collections import deque
                    pend = deque()
                    for kvi in range(nkv):
                        pend.append(stage_a(kvi))
                        if len(pend) > 2:
                            stage_b(pend.popleft())
                    while pend:
                        stage_b(pend.popleft())
                    return (h, av_bank, zb, P.now(TE))

                def head_epi(hs):
                    h, av_bank, zb, tok_avdone = hs
                    tok_ln = P.do(SE, lambda e, zb=zb: e.activation(
                        out=z_row[:], in_=psh[zb][0:1, :], func=AF.Ln),
                        waits=[tok_avdone, z16_free[0]], tick=True)
                    bank_free[zb] = tok_ln
                    tok_zr = P.do(SE, lambda e: e.activation(
                        out=z16[:], in_=z_row[:], func=AF.Exp, scale=-1.0),
                        waits=[tok_ln], tick=True)
                    t_bz = t_mm(psh[7][:, :], ones_r[:], z16[:],
                                start=True, stop=True, waits=[tok_zr, bank_free[7]],
                                tick=True)
                    z16_free[0] = t_bz
                    rz = rzstage[h % 2]
                    tok_rz = P.do(SE, lambda e, rz=rz: e.activation(
                        out=rz, in_=psh[7][:, :], func=AF.Copy), waits=[t_bz], tick=True)
                    bank_free[7] = tok_rz
                    tok_ao = P.do(VE, lambda e, av_bank=av_bank, h=h, rz=rz: e.tensor_mul(
                        attnout[:, h * 512:(h + 1) * 512], psh[av_bank][:, :], rz),
                        waits=[tok_rz, tok_avdone], tick=True)
                    bank_free[av_bank] = tok_ao
                    att_toks.append(tok_ao)

                pend_h = None
                for h in range(NH):
                    hs = head_main(h)
                    if pend_h is not None:
                        head_epi(pend_h)
                    pend_h = hs
                head_epi(pend_h)

                if _DBG == "attnout":
                    dbg_dump(attnout, nslab=4)
                    return
                if is_b:
                    for s in range(NSLAB):
                        b = s % 2
                        tok_g = s_ld2.dma(gstage[b], gate_scr[s, :, qb, :],
                                          waits=[gate_ld_free[b]] + gate_toks)
                        tok_sg = P.do(SE, lambda e, b=b: e.activation(
                            out=gstage[b], in_=gstage[b], func=AF.Sigmoid),
                            waits=[tok_g, att_toks[s] if s < len(att_toks) else None],
                            tick=True)
                        tok_mulg = P.do(VE, lambda e, s=s, b=b: e.tensor_mul(
                            attnout[:, s * 512:(s + 1) * 512],
                            attnout[:, s * 512:(s + 1) * 512], gstage[b]),
                            waits=[tok_sg] + att_toks, tick=True)
                        gate_ld_free[b] = tok_mulg

                ao_ready = P.now(VE)
                for kd in range(NSLAB):
                    j = kd % 2
                    last_mm = None
                    for s in range(NSLAB):
                        last_mm = t_mm(psh[j][:, :],
                                       wo_sb[:, s * 2048 + kd * 128: s * 2048 + (kd + 1) * 128],
                                       attnout[:, s * 512:(s + 1) * 512],
                                       start=(s == 0), stop=(s == NSLAB - 1),
                                       waits=[tok_wo, ao_ready] +
                                             ([bank_free[j]] if s == 0 else []),
                                       tick=(s == NSLAB - 1))
                    ot = ostage[j]
                    tok_oe = P.do(SE, lambda e, j=j, ot=ot: e.activation(
                        out=ot, in_=psh[j][:, :], func=AF.Copy),
                        waits=[last_mm, ostage_free[j]], tick=True)
                    bank_free[j] = tok_oe
                    extra_acc = getattr(P, "first_acc_extra", None)
                    P.first_acc_extra = None
                    ostage_free[j] = s_acc.dma(
                        R_dm[kd * 128:(kd + 1) * 128, qb * 512:(qb + 1) * 512],
                        ot, waits=[tok_oe, extra_acc], accum_op=ALU.add)
            P.barrier()

        def emit_ffn(src_dm, extra_wait=()):
            tok_xn, xn_tb = emit_ln(src_dm, extra_wait=extra_wait)
            wgu_free = [None, None, None, None]
            wgu_u8_free = [tok_xn, tok_xn, tok_xn, tok_xn]
            h_free = [None, None, None]
            hb_pre = []
            nt = 0
            for it in range(64):
                bg, bu = (it % 2) * 2, (it % 2) * 2 + 1
                tg8 = s_ld.dma(wgu_u8[bg][:, 0:2048], gw_blk[it].rearrange("p k i -> p (k i)"),
                               waits=[wgu_u8_free[bg]])
                tu8 = s_ld2.dma(wgu_u8[bu][:, 0:2048], uw_blk[it].rearrange("p k i -> p (k i)"),
                                waits=[wgu_u8_free[bu]])
                tok_wg = P.do(SE, lambda e, bg=bg: e.activation(
                    out=wgu[bg], in_=wgu_u8[bg][:, 0:2048], func=AF.Copy, bias=-128.0),
                    waits=[tg8, wgu_free[bg]], tick=True)
                wgu_u8_free[bg] = tok_wg
                tok_wu = P.do(SE, lambda e, bu=bu: e.activation(
                    out=wgu[bu], in_=wgu_u8[bu][:, 0:2048], func=AF.Copy, bias=-128.0),
                    waits=[tu8, wgu_free[bu]], tick=True)
                wgu_u8_free[bu] = tok_wu
                for tb in range(NTB):
                    jg = tb % 2
                    ju = 2 + tb % 2
                    mg = mu = None
                    for k in range(NSLAB):
                        rhs = xn[:, k * 2048 + tb * 512: k * 2048 + (tb + 1) * 512]
                        mg = t_mm(psh[jg][:, :], wgu[bg][:, k * 128:(k + 1) * 128], rhs,
                                  start=(k == 0), stop=(k == NSLAB - 1),
                                  waits=[tok_wg, xn_tb[tb]] + ([bank_free[jg]] if k == 0 else []),
                                  tick=(k == NSLAB - 1))
                    for k in range(NSLAB):
                        rhs = xn[:, k * 2048 + tb * 512: k * 2048 + (tb + 1) * 512]
                        mu = t_mm(psh[ju][:, :], wgu[bu][:, k * 128:(k + 1) * 128], rhs,
                                  start=(k == 0), stop=(k == NSLAB - 1),
                                  waits=[tok_wu, xn_tb[tb]] + ([bank_free[ju]] if k == 0 else []),
                                  tick=(k == NSLAB - 1))
                    if tb == NTB - 1:
                        wgu_free[bg] = mg
                        wgu_free[bu] = mu
                    fs = fstage[tb % 2]
                    tok_sg = P.do(SE, lambda e, jg=jg, fs=fs, it=it: e.activation(
                        out=fs, in_=psh[jg][:, :], func=AF.Silu,
                        scale=gsc_sb[:, it:it + 1]),
                        waits=[mg, t_consts], tick=True)
                    bank_free[jg] = tok_sg
                    ht = hstage[nt % 3]
                    tok_h = P.do(VE, lambda e, ju=ju, fs=fs, ht=ht, it=it:
                                 e.scalar_tensor_tensor(ht, psh[ju][:, :],
                                                        usc_sb[:, it:it + 1], fs,
                                                        ALU.mult, ALU.mult),
                                 waits=[tok_sg, mu, h_free[nt % 3]], tick=True)
                    bank_free[ju] = tok_h
                    h_free[nt % 3] = s_st.dma(h_scr[it, :, tb, :], ht, waits=[tok_h])
                    nt += 1
                if it % 16 == 15:
                    q4 = it // 16
                    hb_pre.append(s_ld2.dma(
                        h_sb[:, q4 * 8192:(q4 + 1) * 8192].rearrange("p (it t) -> p it t", it=16),
                        h_scr[q4 * 16:(q4 + 1) * 16, :, 0, :].rearrange("it p t -> p it t"),
                        waits=[s_st.wait_all()]))
            P.barrier()
            if _DBG == "ffng":
                pass
            fstage_free = [None, None]
            wd_free = [None, None]
            wd_u8_free = [None, None]
            hbufs = [h_sb, h_sb2]
            hb_tok = {0: hb_pre}
            h_guard = [None, None]
            for tb in range(NTB):
                if tb + 1 < NTB:
                    nb = hbufs[(tb + 1) % 2]
                    toks = []
                    for q4 in range(4):
                        toks.append(s_ld2.dma(
                            nb[:, q4 * 8192:(q4 + 1) * 8192].rearrange("p (it t) -> p it t", it=16),
                            h_scr[q4 * 16:(q4 + 1) * 16, :, tb + 1, :].rearrange("it p t -> p it t"),
                            waits=[h_guard[(tb + 1) % 2]]))
                    hb_tok[tb + 1] = toks
                hb_toks = hb_tok[tb]
                hbuf = hbufs[tb % 2]
                if _DBG == "ffnh" and tb == 0:
                    dbg_dump(h_sb)
                    return
                last_mm = None
                for kd in range(16):
                    b = kd % 2
                    t8 = s_ld.dma(wd_u8[b][:, 0:8192],
                                  dw_blk[kd].rearrange("p it d -> p (it d)"),
                                  waits=[wd_u8_free[b]])
                    tok_wd = P.do(SE, lambda e, b=b: e.activation(
                        out=wd_buf[b], in_=wd_u8[b][:, 0:8192], func=AF.Copy, bias=-128.0),
                        waits=[t8, wd_free[b]], tick=True)
                    wd_u8_free[b] = tok_wd
                    j = kd % 2
                    last_mm = None
                    for it in range(64):
                        last_mm = t_mm(psh[j][:, :],
                                       wd_buf[b][:, it * 128:(it + 1) * 128],
                                       hbuf[:, it * 512:(it + 1) * 512],
                                       start=(it == 0), stop=(it == 63),
                                       waits=[tok_wd, hb_toks[it // 16]] +
                                             ([bank_free[j]] if it == 0 else []),
                                       tick=(it == 63))
                    wd_free[b] = last_mm
                    ot = ostage[j]
                    tok_oe = P.do(SE, lambda e, j=j, ot=ot, kd=kd: e.activation(
                        out=ot, in_=psh[j][:, :], func=AF.Copy,
                        scale=dsc_sb[:, kd:kd + 1]),
                        waits=[last_mm, fstage_free[j]], tick=True)
                    bank_free[j] = tok_oe
                    fstage_free[j] = s_acc.dma(
                        R_dm[kd * 128:(kd + 1) * 128, tb * 512:(tb + 1) * 512],
                        ot, waits=[tok_oe], accum_op=ALU.add)
                h_guard[tb % 2] = last_mm
            P.barrier()

        P.first_acc_extra = t_Rinit
        stage_list = _STAGE_LIST if _STAGE_LIST is not None else list(range(1, _STAGES + 1))
        for si, s in enumerate(stage_list):
            src0 = x_dm if si == 0 else R_dm
            ew = [t_Rinit] if si == 0 else [s_acc.wait_all()]
            if s == 1:
                emit_attn("a", src0, extra_wait=ew)
            elif s == 2:
                emit_ffn(src0, extra_wait=ew)
            elif s == 3:
                emit_attn("b", src0, extra_wait=ew)
            elif s == 4:
                emit_ffn(src0, extra_wait=ew)
            if _DBG is not None:
                break

        P.barrier()
        for e in ENGS:
            P.do(e, lambda ee: None, waits=[s_acc.wait_all(), s_st.wait_all(),
                                            s_cast.wait_all()])

        block = es.enter_context(nc.Block())
        P.replay(block)
    return nc


def kernel(**inputs):
    g = host_prep(inputs)
    x = np.asarray(inputs["L0_attn_a_x"], dtype=np.float32)
    nc = build_program()
    in_maps = []
    for c in range(8):
        m = dict(g)
        m["x_dm"] = np.ascontiguousarray(x[c].T)
        in_maps.append(m)
    res = run_bass_kernel_spmd(nc, in_maps, core_ids=list(range(8)))
    out = np.stack([np.asarray(r["out_dm"]).T for r in res.results], axis=0)
    return out


# version 09
# speedup vs baseline: 1.2992x; 1.2922x over previous
import numpy as np
import concourse.bass as bass
import concourse.mybir as mybir
from concourse.bass_utils import run_bass_kernel_spmd
from contextlib import ExitStack

F32 = mybir.dt.float32
BF16 = mybir.dt.bfloat16
I32 = mybir.dt.int32
U8 = mybir.dt.uint8
AF = mybir.ActivationFunctionType
ALU = mybir.AluOpType
AX = mybir.AxisListType

D = 2048
T = 2048
NH = 16
NKV = 4
HD = 128
NSLAB = 16
NTB = 4
SCALE = float(1.0 / np.sqrt(128.0))
WIN = 512
FRONT = 256
NEGINF = -1.0e9

_STAGES = 4
_STAGE_LIST = None
_DBG = None



def _deint(n_heads):
    p = []
    for h in range(n_heads):
        base = h * HD
        p.extend(base + 2 * f for f in range(64))
        p.extend(base + 2 * f + 1 for f in range(64))
    return np.asarray(p, dtype=np.int64)


def _rope_tables(theta):
    inv_freq = 1.0 / theta ** (np.arange(0, HD, 2)[: HD // 2].astype(np.float32) / HD)
    t = np.arange(T, dtype=np.float32)
    ang = np.outer(t, inv_freq)
    cos = np.cos(ang).T.astype(np.float32)
    sin = np.sin(ang).T.astype(np.float32)
    C2 = np.concatenate([cos, cos], axis=0)
    S2 = np.concatenate([-sin, sin], axis=0)
    return C2, S2


def host_prep(inp):
    g = {}
    pq = _deint(NH)
    pk = _deint(NKV)
    d1 = _deint(1)
    g["wq_a"] = np.ascontiguousarray(inp["L0_attn_a_wq"][:, pq])
    g["wk_a"] = np.ascontiguousarray(inp["L0_attn_a_wk"][:, pk])
    g["wv_a"] = np.ascontiguousarray(inp["L0_attn_a_wv"])
    g["wo_a"] = np.ascontiguousarray(inp["L0_attn_a_wo"])
    wq = np.asarray(inp["L0_attn_b_w_qkv"]).T
    g["wqkv_b"] = np.ascontiguousarray(np.concatenate(
        [wq[:, 0:2048][:, pq], wq[:, 2048:4096],
         wq[:, 4096:4608][:, pk], wq[:, 4608:5120]], axis=1))
    g["wo_b"] = np.ascontiguousarray(np.asarray(inp["L0_attn_b_w_o"]).T)
    g["qnw"] = np.ascontiguousarray(np.asarray(inp["L0_attn_b_q_norm_w"])[d1].reshape(128, 1))
    g["qnb"] = np.ascontiguousarray(np.asarray(inp["L0_attn_b_q_norm_b"])[d1].reshape(128, 1))
    g["knw"] = np.ascontiguousarray(np.asarray(inp["L0_attn_b_k_norm_w"])[d1].reshape(128, 1))
    g["knb"] = np.ascontiguousarray(np.asarray(inp["L0_attn_b_k_norm_b"])[d1].reshape(128, 1))
    def _u8(a):
        a = np.ascontiguousarray(np.asarray(a, dtype=np.int32))
        return a.view(np.uint8).reshape(a.shape + (4,))[..., 0]
    wt = _u8(inp["L0_ffn_gate_w"]).T.reshape(16, 128, 64, 128)
    g["gw_blk"] = np.ascontiguousarray(wt.transpose(2, 1, 0, 3))
    wt = _u8(inp["L0_ffn_up_w"]).T.reshape(16, 128, 64, 128)
    g["uw_blk"] = np.ascontiguousarray(wt.transpose(2, 1, 0, 3))
    wt = _u8(inp["L0_ffn_down_w"]).T.reshape(64, 128, 16, 128)
    g["dw_blk"] = np.ascontiguousarray(wt.transpose(2, 1, 0, 3))
    g["gsc"] = np.ascontiguousarray(np.asarray(inp["L0_ffn_gate_scale"]).reshape(64, 128).T)
    g["usc"] = np.ascontiguousarray(np.asarray(inp["L0_ffn_up_scale"]).reshape(64, 128).T)
    g["dsc"] = np.ascontiguousarray(np.asarray(inp["L0_ffn_down_scale"]).reshape(16, 128).T)
    C2a, S2a = _rope_tables(10000.0)
    C2b, S2b = _rope_tables(1000000.0)
    g["tab_a"] = np.ascontiguousarray(np.concatenate([C2a, S2a], axis=1))
    g["tab_b"] = np.ascontiguousarray(np.concatenate([C2b, S2b], axis=1))
    r = np.arange(128)
    P1 = np.where(r[None, :] >= r[:, None], 0.0, NEGINF).astype(np.float32)
    P2 = np.where(r[None, :] < r[:, None], 0.0, NEGINF).astype(np.float32)
    g["masks"] = np.ascontiguousarray(np.concatenate([P1, P2], axis=1))
    return g


def kv_plan(qb, windowed):
    out = []
    for jt in range(T // 128):
        subs = []
        for a in range(4):
            dd = qb * 512 + a * 128 - jt * 128
            if dd < 0:
                st = None
            elif not windowed or jt * 128 < FRONT:
                st = 1 if dd == 0 else 0
            else:
                if dd == 0:
                    st = 1
                elif dd <= WIN - 128:
                    st = 0
                elif dd == WIN:
                    st = 2
                else:
                    st = None
            subs.append(st)
        alive = [a for a in range(4) if subs[a] is not None]
        if not alive:
            continue
        a_lo, a_hi = min(alive), max(alive)
        assert alive == list(range(a_lo, a_hi + 1))
        out.append((jt, a_lo, a_hi, [(a, subs[a]) for a in alive if subs[a]]))
    return out



ENGS = ("tensor", "vector", "scalar", "sync", "gpsimd")


class Prog:
    def __init__(self, nc, es):
        self.nc = nc
        self.es = es
        self.ops = {e: [] for e in ENGS}
        self.engsem = {}
        self.cnt = {}
        self.streams = []
        for e in ("tensor", "vector", "scalar"):
            self.engsem[e] = self.new_sem("pg_" + e)
            self.cnt[e] = 0
        self.lastwait = {e: {} for e in ENGS}

    def new_sem(self, name):
        return self.es.enter_context(self.nc.semaphore(name))

    def _expand(self, waits):
        out = []
        for w in waits:
            if w is None:
                continue
            if isinstance(w, tuple) and len(w) == 2 and w[0] == "MULTI":
                out.extend(x for x in w[1] if x is not None)
            else:
                out.append(w)
        return out

    def do(self, eng, fn, waits=(), tick=False):
        ws = []
        lw = self.lastwait[eng]
        for (sem, val, producer) in self._expand(waits):
            if producer == eng:
                continue
            k = id(sem)
            if lw.get(k, -1) >= val:
                continue
            lw[k] = val
            ws.append((sem, val))
        tok = None
        esem = self.engsem.get(eng)
        if tick:
            self.cnt[eng] += 1
            tok = (esem, self.cnt[eng], eng)

        def f(e, ws=ws, fn=fn, tick=tick, esem=esem):
            for (s, v) in ws:
                e.wait_ge(s, v)
            ins = fn(e)
            if tick:
                ins.then_inc(esem, 1)
        self.ops[eng].append(f)
        return tok

    def now(self, eng):
        return (self.engsem[eng], self.cnt[eng], eng)

    def barrier(self):
        toks = [self.now(e) for e in ("tensor", "vector", "scalar")]
        for st in self.streams:
            toks.append(st.wait_all())
        for e in ENGS:
            self.do(e, lambda ee: None, waits=toks)

    def replay(self, block):
        def mk(eng):
            def body(e):
                for f in self.ops[eng]:
                    f(e)
            return body
        block.tensor(mk("tensor"))
        block.vector(mk("vector"))
        block.scalar(mk("scalar"))
        block.sync(mk("sync"))
        block.gpsimd(mk("gpsimd"))


class Stream:

    def __init__(self, P, queue, name, R=2):
        self.P = P
        self.queue = queue
        self.R = R
        self.sems = [P.new_sem(f"{name}{i}") for i in range(R)]
        self.counts = [0] * R
        self.n = 0
        P.streams.append(self)

    def dma(self, out, in_, waits=(), **kw):
        i = self.n % self.R
        self.n += 1
        self.counts[i] += 16
        sem = self.sems[i]
        val = self.counts[i]

        def fn(e, out=out, in_=in_, kw=kw, sem=sem):
            return e.dma_start(out=out, in_=in_, **kw).then_inc(sem, 16)
        self.P.do(self.queue, fn, waits=waits)
        return (sem, val, "dma")

    def wait_all(self):
        return ("MULTI", [(self.sems[i], self.counts[i], "dma")
                          for i in range(self.R) if self.counts[i] > 0])



def build_program():
    nc = bass.Bass("TRN2", target_bir_lowering=False)

    def din(name, shape, dt):
        return nc.dram_tensor(name, shape, dt, kind="ExternalInput").ap()

    x_dm = din("x_dm", [D, T], F32)
    wq_a = din("wq_a", [D, 2048], F32)
    wk_a = din("wk_a", [D, 512], F32)
    wv_a = din("wv_a", [D, 512], F32)
    wo_a = din("wo_a", [2048, D], F32)
    wqkv_b = din("wqkv_b", [D, 5120], F32)
    wo_b = din("wo_b", [2048, D], F32)
    qnw_d = din("qnw", [128, 1], F32)
    qnb_d = din("qnb", [128, 1], F32)
    knw_d = din("knw", [128, 1], F32)
    knb_d = din("knb", [128, 1], F32)
    gw_blk = din("gw_blk", [64, 128, 16, 128], U8)
    uw_blk = din("uw_blk", [64, 128, 16, 128], U8)
    dw_blk = din("dw_blk", [16, 128, 64, 128], U8)
    gsc_d = din("gsc", [128, 64], F32)
    usc_d = din("usc", [128, 64], F32)
    dsc_d = din("dsc", [128, 16], F32)
    tab_a = din("tab_a", [128, 2 * T], F32)
    tab_b = din("tab_b", [128, 2 * T], F32)
    masks_d = din("masks", [128, 256], F32)

    R_dm = nc.dram_tensor("out_dm", [D, T], F32, kind="ExternalOutput").ap()

    h_scr = nc.dram_tensor("h_scr", [64, 128, 4, 512], BF16, kind="Internal").ap()
    gate_scr = nc.dram_tensor("gate_scr", [16, 128, 4, 512], BF16, kind="Internal").ap()

    es = ExitStack()
    with es:
        A1 = es.enter_context(nc.sbuf_tensor("A1", [128, 32768], BF16))
        A2 = es.enter_context(nc.sbuf_tensor("A2", [128, 32768], BF16))
        A3 = es.enter_context(nc.sbuf_tensor("A3", [128, 16384], BF16))
        A4 = es.enter_context(nc.sbuf_tensor("A4", [128, 16384], BF16))
        m_row = es.enter_context(nc.sbuf_tensor("m_row", [1, T], BF16))
        r_row = es.enter_context(nc.sbuf_tensor("r_row", [1, T], F32))

        tiny = es.enter_context(nc.sbuf_tensor("tiny", [1, 64], F32))
        negB = es.enter_context(nc.sbuf_tensor("negB", [128, 16], F32))
        masks_sb = es.enter_context(nc.sbuf_tensor("masks_sb", [128, 256], BF16))
        gsc_sb = es.enter_context(nc.sbuf_tensor("gsc_sb", [128, 64], F32))
        usc_sb = es.enter_context(nc.sbuf_tensor("usc_sb", [128, 64], F32))
        dsc_sb = es.enter_context(nc.sbuf_tensor("dsc_sb", [128, 16], F32))
        qnw_sb = es.enter_context(nc.sbuf_tensor("qnw_sb", [128, 1], F32))
        qnb_sb = es.enter_context(nc.sbuf_tensor("qnb_sb", [128, 1], F32))
        knw_sb = es.enter_context(nc.sbuf_tensor("knw_sb", [128, 1], F32))
        knb_sb = es.enter_context(nc.sbuf_tensor("knb_sb", [128, 1], F32))
        epsc = es.enter_context(nc.sbuf_tensor("epsc", [128, 2], F32))
        ones_c = es.enter_context(nc.sbuf_tensor("ones_c", [128, 1], BF16))
        ones_r = es.enter_context(nc.sbuf_tensor("ones_r", [1, 128], BF16))
        onesf_r = es.enter_context(nc.sbuf_tensor("onesf_r", [1, 128], F32))
        oneshk_r = es.enter_context(nc.sbuf_tensor("oneshk_r", [1, 128], F32))
        oneshk_c = es.enter_context(nc.sbuf_tensor("oneshk_c", [128, 1], F32))

        psh = [es.enter_context(nc.psum_tensor(f"ps{i}", [128, 512], F32)) for i in range(8)]
        bank_free = [None] * 8

        P = Prog(nc, es)
        bcast_free = [None, None, None, None]

        s_ld = Stream(P, "sync", "ld", R=2)
        s_ld2 = Stream(P, "scalar", "ld2", R=2)
        s_cast = Stream(P, "gpsimd", "cast", R=4)
        s_acc = Stream(P, "gpsimd", "acc", R=2)
        s_st = Stream(P, "sync", "st", R=2)
        s_swp = Stream(P, "gpsimd", "swp", R=2)

        scrA2 = [A4[:, 0:2048], A4[:, 2048:4096]]
        scrB2 = [A4[:, 4096:6144], A4[:, 6144:8192]]
        xsq0 = A4[:, 0:2048]
        xsq1 = A4[:, 2048:4096]
        tabs = A4[:, 8192:12288]
        attnout = A4[:, 0:8192]
        phat = [A4[:, 8192 + i * 512: 8704 + i * 512] for i in range(4)]
        gstage = [A4[:, 10240 + i * 512: 10752 + i * 512] for i in range(2)]
        ostage = [A4[:, 11264 + i * 1024: 12288 + i * 1024].bitcast(F32)
                  for i in range(2)]
        rzstage = [A4[:, 13312:14336].bitcast(F32),
                   A4[:, 14336:15360].bitcast(F32)]
        z_row = A4[0:1, 15360:16384].bitcast(F32)
        z16 = A4[0:1, 9728:10240]

        wgu = [A4[:, 8192 + i * 2048: 10240 + i * 2048] for i in range(4)]
        wgu_u8 = [A4[:, 3584 + i * 1024: 4608 + i * 1024].bitcast(U8)
                  for i in range(4)]
        wd_u8 = [A4[:, 0 + i * 4096: 4096 + i * 4096].bitcast(U8)
                 for i in range(2)]
        hstage = [A4[:, 2048 + i * 512: 2560 + i * 512] for i in range(3)]
        fstage = [A4[:, 0 + i * 1024: 1024 + i * 1024].bitcast(F32) for i in range(2)]
        wq_stage2 = [A4[:, 12288:14336], A4[:, 14336:16384]]
        bcast_sb = [A4[:, 12288 + i * 512: 12800 + i * 512] for i in range(4)]
        wv_stage = [A4[:, 12288 + i * 512: 12800 + i * 512] for i in range(2)]

        xn = A1
        wo_sb = A1
        qhat = A2
        h_sb = A2
        h_sb2 = A1

        kT = A3[:, 0:8192]
        vT = A3[:, 8192:16384]
        wd_buf = [A3[:, i * 8192:(i + 1) * 8192] for i in range(2)]

        TE, VE, SE = "tensor", "vector", "scalar"

        def t_mm(out, lhsT, rhs, start, stop, waits=(), tick=False):
            return P.do(TE, lambda e: e.matmul(out, lhsT, rhs, start=start, stop=stop),
                        waits=waits, tick=tick)

        s_cast.dma(masks_sb[:], masks_d)
        for dst, src in ((gsc_sb, gsc_d), (usc_sb, usc_d), (dsc_sb, dsc_d),
                         (qnw_sb, qnw_d), (qnb_sb, qnb_d), (knw_sb, knw_d),
                         (knb_sb, knb_d)):
            s_ld.dma(dst[:], src)
        t_consts = s_ld.wait_all()
        P.do(VE, lambda e: e.memset(epsc[:, 0:1], 1e-5), tick=True)
        P.do(VE, lambda e: e.memset(epsc[:, 1:2], 1e-6), tick=True)
        P.do(VE, lambda e: e.memset(ones_c[:], 1.0), tick=True)
        P.do(VE, lambda e: e.memset(ones_r[:], 1.0), tick=True)
        P.do(VE, lambda e: e.memset(onesf_r[:], 1.0), tick=True)
        P.do(VE, lambda e: e.memset(oneshk_r[:], 1.0 / 128.0), tick=True)
        tok_ones = P.do(VE, lambda e: e.memset(oneshk_c[:], 1.0 / 128.0), tick=True)

        for c in range(4):
            s_st.dma(R_dm[c * 512:(c + 1) * 512, :], x_dm[c * 512:(c + 1) * 512, :])

        t_Rinit = s_st.wait_all()

        def dbg_dump(region, nslab=16):
            P.barrier()
            for k in range(nslab):
                s_cast.dma(R_dm[k * 128:(k + 1) * 128, :],
                           region[:, k * 2048:(k + 1) * 2048])
            P.barrier()

        def emit_ln(src_dm, extra_wait=()):
            ld_toks = []
            for k in range(NSLAB):
                ld_toks.append(s_cast.dma(xn[:, k * 2048:(k + 1) * 2048],
                               src_dm[k * 128:(k + 1) * 128, :],
                               waits=list(extra_wait) if k == 0 else []))
            mm_tok = None
            sq_prev = [None, None]
            for k in range(NSLAB):
                xk = xn[:, k * 2048:(k + 1) * 2048]
                xq = xsq0 if k % 2 == 0 else xsq1
                tok_s = P.do(SE, lambda e, xq=xq, xk=xk: e.activation(
                    out=xq, in_=xk, func=AF.Square),
                    waits=[ld_toks[k], sq_prev[k % 2]], tick=True)
                for tb in range(NTB):
                    t_mm(psh[tb][0:1, :], ones_c[:], xk[:, tb * 512:(tb + 1) * 512],
                         start=(k == 0), stop=(k == NSLAB - 1),
                         waits=[ld_toks[k], tok_ones] +
                               ([bank_free[tb]] if k == 0 else []))
                for tb in range(NTB):
                    mm_tok = t_mm(psh[4 + tb][0:1, :], ones_c[:],
                                  xq[:, tb * 512:(tb + 1) * 512],
                                  start=(k == 0), stop=(k == NSLAB - 1),
                                  waits=[tok_s] + ([bank_free[4 + tb]] if k == 0 else []),
                                  tick=(tb == NTB - 1))
                sq_prev[k % 2] = mm_tok
            row_toks = []
            for tb in range(NTB):
                row_toks.append(P.do(SE, lambda e, tb=tb: e.activation(
                    out=m_row[:, tb * 512:(tb + 1) * 512], in_=psh[tb][0:1, :],
                    func=AF.Copy, scale=1.0 / 2048.0), waits=[mm_tok], tick=True))
                row_toks.append(P.do(SE, lambda e, tb=tb: e.activation(
                    out=r_row[:, tb * 512:(tb + 1) * 512], in_=psh[4 + tb][0:1, :],
                    func=AF.Copy, scale=1.0 / 2048.0), waits=[mm_tok], tick=True))
            var_toks = []
            for tb in range(NTB):
                ta = P.do(VE, lambda e, tb=tb: e.tensor_mul(
                    psh[tb][0:1, :], m_row[:, tb * 512:(tb + 1) * 512],
                    m_row[:, tb * 512:(tb + 1) * 512]),
                    waits=row_toks, tick=True)
                tv = P.do(VE, lambda e, tb=tb: e.tensor_sub(
                    r_row[:, tb * 512:(tb + 1) * 512],
                    r_row[:, tb * 512:(tb + 1) * 512], psh[tb][0:1, :]),
                    waits=[ta], tick=True)
                var_toks.append(tv)
                bank_free[tb] = tv
            tok_sd = P.do(SE, lambda e: e.activation(out=r_row[:], in_=r_row[:],
                                                     func=AF.Ln, bias=epsc[0:1, 0:1]),
                          waits=var_toks, tick=True)
            tok_r = P.do(SE, lambda e: e.activation(out=r_row[:], in_=r_row[:],
                                                    func=AF.Exp, scale=-0.5),
                         waits=[tok_sd], tick=True)
            tok_mr = P.do(VE, lambda e: e.tensor_mul(m_row[:], m_row[:], r_row[:]),
                          waits=[tok_r], tick=True)
            bank_free[4] = tok_mr
            last = None
            xn_tb = []
            for tb in range(NTB):
                br, bm = (6, 7) if tb % 2 == 0 else (2, 3)
                t_r = t_mm(psh[br][:, :], onesf_r[:], r_row[:, tb * 512:(tb + 1) * 512],
                           start=True, stop=True, waits=[tok_r, bank_free[br]], tick=True)
                t_m = t_mm(psh[bm][:, :], ones_r[:], m_row[:, tb * 512:(tb + 1) * 512],
                           start=True, stop=True, waits=[tok_mr, bank_free[bm]], tick=True)
                rb = bcast_sb[tb % 2]
                mb = bcast_sb[2 + tb % 2]
                t_re = P.do(SE, lambda e, br=br, rb=rb: e.activation(
                    out=rb, in_=psh[br][:, :], func=AF.Copy),
                    waits=[t_r, bcast_free[tb % 2]], tick=True)
                bank_free[br] = t_re
                t_me = P.do(SE, lambda e, bm=bm, mb=mb: e.activation(
                    out=mb, in_=psh[bm][:, :], func=AF.Copy),
                    waits=[t_m, bcast_free[2 + tb % 2]], tick=True)
                bank_free[bm] = t_me
                for k in range(NSLAB):
                    xkb = xn[:, k * 2048 + tb * 512: k * 2048 + (tb + 1) * 512]
                    P.do(VE, lambda e, xkb=xkb, rb=rb: e.tensor_mul(xkb, xkb, rb),
                         waits=[t_re], tick=False)
                    last = P.do(VE, lambda e, xkb=xkb, mb=mb: e.tensor_sub(xkb, xkb, mb),
                                waits=[t_me], tick=(k == NSLAB - 1))
                bcast_free[tb % 2] = last
                bcast_free[2 + tb % 2] = last
                xn_tb.append(last)
            return last, xn_tb


        def emit_attn(which, src_dm, extra_wait=()):
            is_b = which == "b"
            tab_src = tab_b if is_b else tab_a
            wq_src = (wqkv_b[:, 0:2048] if is_b else wq_a)
            wk_src = (wqkv_b[:, 4096:4608] if is_b else wk_a)
            wv_src = (wqkv_b[:, 4608:5120] if is_b else wv_a)
            wo_src = wo_b if is_b else wo_a

            t1 = s_cast.dma(tabs[:, 0:2048], tab_src[:, 0:T])
            t2 = s_cast.dma(tabs[:, 2048:4096], tab_src[:, T:2 * T])
            tok_tabs = ("MULTI", [t1, t2])

            tok_xn, xn_tb = emit_ln(src_dm, extra_wait=extra_wait)
            if _DBG == "xn":
                dbg_dump(xn)
                return

            ev_tok = None
            wv_free = [tok_xn, tok_xn]
            for half in range(2):
                for k in range(NSLAB):
                    b = k % 2
                    tok_w = s_cast.dma(wv_stage[b][:, 0:512],
                                       wv_src[k * 128:(k + 1) * 128, :],
                                       waits=[wv_free[b]])
                    for j, tt in enumerate(range(half * 8, half * 8 + 8)):
                        last_mm = t_mm(
                            psh[j][:, :],
                            xn[:, k * 2048 + tt * 128: k * 2048 + tt * 128 + 128],
                            wv_stage[b][:, 0:512],
                            start=(k == 0), stop=(k == NSLAB - 1),
                            waits=[tok_w, xn_tb[tt // 4]] +
                                  ([bank_free[j]] if k == 0 else []),
                            tick=(j == 7))
                    wv_free[b] = last_mm
                    if k == NSLAB - 1:
                        for j, tt in enumerate(range(half * 8, half * 8 + 8)):
                            ev_tok = P.do(SE, lambda e, j=j, tt=tt: e.activation(
                                out=vT[:, tt * 512:(tt + 1) * 512], in_=psh[j][:, :],
                                func=AF.Copy), waits=[P.now(TE)], tick=True)
                            bank_free[j] = ev_tok
            tok_v = ev_tok
            wv_guard = ev_tok
            if _DBG == "v":
                dbg_dump(vT, nslab=4)
                return

            wq_free = [wv_guard, None]
            wq_i = [0]
            sfreeA = [[None] * NTB, [None] * NTB]
            sfreeB = [[None] * NTB, [None] * NTB]
            rows_free_m = [None] * NTB
            rows_free_r = [None] * NTB

            rope_par = [0]
            gstage_free = [None, None]

            def emit_proj(st):
                wb = wq_i[0] % 2
                wq_i[0] += 1
                is_gate = st.get("gate") is not None
                if not is_gate:
                    par = rope_par[0] % 2
                    rope_par[0] += 1
                    st["par"] = par
                    st["sA"] = scrA2[par]
                    st["sB"] = scrB2[par]
                wq_stage = wq_stage2[wb]
                st["wb"] = wb
                tok_w = s_cast.dma(wq_stage[:].rearrange("p (k c) -> p k c", k=16),
                                   st["w_cols"].rearrange("(k p) c -> p k c", p=128),
                                   waits=[wq_free[wb]])
                st["evac"] = []
                for tb in range(NTB):
                    sec = slice(tb * 512, (tb + 1) * 512)
                    j = tb % 2
                    last_mm = None
                    for k in range(NSLAB):
                        last_mm = t_mm(psh[j][:, :], wq_stage[:, k * 128:(k + 1) * 128],
                                       xn[:, k * 2048 + tb * 512: k * 2048 + (tb + 1) * 512],
                                       start=(k == 0), stop=(k == NSLAB - 1),
                                       waits=[tok_w, xn_tb[tb]] +
                                             ([bank_free[j]] if k == 0 else []),
                                       tick=(k == NSLAB - 1))
                    if is_gate:
                        gs = st["gate"]
                        gb = tb % 2
                        tok_e = P.do(SE, lambda e, j=j, gb=gb: e.activation(
                            out=gstage[gb], in_=psh[j][:, :], func=AF.Copy),
                            waits=[last_mm, gstage_free[gb]], tick=True)
                        gstage_free[gb] = s_st.dma(gate_scr[gs, :, tb, :], gstage[gb],
                                                   waits=[tok_e])
                    else:
                        tok_e = P.do(SE, lambda e, j=j, sA=st["sA"], sec=sec: e.activation(
                            out=sA[:, sec], in_=psh[j][:, :], func=AF.Copy),
                            waits=[last_mm, sfreeA[st["par"]][tb]], tick=True)
                    bank_free[j] = tok_e
                    st["evac"].append(tok_e)
                wq_free[wb] = st["evac"][-1]

            def emit_post(st):
                par = st["par"]
                sA, sB = st["sA"], st["sB"]
                ln_w, ln_b, dst, bound_idx = st["ln_w"], st["ln_b"], st["dst"], st["bidx"]
                tok_o = None
                tmax = None
                for tb in range(NTB):
                    sec = slice(tb * 512, (tb + 1) * 512)
                    tabC = tabs[:, tb * 512:(tb + 1) * 512]
                    tabS = tabs[:, 2048 + tb * 512: 2048 + (tb + 1) * 512]
                    tok_e = st["evac"][tb]
                    tok_q = tok_e
                    tok_sumsq = None
                    S, Q, R = 2 + tb % 2, 4 + tb % 2, 6 + tb % 2
                    if ln_w is not None:
                        tok_sq = P.do(SE, lambda e, sA=sA, sB=sB, sec=sec: e.activation(
                            out=sB[:, sec], in_=sA[:, sec], func=AF.Square),
                            waits=[tok_e, sfreeB[par][tb]], tick=True)
                        tma = t_mm(psh[S][0:1, :], ones_c[:], sA[:, sec],
                                   start=True, stop=True,
                                   waits=[tok_e, bank_free[S]], tick=True)
                        te1 = P.do(SE, lambda e, sec=sec, S=S: e.activation(
                            out=m_row[0:1, sec], in_=psh[S][0:1, :],
                            func=AF.Copy, scale=1.0 / 128.0),
                            waits=[tma, rows_free_m[tb]], tick=True)
                        bank_free[S] = te1
                        tmb = t_mm(psh[Q][0:1, :], ones_c[:], sB[:, sec],
                                   start=True, stop=True,
                                   waits=[tok_sq, bank_free[Q]], tick=True)
                        tok_sumsq = tmb
                        te2 = P.do(SE, lambda e, sec=sec, Q=Q: e.activation(
                            out=r_row[0:1, sec], in_=psh[Q][0:1, :],
                            func=AF.Copy, scale=1.0 / 128.0),
                            waits=[tmb, rows_free_r[tb]], tick=True)
                        bank_free[Q] = te2
                        ta = P.do(VE, lambda e, sec=sec, S=S: e.tensor_mul(
                            psh[S][0:1, :], m_row[0:1, sec], m_row[0:1, sec]),
                            waits=[te1, bank_free[S]], tick=True)
                        tv = P.do(VE, lambda e, sec=sec, S=S: e.tensor_sub(
                            r_row[0:1, sec], r_row[0:1, sec], psh[S][0:1, :]),
                            waits=[ta, te2], tick=True)
                        bank_free[S] = tv
                        tsd = P.do(SE, lambda e, sec=sec: e.activation(
                            out=r_row[0:1, sec], in_=r_row[0:1, sec], func=AF.Ln,
                            bias=epsc[0:1, 1:2]), waits=[tv], tick=True)
                        tr = P.do(SE, lambda e, sec=sec: e.activation(
                            out=r_row[0:1, sec], in_=r_row[0:1, sec], func=AF.Exp,
                            scale=-0.5), waits=[tsd], tick=True)
                        tmr = P.do(VE, lambda e, sec=sec: e.tensor_mul(
                            m_row[0:1, sec], m_row[0:1, sec], r_row[0:1, sec]),
                            waits=[tr], tick=True)
                        t_r = t_mm(psh[R][:, :], onesf_r[:], r_row[0:1, sec],
                                   start=True, stop=True,
                                   waits=[tr, bank_free[R]], tick=True)
                        t_m = t_mm(psh[S][:, :], ones_r[:], m_row[0:1, sec],
                                   start=True, stop=True,
                                   waits=[tmr, bank_free[S]], tick=True)
                        rows_free_r[tb] = ("MULTI", [t_r, tmr])
                        rows_free_m[tb] = t_m
                        tn1 = P.do(VE, lambda e, sA=sA, sec=sec, R=R: e.tensor_mul(
                            sA[:, sec], sA[:, sec], psh[R][:, :]),
                            waits=[t_r], tick=True)
                        tn2 = P.do(VE, lambda e, sA=sA, sec=sec, S=S: e.tensor_sub(
                            sA[:, sec], sA[:, sec], psh[S][:, :]),
                            waits=[t_m, tn1], tick=True)
                        bank_free[R] = tn2
                        bank_free[S] = tn2
                        tok_q = P.do(SE, lambda e, sA=sA, sec=sec, ln_w=ln_w, ln_b=ln_b:
                                     e.activation(out=sA[:, sec], in_=sA[:, sec],
                                                  func=AF.Identity,
                                                  scale=ln_w[:, 0:1], bias=ln_b[:, 0:1]),
                                     waits=[tn2], tick=True)
                    swp_w = [tok_q]
                    if tok_sumsq is not None:
                        swp_w.append(tok_sumsq)
                    else:
                        swp_w.append(sfreeB[par][tb])
                    sw1 = P.do(SE, lambda e, sA=sA, sB=sB, sec=sec: e.activation(
                        out=sB[0:64, sec], in_=sA[64:128, sec], func=AF.Copy),
                        waits=swp_w, tick=True)
                    sw2 = P.do(SE, lambda e, sA=sA, sB=sB, sec=sec: e.activation(
                        out=sB[64:128, sec], in_=sA[0:64, sec], func=AF.Copy),
                        waits=[tok_q], tick=True)
                    tm1 = P.do(VE, lambda e, sA=sA, sec=sec, tabC=tabC: e.tensor_mul(
                        sA[:, sec], sA[:, sec], tabC),
                        waits=[tok_tabs, tok_q, sw1, sw2], tick=True)
                    tm2 = P.do(VE, lambda e, sB=sB, sec=sec, tabS=tabS: e.tensor_mul(
                        sB[:, sec], sB[:, sec], tabS),
                        waits=[sw1, sw2], tick=True)
                    tok_o = P.do(VE, lambda e, dst=dst, sA=sA, sB=sB, sec=sec:
                                 e.tensor_add(dst[:, sec], sA[:, sec], sB[:, sec]),
                                 waits=[tm1, tm2], tick=True)
                    sfreeA[par][tb] = tok_o
                    tsq2 = P.do(SE, lambda e, dst=dst, sB=sB, sec=sec: e.activation(
                        out=sB[:, sec], in_=dst[:, sec], func=AF.Square),
                        waits=[tok_o], tick=True)
                    tmb2 = t_mm(psh[Q][0:1, :], ones_c[:], sB[:, sec],
                                start=True, stop=True,
                                waits=[tsq2, bank_free[Q]], tick=True)
                    tred = P.do(VE, lambda e, Q=Q: e.reduce_max(
                        tiny[0:1, 32:33], psh[Q][0:1, :], axis=AX.X),
                        waits=[tmb2], tick=True)
                    bank_free[Q] = tred
                    sfreeB[par][tb] = tred
                    if tb == 0:
                        tmax = P.do(VE, lambda e, bi=bound_idx: e.tensor_copy(
                            tiny[0:1, bi:bi + 1], tiny[0:1, 32:33]), waits=[tred],
                            tick=True)
                    else:
                        tmax = P.do(VE, lambda e, bi=bound_idx: e.tensor_max(
                            tiny[0:1, bi:bi + 1], tiny[0:1, bi:bi + 1],
                            tiny[0:1, 32:33]), waits=[tred], tick=True)
                st["tok_o"] = tok_o
                st["tmax"] = tmax

            slabs = []
            kst, qst = [], []
            for g in range(NKV):
                st = {"w_cols": wk_src[:, g * 128:(g + 1) * 128],
                      "dst": kT[:, g * 2048:(g + 1) * 2048],
                      "ln_w": knw_sb if is_b else None,
                      "ln_b": knb_sb if is_b else None, "bidx": 16 + g}
                slabs.append(st)
                kst.append(st)
            for h in range(NH):
                st = {"w_cols": wq_src[:, h * 128:(h + 1) * 128],
                      "dst": qhat[:, h * 2048:(h + 1) * 2048],
                      "ln_w": qnw_sb if is_b else None,
                      "ln_b": qnb_sb if is_b else None, "bidx": h}
                slabs.append(st)
                qst.append(st)
                if is_b:
                    slabs.append({"w_cols": wqkv_b[:, 2048 + h * 128: 2048 + (h + 1) * 128],
                                  "gate": h})
            prev = None
            for st in slabs:
                emit_proj(st)
                if prev is not None:
                    emit_post(prev)
                prev = None if st.get("gate") is not None else st
            if prev is not None:
                emit_post(prev)
            ktoks = [(st["tok_o"], st["tmax"]) for st in kst]
            qtoks = [(st["tok_o"], st["tmax"]) for st in qst]
            if _DBG == "k":
                dbg_dump(kT, nslab=4)
                return
            if _DBG == "q":
                dbg_dump(qhat)
                return

            gate_toks = [gstage_free[0], gstage_free[1]]

            bound_toks = [t for (_, t) in qtoks] + [t for (_, t) in ktoks]
            nkv_view = tiny[0:1, 16:20].unsqueeze(2).broadcast_to([1, 4, 4])
            nq_view = tiny[0:1, 0:16].rearrange("a (b c) -> a b c", b=4)
            out_view = tiny[0:1, 40:56].rearrange("a (b c) -> a b c", b=4)
            t_b1 = P.do(VE, lambda e: e.tensor_mul(out_view, nq_view, nkv_view),
                        waits=bound_toks, tick=True)
            t_b2 = P.do(SE, lambda e: e.activation(out=tiny[0:1, 40:56],
                                                   in_=tiny[0:1, 40:56], func=AF.Sqrt),
                        waits=[t_b1], tick=True)
            t_b3 = P.do(VE, lambda e: e.tensor_scalar_mul(tiny[0:1, 40:56],
                                                          tiny[0:1, 40:56], -SCALE),
                        waits=[t_b2], tick=True)
            t_b4 = t_mm(psh[2][:, 0:16], onesf_r[:], tiny[0:1, 40:56],
                        start=True, stop=True, waits=[t_b3, bank_free[2]], tick=True)
            tok_negB = P.do(SE, lambda e: e.activation(out=negB[:], in_=psh[2][:, 0:16],
                                                       func=AF.Copy),
                            waits=[t_b4], tick=True)
            bank_free[2] = tok_negB

            P.barrier()
            wo_toks = []
            for s4 in range(4):
                wo_toks.append(s_cast.dma(
                    wo_sb[:, s4 * 8192:(s4 + 1) * 8192].rearrange("p (s d) -> p s d", s=4),
                    wo_src[s4 * 512:(s4 + 1) * 512, :].rearrange("(s p) d -> p s d", p=128)))
            tok_wo = ("MULTI", wo_toks)

            plan = [kv_plan(qb, windowed=(not is_b)) for qb in range(NTB)]
            gate_ld_free = [None, None]
            ostage_free = [None, None]
            qb_list = [1] if _DBG == "attnout" else list(range(NTB))
            phat_free = [None, None, None]
            z16_free = [None]
            for qb in qb_list:
                att_toks = []
                np_count = 0

                def head_main(h):
                    nonlocal np_count
                    g = h // 4
                    av_bank = 4 + (h % 2)
                    zb = 6 if h % 2 == 0 else 3
                    ents = plan[qb]
                    nkv = len(ents)

                    def stage_a(kvi):
                        jt, a_lo, a_hi, mks = ents[kvi]
                        nonlocal np_count
                        cols = slice(a_lo * 128, (a_hi + 1) * 128)
                        sbank = kvi % 3
                        tok_s = t_mm(psh[sbank][:, cols],
                                     kT[:, g * 2048 + jt * 128: g * 2048 + (jt + 1) * 128],
                                     qhat[:, h * 2048 + qb * 512 + a_lo * 128:
                                          h * 2048 + qb * 512 + (a_hi + 1) * 128],
                                     start=True, stop=True,
                                     waits=[qtoks[h][0], ktoks[g][0], bank_free[sbank]],
                                     tick=True)
                        for (a, mid) in mks:
                            mc = slice((mid - 1) * 128, mid * 128)
                            tok_s = P.do(VE, lambda e, sbank=sbank, a=a, mc=mc: e.tensor_add(
                                psh[sbank][:, a * 128:(a + 1) * 128],
                                psh[sbank][:, a * 128:(a + 1) * 128], masks_sb[:, mc]),
                                waits=[tok_s, t_consts], tick=True)
                        pi = np_count % 3
                        np_count += 1
                        pt = phat[pi]
                        tok_e = P.do(SE, lambda e, sbank=sbank, pt=pt, cols=cols, h=h:
                                     e.activation(out=pt[:, cols], in_=psh[sbank][:, cols],
                                                  func=AF.Exp, scale=SCALE,
                                                  bias=negB[:, h:h + 1]),
                                     waits=[tok_s, tok_negB, phat_free[pi]], tick=True)
                        bank_free[sbank] = tok_e
                        return (kvi, pi, pt, cols, tok_e)

                    def stage_b(sa):
                        kvi, pi, pt, cols, tok_e = sa
                        jt = ents[kvi][0]
                        first = kvi == 0
                        is_last = kvi == nkv - 1
                        t_mm(psh[zb][0:1, cols], ones_c[:], pt[:, cols],
                             start=first, stop=is_last,
                             waits=[tok_e] + ([bank_free[zb]] if first else []), tick=False)
                        tok_av = t_mm(psh[av_bank][:, cols],
                                      vT[:, jt * 512 + g * 128: jt * 512 + (g + 1) * 128],
                                      pt[:, cols],
                                      start=first, stop=is_last,
                                      waits=[tok_e, tok_v] +
                                            ([bank_free[av_bank]] if first else []),
                                      tick=True)
                        phat_free[pi] = tok_av

                    from collections import deque
                    pend = deque()
                    for kvi in range(nkv):
                        pend.append(stage_a(kvi))
                        if len(pend) > 2:
                            stage_b(pend.popleft())
                    while pend:
                        stage_b(pend.popleft())
                    return (h, av_bank, zb, P.now(TE))

                def head_epi(hs):
                    h, av_bank, zb, tok_avdone = hs
                    tok_ln = P.do(SE, lambda e, zb=zb: e.activation(
                        out=z_row[:], in_=psh[zb][0:1, :], func=AF.Ln),
                        waits=[tok_avdone, z16_free[0]], tick=True)
                    bank_free[zb] = tok_ln
                    tok_zr = P.do(SE, lambda e: e.activation(
                        out=z16[:], in_=z_row[:], func=AF.Exp, scale=-1.0),
                        waits=[tok_ln], tick=True)
                    t_bz = t_mm(psh[7][:, :], ones_r[:], z16[:],
                                start=True, stop=True, waits=[tok_zr, bank_free[7]],
                                tick=True)
                    z16_free[0] = t_bz
                    rz = rzstage[h % 2]
                    tok_rz = P.do(SE, lambda e, rz=rz: e.activation(
                        out=rz, in_=psh[7][:, :], func=AF.Copy), waits=[t_bz], tick=True)
                    bank_free[7] = tok_rz
                    tok_ao = P.do(VE, lambda e, av_bank=av_bank, h=h, rz=rz: e.tensor_mul(
                        attnout[:, h * 512:(h + 1) * 512], psh[av_bank][:, :], rz),
                        waits=[tok_rz, tok_avdone], tick=True)
                    bank_free[av_bank] = tok_ao
                    att_toks.append(tok_ao)

                pend_h = None
                for h in range(NH):
                    hs = head_main(h)
                    if pend_h is not None:
                        head_epi(pend_h)
                    pend_h = hs
                head_epi(pend_h)

                if _DBG == "attnout":
                    dbg_dump(attnout, nslab=4)
                    return
                if is_b:
                    for s in range(NSLAB):
                        b = s % 2
                        tok_g = s_ld2.dma(gstage[b], gate_scr[s, :, qb, :],
                                          waits=[gate_ld_free[b]] + gate_toks)
                        tok_sg = P.do(SE, lambda e, b=b: e.activation(
                            out=gstage[b], in_=gstage[b], func=AF.Sigmoid),
                            waits=[tok_g, att_toks[s] if s < len(att_toks) else None],
                            tick=True)
                        tok_mulg = P.do(VE, lambda e, s=s, b=b: e.tensor_mul(
                            attnout[:, s * 512:(s + 1) * 512],
                            attnout[:, s * 512:(s + 1) * 512], gstage[b]),
                            waits=[tok_sg] + att_toks, tick=True)
                        gate_ld_free[b] = tok_mulg

                ao_ready = P.now(VE)
                for kd in range(NSLAB):
                    j = kd % 2
                    last_mm = None
                    for s in range(NSLAB):
                        last_mm = t_mm(psh[j][:, :],
                                       wo_sb[:, s * 2048 + kd * 128: s * 2048 + (kd + 1) * 128],
                                       attnout[:, s * 512:(s + 1) * 512],
                                       start=(s == 0), stop=(s == NSLAB - 1),
                                       waits=[tok_wo, ao_ready] +
                                             ([bank_free[j]] if s == 0 else []),
                                       tick=(s == NSLAB - 1))
                    ot = ostage[j]
                    tok_oe = P.do(SE, lambda e, j=j, ot=ot: e.activation(
                        out=ot, in_=psh[j][:, :], func=AF.Copy),
                        waits=[last_mm, ostage_free[j]], tick=True)
                    bank_free[j] = tok_oe
                    extra_acc = getattr(P, "first_acc_extra", None)
                    P.first_acc_extra = None
                    ostage_free[j] = s_acc.dma(
                        R_dm[kd * 128:(kd + 1) * 128, qb * 512:(qb + 1) * 512],
                        ot, waits=[tok_oe, extra_acc], accum_op=ALU.add)
            P.barrier()

        def emit_ffn(src_dm, extra_wait=()):
            tok_xn, xn_tb = emit_ln(src_dm, extra_wait=extra_wait)
            wgu_free = [None, None, None, None]
            wgu_u8_free = [tok_xn, tok_xn, tok_xn, tok_xn]
            h_free = [None, None, None]
            hb_pre = []
            nt = 0

            def emit_wgu(it):
                bg, bu = (it % 2) * 2, (it % 2) * 2 + 1
                tg8 = s_ld.dma(wgu_u8[bg][:, 0:2048], gw_blk[it].rearrange("p k i -> p (k i)"),
                               waits=[wgu_u8_free[bg]])
                tu8 = s_ld2.dma(wgu_u8[bu][:, 0:2048], uw_blk[it].rearrange("p k i -> p (k i)"),
                                waits=[wgu_u8_free[bu]])
                tok_wg = P.do(VE, lambda e, bg=bg: e.tensor_scalar_add(
                    wgu[bg], wgu_u8[bg][:, 0:2048], -128.0),
                    waits=[tg8, wgu_free[bg]], tick=True)
                wgu_u8_free[bg] = tok_wg
                tok_wu = P.do(VE, lambda e, bu=bu: e.tensor_scalar_add(
                    wgu[bu], wgu_u8[bu][:, 0:2048], -128.0),
                    waits=[tu8, wgu_free[bu]], tick=True)
                wgu_u8_free[bu] = tok_wu
                return tok_wg, tok_wu

            pend_gu = emit_wgu(0)
            for it in range(64):
                bg, bu = (it % 2) * 2, (it % 2) * 2 + 1
                tok_wg, tok_wu = pend_gu
                if it + 1 < 64:
                    pend_gu = emit_wgu(it + 1)
                for tb in range(NTB):
                    jg = tb % 2
                    ju = 2 + tb % 2
                    mg = mu = None
                    for k in range(NSLAB):
                        rhs = xn[:, k * 2048 + tb * 512: k * 2048 + (tb + 1) * 512]
                        mg = t_mm(psh[jg][:, :], wgu[bg][:, k * 128:(k + 1) * 128], rhs,
                                  start=(k == 0), stop=(k == NSLAB - 1),
                                  waits=[tok_wg, xn_tb[tb]] + ([bank_free[jg]] if k == 0 else []),
                                  tick=(k == NSLAB - 1))
                    for k in range(NSLAB):
                        rhs = xn[:, k * 2048 + tb * 512: k * 2048 + (tb + 1) * 512]
                        mu = t_mm(psh[ju][:, :], wgu[bu][:, k * 128:(k + 1) * 128], rhs,
                                  start=(k == 0), stop=(k == NSLAB - 1),
                                  waits=[tok_wu, xn_tb[tb]] + ([bank_free[ju]] if k == 0 else []),
                                  tick=(k == NSLAB - 1))
                    if tb == NTB - 1:
                        wgu_free[bg] = mg
                        wgu_free[bu] = mu
                    fs = fstage[tb % 2]
                    tok_sg = P.do(SE, lambda e, jg=jg, fs=fs, it=it: e.activation(
                        out=fs, in_=psh[jg][:, :], func=AF.Silu,
                        scale=gsc_sb[:, it:it + 1]),
                        waits=[mg, t_consts], tick=True)
                    bank_free[jg] = tok_sg
                    ht = hstage[nt % 3]
                    tok_h = P.do(VE, lambda e, ju=ju, fs=fs, ht=ht, it=it:
                                 e.scalar_tensor_tensor(ht, psh[ju][:, :],
                                                        usc_sb[:, it:it + 1], fs,
                                                        ALU.mult, ALU.mult),
                                 waits=[tok_sg, mu, h_free[nt % 3]], tick=True)
                    bank_free[ju] = tok_h
                    h_free[nt % 3] = s_st.dma(h_scr[it, :, tb, :], ht, waits=[tok_h])
                    nt += 1
                if it % 16 == 15:
                    q4 = it // 16
                    hb_pre.append(s_ld2.dma(
                        h_sb[:, q4 * 8192:(q4 + 1) * 8192].rearrange("p (it t) -> p it t", it=16),
                        h_scr[q4 * 16:(q4 + 1) * 16, :, 0, :].rearrange("it p t -> p it t"),
                        waits=[s_st.wait_all()]))
            P.barrier()
            if _DBG == "ffng":
                pass
            fstage_free = [None, None]
            wd_free = [None, None]
            wd_u8_free = [None, None]
            hbufs = [h_sb, h_sb2]
            hb_tok = {0: hb_pre}
            h_guard = [None, None]

            def emit_wd(g):
                kd = g % 16
                b = g % 2
                t8 = s_ld.dma(wd_u8[b][:, 0:8192],
                              dw_blk[kd].rearrange("p it d -> p (it d)"),
                              waits=[wd_u8_free[b]])
                tok_wd = P.do(VE, lambda e, b=b: e.tensor_scalar_add(
                    wd_buf[b], wd_u8[b][:, 0:8192], -128.0),
                    waits=[t8, wd_free[b]], tick=True)
                wd_u8_free[b] = tok_wd
                return tok_wd

            pend_wd = emit_wd(0)
            for g in range(4 * 16):
                tb, kd = divmod(g, 16)
                if kd == 0:
                    if tb + 1 < NTB:
                        nb = hbufs[(tb + 1) % 2]
                        toks = []
                        for q4 in range(4):
                            toks.append(s_ld2.dma(
                                nb[:, q4 * 8192:(q4 + 1) * 8192].rearrange("p (it t) -> p it t", it=16),
                                h_scr[q4 * 16:(q4 + 1) * 16, :, tb + 1, :].rearrange("it p t -> p it t"),
                                waits=[h_guard[(tb + 1) % 2]]))
                        hb_tok[tb + 1] = toks
                    hb_toks = hb_tok[tb]
                    hbuf = hbufs[tb % 2]
                    if _DBG == "ffnh" and tb == 0:
                        dbg_dump(h_sb)
                        return
                b = g % 2
                j = g % 2
                tok_wd = pend_wd
                if g + 1 < 64:
                    pend_wd = emit_wd(g + 1)
                last_mm = None
                for it in range(64):
                    last_mm = t_mm(psh[j][:, :],
                                   wd_buf[b][:, it * 128:(it + 1) * 128],
                                   hbuf[:, it * 512:(it + 1) * 512],
                                   start=(it == 0), stop=(it == 63),
                                   waits=[tok_wd, hb_toks[it // 16]] +
                                         ([bank_free[j]] if it == 0 else []),
                                   tick=(it == 63))
                wd_free[b] = last_mm
                ot = ostage[j]
                tok_oe = P.do(SE, lambda e, j=j, ot=ot, kd=kd: e.activation(
                    out=ot, in_=psh[j][:, :], func=AF.Copy,
                    scale=dsc_sb[:, kd:kd + 1]),
                    waits=[last_mm, fstage_free[j]], tick=True)
                bank_free[j] = tok_oe
                fstage_free[j] = s_acc.dma(
                    R_dm[kd * 128:(kd + 1) * 128, tb * 512:(tb + 1) * 512],
                    ot, waits=[tok_oe], accum_op=ALU.add)
                if kd == 15:
                    h_guard[tb % 2] = last_mm
            P.barrier()

        P.first_acc_extra = t_Rinit
        stage_list = _STAGE_LIST if _STAGE_LIST is not None else list(range(1, _STAGES + 1))
        for si, s in enumerate(stage_list):
            src0 = x_dm if si == 0 else R_dm
            ew = [] if si == 0 else [s_acc.wait_all()]
            if s == 1:
                emit_attn("a", src0, extra_wait=ew)
            elif s == 2:
                emit_ffn(src0, extra_wait=ew)
            elif s == 3:
                emit_attn("b", src0, extra_wait=ew)
            elif s == 4:
                emit_ffn(src0, extra_wait=ew)
            if _DBG is not None:
                break

        P.barrier()
        for e in ENGS:
            P.do(e, lambda ee: None, waits=[s_acc.wait_all(), s_st.wait_all(),
                                            s_cast.wait_all()])

        block = es.enter_context(nc.Block())
        P.replay(block)
    return nc


def kernel(**inputs):
    g = host_prep(inputs)
    x = np.asarray(inputs["L0_attn_a_x"], dtype=np.float32)
    nc = build_program()
    in_maps = []
    for c in range(8):
        m = dict(g)
        m["x_dm"] = np.ascontiguousarray(x[c].T)
        in_maps.append(m)
    res = run_bass_kernel_spmd(nc, in_maps, core_ids=list(range(8)))
    out = np.stack([np.asarray(r["out_dm"]).T for r in res.results], axis=0)
    return out


# version 11
# speedup vs baseline: 1.3341x; 1.0268x over previous
import numpy as np
import concourse.bass as bass
import concourse.mybir as mybir
from concourse.bass_utils import run_bass_kernel_spmd
from contextlib import ExitStack

F32 = mybir.dt.float32
BF16 = mybir.dt.bfloat16
I32 = mybir.dt.int32
U8 = mybir.dt.uint8
AF = mybir.ActivationFunctionType
ALU = mybir.AluOpType
AX = mybir.AxisListType

D = 2048
T = 2048
NH = 16
NKV = 4
HD = 128
NSLAB = 16
NTB = 4
SCALE = float(1.0 / np.sqrt(128.0))
WIN = 512
FRONT = 256
NEGINF = -1.0e9

_STAGES = 4
_STAGE_LIST = None
_DBG = None



def _deint(n_heads):
    p = []
    for h in range(n_heads):
        base = h * HD
        p.extend(base + 2 * f for f in range(64))
        p.extend(base + 2 * f + 1 for f in range(64))
    return np.asarray(p, dtype=np.int64)


def _rope_tables(theta):
    inv_freq = 1.0 / theta ** (np.arange(0, HD, 2)[: HD // 2].astype(np.float32) / HD)
    t = np.arange(T, dtype=np.float32)
    ang = np.outer(t, inv_freq)
    cos = np.cos(ang).T.astype(np.float32)
    sin = np.sin(ang).T.astype(np.float32)
    C2 = np.concatenate([cos, cos], axis=0)
    S2 = np.concatenate([-sin, sin], axis=0)
    return C2, S2


def host_prep(inp):
    g = {}
    pq = _deint(NH)
    pk = _deint(NKV)
    d1 = _deint(1)
    g["wq_a"] = np.ascontiguousarray(inp["L0_attn_a_wq"][:, pq])
    g["wk_a"] = np.ascontiguousarray(inp["L0_attn_a_wk"][:, pk])
    g["wv_a"] = np.ascontiguousarray(inp["L0_attn_a_wv"])
    g["wo_a"] = np.ascontiguousarray(inp["L0_attn_a_wo"])
    wq = np.asarray(inp["L0_attn_b_w_qkv"]).T
    g["wqkv_b"] = np.ascontiguousarray(np.concatenate(
        [wq[:, 0:2048][:, pq], wq[:, 2048:4096],
         wq[:, 4096:4608][:, pk], wq[:, 4608:5120]], axis=1))
    g["wo_b"] = np.ascontiguousarray(np.asarray(inp["L0_attn_b_w_o"]).T)
    g["qnw"] = np.ascontiguousarray(np.asarray(inp["L0_attn_b_q_norm_w"])[d1].reshape(128, 1))
    g["qnb"] = np.ascontiguousarray(np.asarray(inp["L0_attn_b_q_norm_b"])[d1].reshape(128, 1))
    g["knw"] = np.ascontiguousarray(np.asarray(inp["L0_attn_b_k_norm_w"])[d1].reshape(128, 1))
    g["knb"] = np.ascontiguousarray(np.asarray(inp["L0_attn_b_k_norm_b"])[d1].reshape(128, 1))
    def _u8(a):
        a = np.ascontiguousarray(np.asarray(a, dtype=np.int32))
        return a.view(np.uint8).reshape(a.shape + (4,))[..., 0]
    wt = _u8(inp["L0_ffn_gate_w"]).T.reshape(16, 128, 64, 128)
    g["gw_blk"] = np.ascontiguousarray(wt.transpose(2, 1, 0, 3))
    wt = _u8(inp["L0_ffn_up_w"]).T.reshape(16, 128, 64, 128)
    g["uw_blk"] = np.ascontiguousarray(wt.transpose(2, 1, 0, 3))
    wt = _u8(inp["L0_ffn_down_w"]).T.reshape(64, 128, 16, 128)
    g["dw_blk"] = np.ascontiguousarray(wt.transpose(2, 1, 0, 3))
    g["gsc"] = np.ascontiguousarray(np.asarray(inp["L0_ffn_gate_scale"]).reshape(64, 128).T)
    g["usc"] = np.ascontiguousarray(np.asarray(inp["L0_ffn_up_scale"]).reshape(64, 128).T)
    g["dsc"] = np.ascontiguousarray(np.asarray(inp["L0_ffn_down_scale"]).reshape(16, 128).T)
    C2a, S2a = _rope_tables(10000.0)
    C2b, S2b = _rope_tables(1000000.0)
    g["tab_a"] = np.ascontiguousarray(np.concatenate([C2a, S2a], axis=1))
    g["tab_b"] = np.ascontiguousarray(np.concatenate([C2b, S2b], axis=1))
    r = np.arange(128)
    P1 = np.where(r[None, :] >= r[:, None], 0.0, NEGINF).astype(np.float32)
    P2 = np.where(r[None, :] < r[:, None], 0.0, NEGINF).astype(np.float32)
    g["masks"] = np.ascontiguousarray(np.concatenate([P1, P2], axis=1))
    return g


def kv_plan(qb, windowed):
    out = []
    for jt in range(T // 128):
        subs = []
        for a in range(4):
            dd = qb * 512 + a * 128 - jt * 128
            if dd < 0:
                st = None
            elif not windowed or jt * 128 < FRONT:
                st = 1 if dd == 0 else 0
            else:
                if dd == 0:
                    st = 1
                elif dd <= WIN - 128:
                    st = 0
                elif dd == WIN:
                    st = 2
                else:
                    st = None
            subs.append(st)
        alive = [a for a in range(4) if subs[a] is not None]
        if not alive:
            continue
        a_lo, a_hi = min(alive), max(alive)
        assert alive == list(range(a_lo, a_hi + 1))
        out.append((jt, a_lo, a_hi, [(a, subs[a]) for a in alive if subs[a]]))
    return out



ENGS = ("tensor", "vector", "scalar", "sync", "gpsimd")


class Prog:
    def __init__(self, nc, es):
        self.nc = nc
        self.es = es
        self.ops = {e: [] for e in ENGS}
        self.engsem = {}
        self.cnt = {}
        self.streams = []
        for e in ("tensor", "vector", "scalar"):
            self.engsem[e] = self.new_sem("pg_" + e)
            self.cnt[e] = 0
        self.lastwait = {e: {} for e in ENGS}

    def new_sem(self, name):
        return self.es.enter_context(self.nc.semaphore(name))

    def _expand(self, waits):
        out = []
        for w in waits:
            if w is None:
                continue
            if isinstance(w, tuple) and len(w) == 2 and w[0] == "MULTI":
                out.extend(x for x in w[1] if x is not None)
            else:
                out.append(w)
        return out

    def do(self, eng, fn, waits=(), tick=False):
        ws = []
        lw = self.lastwait[eng]
        for (sem, val, producer) in self._expand(waits):
            if producer == eng:
                continue
            k = id(sem)
            if lw.get(k, -1) >= val:
                continue
            lw[k] = val
            ws.append((sem, val))
        tok = None
        esem = self.engsem.get(eng)
        if tick:
            self.cnt[eng] += 1
            tok = (esem, self.cnt[eng], eng)

        def f(e, ws=ws, fn=fn, tick=tick, esem=esem):
            for (s, v) in ws:
                e.wait_ge(s, v)
            ins = fn(e)
            if tick:
                ins.then_inc(esem, 1)
        self.ops[eng].append(f)
        return tok

    def now(self, eng):
        return (self.engsem[eng], self.cnt[eng], eng)

    def barrier(self):
        toks = [self.now(e) for e in ("tensor", "vector", "scalar")]
        for st in self.streams:
            toks.append(st.wait_all())
        for e in ENGS:
            self.do(e, lambda ee: None, waits=toks)

    def replay(self, block):
        def mk(eng):
            def body(e):
                for f in self.ops[eng]:
                    f(e)
            return body
        block.tensor(mk("tensor"))
        block.vector(mk("vector"))
        block.scalar(mk("scalar"))
        block.sync(mk("sync"))
        block.gpsimd(mk("gpsimd"))


class Stream:

    def __init__(self, P, queue, name, R=2):
        self.P = P
        self.queue = queue
        self.R = R
        self.sems = [P.new_sem(f"{name}{i}") for i in range(R)]
        self.counts = [0] * R
        self.n = 0
        P.streams.append(self)

    def dma(self, out, in_, waits=(), **kw):
        i = self.n % self.R
        self.n += 1
        self.counts[i] += 16
        sem = self.sems[i]
        val = self.counts[i]

        def fn(e, out=out, in_=in_, kw=kw, sem=sem):
            return e.dma_start(out=out, in_=in_, **kw).then_inc(sem, 16)
        self.P.do(self.queue, fn, waits=waits)
        return (sem, val, "dma")

    def wait_all(self):
        return ("MULTI", [(self.sems[i], self.counts[i], "dma")
                          for i in range(self.R) if self.counts[i] > 0])



def build_program():
    nc = bass.Bass("TRN2", target_bir_lowering=False)

    def din(name, shape, dt):
        return nc.dram_tensor(name, shape, dt, kind="ExternalInput").ap()

    x_dm = din("x_dm", [D, T], F32)
    wq_a = din("wq_a", [D, 2048], F32)
    wk_a = din("wk_a", [D, 512], F32)
    wv_a = din("wv_a", [D, 512], F32)
    wo_a = din("wo_a", [2048, D], F32)
    wqkv_b = din("wqkv_b", [D, 5120], F32)
    wo_b = din("wo_b", [2048, D], F32)
    qnw_d = din("qnw", [128, 1], F32)
    qnb_d = din("qnb", [128, 1], F32)
    knw_d = din("knw", [128, 1], F32)
    knb_d = din("knb", [128, 1], F32)
    gw_blk = din("gw_blk", [64, 128, 16, 128], U8)
    uw_blk = din("uw_blk", [64, 128, 16, 128], U8)
    dw_blk = din("dw_blk", [16, 128, 64, 128], U8)
    gsc_d = din("gsc", [128, 64], F32)
    usc_d = din("usc", [128, 64], F32)
    dsc_d = din("dsc", [128, 16], F32)
    tab_a = din("tab_a", [128, 2 * T], F32)
    tab_b = din("tab_b", [128, 2 * T], F32)
    masks_d = din("masks", [128, 256], F32)

    R_dm = nc.dram_tensor("out_dm", [D, T], F32, kind="ExternalOutput").ap()

    h_scr = nc.dram_tensor("h_scr", [64, 128, 4, 512], BF16, kind="Internal").ap()
    gate_scr = nc.dram_tensor("gate_scr", [16, 128, 4, 512], BF16, kind="Internal").ap()

    es = ExitStack()
    with es:
        A1 = es.enter_context(nc.sbuf_tensor("A1", [128, 32768], BF16))
        A2 = es.enter_context(nc.sbuf_tensor("A2", [128, 32768], BF16))
        A3 = es.enter_context(nc.sbuf_tensor("A3", [128, 16384], BF16))
        A4 = es.enter_context(nc.sbuf_tensor("A4", [128, 16384], BF16))
        m_row = es.enter_context(nc.sbuf_tensor("m_row", [1, T], BF16))
        r_row = es.enter_context(nc.sbuf_tensor("r_row", [1, T], F32))

        tiny = es.enter_context(nc.sbuf_tensor("tiny", [1, 64], F32))
        negB = es.enter_context(nc.sbuf_tensor("negB", [128, 16], F32))
        masks_sb = es.enter_context(nc.sbuf_tensor("masks_sb", [128, 256], BF16))
        gsc_sb = es.enter_context(nc.sbuf_tensor("gsc_sb", [128, 64], F32))
        usc_sb = es.enter_context(nc.sbuf_tensor("usc_sb", [128, 64], F32))
        dsc_sb = es.enter_context(nc.sbuf_tensor("dsc_sb", [128, 16], F32))
        qnw_sb = es.enter_context(nc.sbuf_tensor("qnw_sb", [128, 1], F32))
        qnb_sb = es.enter_context(nc.sbuf_tensor("qnb_sb", [128, 1], F32))
        knw_sb = es.enter_context(nc.sbuf_tensor("knw_sb", [128, 1], F32))
        knb_sb = es.enter_context(nc.sbuf_tensor("knb_sb", [128, 1], F32))
        epsc = es.enter_context(nc.sbuf_tensor("epsc", [128, 2], F32))
        ones_c = es.enter_context(nc.sbuf_tensor("ones_c", [128, 1], BF16))
        ones_r = es.enter_context(nc.sbuf_tensor("ones_r", [1, 128], BF16))
        onesf_r = es.enter_context(nc.sbuf_tensor("onesf_r", [1, 128], F32))
        oneshk_r = es.enter_context(nc.sbuf_tensor("oneshk_r", [1, 128], F32))
        oneshk_c = es.enter_context(nc.sbuf_tensor("oneshk_c", [128, 1], F32))

        psh = [es.enter_context(nc.psum_tensor(f"ps{i}", [128, 512], F32)) for i in range(8)]
        bank_free = [None] * 8

        P = Prog(nc, es)
        bcast_free = [None, None, None, None]

        s_ld = Stream(P, "sync", "ld", R=2)
        s_ld2 = Stream(P, "scalar", "ld2", R=2)
        s_cast = Stream(P, "gpsimd", "cast", R=4)
        s_acc = Stream(P, "gpsimd", "acc", R=2)
        s_st = Stream(P, "sync", "st", R=2)
        s_swp = Stream(P, "gpsimd", "swp", R=2)

        scrA2 = [A4[:, 0:2048], A4[:, 2048:4096]]
        scrB2 = [A4[:, 4096:6144], A4[:, 6144:8192]]
        xsq0 = A4[:, 0:2048]
        xsq1 = A4[:, 2048:4096]
        tabs = A4[:, 8192:12288]
        attnout = A4[:, 0:8192]
        phat = [A4[:, 8192 + i * 512: 8704 + i * 512] for i in range(4)]
        gstage = [A4[:, 10240 + i * 512: 10752 + i * 512] for i in range(2)]
        ostage = [A4[:, 11264 + i * 1024: 12288 + i * 1024].bitcast(F32)
                  for i in range(2)]
        rzstage = [A4[:, 13312:14336].bitcast(F32),
                   A4[:, 14336:15360].bitcast(F32)]
        z_row = A4[0:1, 15360:16384].bitcast(F32)
        z16 = A4[0:1, 9728:10240]

        wgu = [A4[:, 8192 + i * 2048: 10240 + i * 2048] for i in range(4)]
        wgu_u8 = [A4[:, 3584 + i * 1024: 4608 + i * 1024].bitcast(U8)
                  for i in range(4)]
        wd_u8 = [A4[:, 0 + i * 4096: 4096 + i * 4096].bitcast(U8)
                 for i in range(2)]
        hstage = [A4[:, 2048 + i * 512: 2560 + i * 512] for i in range(3)]
        fstage = [A4[:, 0 + i * 1024: 1024 + i * 1024].bitcast(F32) for i in range(2)]
        wq_stage2 = [A4[:, 12288:14336], A4[:, 14336:16384]]
        bcast_sb = [A4[:, 12288 + i * 512: 12800 + i * 512] for i in range(4)]
        wv_stage = [A4[:, 12288 + i * 512: 12800 + i * 512] for i in range(2)]

        xn = A1
        wo_sb = A1
        qhat = A2
        h_sb = A2
        h_sb2 = A1

        kT = A3[:, 0:8192]
        vT = A3[:, 8192:16384]
        wd_buf = [A3[:, i * 8192:(i + 1) * 8192] for i in range(2)]

        TE, VE, SE = "tensor", "vector", "scalar"

        def t_mm(out, lhsT, rhs, start, stop, waits=(), tick=False):
            return P.do(TE, lambda e: e.matmul(out, lhsT, rhs, start=start, stop=stop),
                        waits=waits, tick=tick)

        s_cast.dma(masks_sb[:], masks_d)
        for dst, src in ((gsc_sb, gsc_d), (usc_sb, usc_d), (dsc_sb, dsc_d),
                         (qnw_sb, qnw_d), (qnb_sb, qnb_d), (knw_sb, knw_d),
                         (knb_sb, knb_d)):
            s_ld.dma(dst[:], src)
        t_consts = s_ld.wait_all()
        P.do(VE, lambda e: e.memset(epsc[:, 0:1], 1e-5), tick=True)
        P.do(VE, lambda e: e.memset(epsc[:, 1:2], 1e-6), tick=True)
        P.do(VE, lambda e: e.memset(ones_c[:], 1.0), tick=True)
        P.do(VE, lambda e: e.memset(ones_r[:], 1.0), tick=True)
        P.do(VE, lambda e: e.memset(onesf_r[:], 1.0), tick=True)
        P.do(VE, lambda e: e.memset(oneshk_r[:], 1.0 / 128.0), tick=True)
        tok_ones = P.do(VE, lambda e: e.memset(oneshk_c[:], 1.0 / 128.0), tick=True)

        for c in range(4):
            s_st.dma(R_dm[c * 512:(c + 1) * 512, :], x_dm[c * 512:(c + 1) * 512, :])

        t_Rinit = s_st.wait_all()

        def dbg_dump(region, nslab=16):
            P.barrier()
            for k in range(nslab):
                s_cast.dma(R_dm[k * 128:(k + 1) * 128, :],
                           region[:, k * 2048:(k + 1) * 2048])
            P.barrier()

        def emit_ln(src_dm, extra_wait=()):
            ld_toks = []
            for k in range(NSLAB):
                ld_toks.append(s_cast.dma(xn[:, k * 2048:(k + 1) * 2048],
                               src_dm[k * 128:(k + 1) * 128, :],
                               waits=list(extra_wait) if k == 0 else []))
            mm_tok = None
            sq_prev = [None, None]
            for k in range(NSLAB):
                xk = xn[:, k * 2048:(k + 1) * 2048]
                xq = xsq0 if k % 2 == 0 else xsq1
                tok_s = P.do(SE, lambda e, xq=xq, xk=xk: e.activation(
                    out=xq, in_=xk, func=AF.Square),
                    waits=[ld_toks[k], sq_prev[k % 2]], tick=True)
                for tb in range(NTB):
                    t_mm(psh[tb][0:1, :], ones_c[:], xk[:, tb * 512:(tb + 1) * 512],
                         start=(k == 0), stop=(k == NSLAB - 1),
                         waits=[ld_toks[k], tok_ones] +
                               ([bank_free[tb]] if k == 0 else []))
                for tb in range(NTB):
                    mm_tok = t_mm(psh[4 + tb][0:1, :], ones_c[:],
                                  xq[:, tb * 512:(tb + 1) * 512],
                                  start=(k == 0), stop=(k == NSLAB - 1),
                                  waits=[tok_s] + ([bank_free[4 + tb]] if k == 0 else []),
                                  tick=(tb == NTB - 1))
                sq_prev[k % 2] = mm_tok
            row_toks = []
            for tb in range(NTB):
                row_toks.append(P.do(SE, lambda e, tb=tb: e.activation(
                    out=m_row[:, tb * 512:(tb + 1) * 512], in_=psh[tb][0:1, :],
                    func=AF.Copy, scale=1.0 / 2048.0), waits=[mm_tok], tick=True))
                row_toks.append(P.do(SE, lambda e, tb=tb: e.activation(
                    out=r_row[:, tb * 512:(tb + 1) * 512], in_=psh[4 + tb][0:1, :],
                    func=AF.Copy, scale=1.0 / 2048.0), waits=[mm_tok], tick=True))
            var_toks = []
            for tb in range(NTB):
                ta = P.do(VE, lambda e, tb=tb: e.tensor_mul(
                    psh[tb][0:1, :], m_row[:, tb * 512:(tb + 1) * 512],
                    m_row[:, tb * 512:(tb + 1) * 512]),
                    waits=row_toks, tick=True)
                tv = P.do(VE, lambda e, tb=tb: e.tensor_sub(
                    r_row[:, tb * 512:(tb + 1) * 512],
                    r_row[:, tb * 512:(tb + 1) * 512], psh[tb][0:1, :]),
                    waits=[ta], tick=True)
                var_toks.append(tv)
                bank_free[tb] = tv
            tok_sd = P.do(SE, lambda e: e.activation(out=r_row[:], in_=r_row[:],
                                                     func=AF.Ln, bias=epsc[0:1, 0:1]),
                          waits=var_toks, tick=True)
            tok_r = P.do(SE, lambda e: e.activation(out=r_row[:], in_=r_row[:],
                                                    func=AF.Exp, scale=-0.5),
                         waits=[tok_sd], tick=True)
            tok_mr = P.do(VE, lambda e: e.tensor_mul(m_row[:], m_row[:], r_row[:]),
                          waits=[tok_r], tick=True)
            bank_free[4] = tok_mr
            last = None
            xn_tb = []
            for tb in range(NTB):
                br, bm = (6, 7) if tb % 2 == 0 else (2, 3)
                t_r = t_mm(psh[br][:, :], onesf_r[:], r_row[:, tb * 512:(tb + 1) * 512],
                           start=True, stop=True, waits=[tok_r, bank_free[br]], tick=True)
                t_m = t_mm(psh[bm][:, :], ones_r[:], m_row[:, tb * 512:(tb + 1) * 512],
                           start=True, stop=True, waits=[tok_mr, bank_free[bm]], tick=True)
                rb = bcast_sb[tb % 2]
                mb = bcast_sb[2 + tb % 2]
                t_re = P.do(SE, lambda e, br=br, rb=rb: e.activation(
                    out=rb, in_=psh[br][:, :], func=AF.Copy),
                    waits=[t_r, bcast_free[tb % 2]], tick=True)
                bank_free[br] = t_re
                t_me = P.do(SE, lambda e, bm=bm, mb=mb: e.activation(
                    out=mb, in_=psh[bm][:, :], func=AF.Copy),
                    waits=[t_m, bcast_free[2 + tb % 2]], tick=True)
                bank_free[bm] = t_me
                for k in range(NSLAB):
                    xkb = xn[:, k * 2048 + tb * 512: k * 2048 + (tb + 1) * 512]
                    P.do(VE, lambda e, xkb=xkb, rb=rb: e.tensor_mul(xkb, xkb, rb),
                         waits=[t_re], tick=False)
                    last = P.do(VE, lambda e, xkb=xkb, mb=mb: e.tensor_sub(xkb, xkb, mb),
                                waits=[t_me], tick=(k == NSLAB - 1))
                bcast_free[tb % 2] = last
                bcast_free[2 + tb % 2] = last
                xn_tb.append(last)
            return last, xn_tb


        def emit_attn(which, src_dm, extra_wait=()):
            is_b = which == "b"
            tab_src = tab_b if is_b else tab_a
            wq_src = (wqkv_b[:, 0:2048] if is_b else wq_a)
            wk_src = (wqkv_b[:, 4096:4608] if is_b else wk_a)
            wv_src = (wqkv_b[:, 4608:5120] if is_b else wv_a)
            wo_src = wo_b if is_b else wo_a

            t1 = s_cast.dma(tabs[:, 0:2048], tab_src[:, 0:T])
            t2 = s_cast.dma(tabs[:, 2048:4096], tab_src[:, T:2 * T])
            tok_tabs = ("MULTI", [t1, t2])

            tok_xn, xn_tb = emit_ln(src_dm, extra_wait=extra_wait)
            if _DBG == "xn":
                dbg_dump(xn)
                return

            ev_tok = None
            wv_free = [tok_xn, tok_xn]
            for half in range(2):
                for k in range(NSLAB):
                    b = k % 2
                    tok_w = s_cast.dma(wv_stage[b][:, 0:512],
                                       wv_src[k * 128:(k + 1) * 128, :],
                                       waits=[wv_free[b]])
                    for j, tt in enumerate(range(half * 8, half * 8 + 8)):
                        last_mm = t_mm(
                            psh[j][:, :],
                            xn[:, k * 2048 + tt * 128: k * 2048 + tt * 128 + 128],
                            wv_stage[b][:, 0:512],
                            start=(k == 0), stop=(k == NSLAB - 1),
                            waits=[tok_w, xn_tb[tt // 4]] +
                                  ([bank_free[j]] if k == 0 else []),
                            tick=(j == 7))
                    wv_free[b] = last_mm
                    if k == NSLAB - 1:
                        for j, tt in enumerate(range(half * 8, half * 8 + 8)):
                            ev_tok = P.do(SE, lambda e, j=j, tt=tt: e.activation(
                                out=vT[:, tt * 512:(tt + 1) * 512], in_=psh[j][:, :],
                                func=AF.Copy), waits=[P.now(TE)], tick=True)
                            bank_free[j] = ev_tok
            tok_v = ev_tok
            wv_guard = ev_tok
            if _DBG == "v":
                dbg_dump(vT, nslab=4)
                return

            wq_free = [wv_guard, None]
            wq_i = [0]
            sfreeA = [[None] * NTB, [None] * NTB]
            sfreeB = [[None] * NTB, [None] * NTB]
            rows_free_m = [None] * NTB
            rows_free_r = [None] * NTB

            rope_par = [0]
            gstage_free = [None, None]

            def emit_proj(st):
                wb = wq_i[0] % 2
                wq_i[0] += 1
                is_gate = st.get("gate") is not None
                if not is_gate:
                    par = rope_par[0] % 2
                    rope_par[0] += 1
                    st["par"] = par
                    st["sA"] = scrA2[par]
                    st["sB"] = scrB2[par]
                wq_stage = wq_stage2[wb]
                st["wb"] = wb
                tok_w = s_cast.dma(wq_stage[:].rearrange("p (k c) -> p k c", k=16),
                                   st["w_cols"].rearrange("(k p) c -> p k c", p=128),
                                   waits=[wq_free[wb]])
                st["evac"] = []
                for tb in range(NTB):
                    sec = slice(tb * 512, (tb + 1) * 512)
                    j = tb % 2
                    last_mm = None
                    for k in range(NSLAB):
                        last_mm = t_mm(psh[j][:, :], wq_stage[:, k * 128:(k + 1) * 128],
                                       xn[:, k * 2048 + tb * 512: k * 2048 + (tb + 1) * 512],
                                       start=(k == 0), stop=(k == NSLAB - 1),
                                       waits=[tok_w, xn_tb[tb]] +
                                             ([bank_free[j]] if k == 0 else []),
                                       tick=(k == NSLAB - 1))
                    if is_gate:
                        gs = st["gate"]
                        gb = tb % 2
                        tok_e = P.do(SE, lambda e, j=j, gb=gb: e.activation(
                            out=gstage[gb], in_=psh[j][:, :], func=AF.Copy),
                            waits=[last_mm, gstage_free[gb]], tick=True)
                        gstage_free[gb] = s_st.dma(gate_scr[gs, :, tb, :], gstage[gb],
                                                   waits=[tok_e])
                    else:
                        tok_e = P.do(SE, lambda e, j=j, sA=st["sA"], sec=sec: e.activation(
                            out=sA[:, sec], in_=psh[j][:, :], func=AF.Copy),
                            waits=[last_mm, sfreeA[st["par"]][tb]], tick=True)
                    bank_free[j] = tok_e
                    st["evac"].append(tok_e)
                wq_free[wb] = st["evac"][-1]

            def post_A(st):
                par, sA, sB = st["par"], st["sA"], st["sB"]
                st["tr"], st["tmr"], st["sumsq"] = [], [], []
                for tb in range(NTB):
                    sec = slice(tb * 512, (tb + 1) * 512)
                    tok_e = st["evac"][tb]
                    S, Q = 2 + tb % 2, 4 + tb % 2
                    tok_sq = P.do(SE, lambda e, sA=sA, sB=sB, sec=sec: e.activation(
                        out=sB[:, sec], in_=sA[:, sec], func=AF.Square),
                        waits=[tok_e, sfreeB[par][tb]], tick=True)
                    tma = t_mm(psh[S][0:1, :], ones_c[:], sA[:, sec],
                               start=True, stop=True,
                               waits=[tok_e, bank_free[S]], tick=True)
                    te1 = P.do(SE, lambda e, sec=sec, S=S: e.activation(
                        out=m_row[0:1, sec], in_=psh[S][0:1, :],
                        func=AF.Copy, scale=1.0 / 128.0),
                        waits=[tma, rows_free_m[tb]], tick=True)
                    bank_free[S] = te1
                    tmb = t_mm(psh[Q][0:1, :], ones_c[:], sB[:, sec],
                               start=True, stop=True,
                               waits=[tok_sq, bank_free[Q]], tick=True)
                    te2 = P.do(SE, lambda e, sec=sec, Q=Q: e.activation(
                        out=r_row[0:1, sec], in_=psh[Q][0:1, :],
                        func=AF.Copy, scale=1.0 / 128.0),
                        waits=[tmb, rows_free_r[tb]], tick=True)
                    bank_free[Q] = te2
                    ta = P.do(SE, lambda e, sec=sec, S=S: e.activation(
                        out=psh[S][0:1, :], in_=m_row[0:1, sec], func=AF.Square),
                        waits=[te1, bank_free[S]], tick=True)
                    tv = P.do(VE, lambda e, sec=sec, S=S: e.tensor_sub(
                        r_row[0:1, sec], r_row[0:1, sec], psh[S][0:1, :]),
                        waits=[ta, te2], tick=True)
                    bank_free[S] = tv
                    tsd = P.do(SE, lambda e, sec=sec: e.activation(
                        out=r_row[0:1, sec], in_=r_row[0:1, sec], func=AF.Ln,
                        bias=epsc[0:1, 1:2]), waits=[tv], tick=True)
                    tr = P.do(SE, lambda e, sec=sec: e.activation(
                        out=r_row[0:1, sec], in_=r_row[0:1, sec], func=AF.Exp,
                        scale=-0.5), waits=[tsd], tick=True)
                    tmr = P.do(VE, lambda e, sec=sec: e.tensor_mul(
                        m_row[0:1, sec], m_row[0:1, sec], r_row[0:1, sec]),
                        waits=[tr], tick=True)
                    st["tr"].append(tr)
                    st["tmr"].append(tmr)
                    st["sumsq"].append(tmb)

            def post_B(st):
                par, sA, sB = st["par"], st["sA"], st["sB"]
                ln_w, ln_b, dst = st["ln_w"], st["ln_b"], st["dst"]
                st["o"] = []
                for tb in range(NTB):
                    sec = slice(tb * 512, (tb + 1) * 512)
                    tabC = tabs[:, tb * 512:(tb + 1) * 512]
                    tabS = tabs[:, 2048 + tb * 512: 2048 + (tb + 1) * 512]
                    tok_q = st["evac"][tb]
                    S, R = 2 + tb % 2, 6 + tb % 2
                    if ln_w is not None:
                        t_r = t_mm(psh[R][:, :], onesf_r[:], r_row[0:1, sec],
                                   start=True, stop=True,
                                   waits=[st["tr"][tb], bank_free[R]], tick=True)
                        t_m = t_mm(psh[S][:, :], ones_r[:], m_row[0:1, sec],
                                   start=True, stop=True,
                                   waits=[st["tmr"][tb], bank_free[S]], tick=True)
                        rows_free_r[tb] = ("MULTI", [t_r, st["tmr"][tb]])
                        rows_free_m[tb] = t_m
                        tn1 = P.do(VE, lambda e, sA=sA, sec=sec, R=R: e.tensor_mul(
                            sA[:, sec], sA[:, sec], psh[R][:, :]),
                            waits=[t_r], tick=True)
                        tn2 = P.do(VE, lambda e, sA=sA, sec=sec, S=S: e.tensor_sub(
                            sA[:, sec], sA[:, sec], psh[S][:, :]),
                            waits=[t_m, tn1], tick=True)
                        bank_free[R] = tn2
                        bank_free[S] = tn2
                        tok_q = P.do(VE, lambda e, sA=sA, sec=sec, ln_w=ln_w, ln_b=ln_b:
                                     e.tensor_scalar(sA[:, sec], sA[:, sec],
                                                     ln_w[:, 0:1], ln_b[:, 0:1],
                                                     ALU.mult, ALU.add),
                                     waits=[tn2], tick=True)
                        swp_w = [tok_q, st["sumsq"][tb]]
                    else:
                        swp_w = [tok_q, sfreeB[par][tb]]
                    sw1 = P.do(VE, lambda e, sA=sA, sB=sB, sec=sec: e.tensor_copy(
                        sB[0:64, sec], sA[64:128, sec]),
                        waits=swp_w, tick=True)
                    sw2 = P.do(VE, lambda e, sA=sA, sB=sB, sec=sec: e.tensor_copy(
                        sB[64:128, sec], sA[0:64, sec]),
                        waits=[tok_q], tick=True)
                    tm1 = P.do(VE, lambda e, sA=sA, sec=sec, tabC=tabC: e.tensor_mul(
                        sA[:, sec], sA[:, sec], tabC),
                        waits=[tok_tabs, tok_q, sw1, sw2], tick=True)
                    tm2 = P.do(VE, lambda e, sB=sB, sec=sec, tabS=tabS: e.tensor_mul(
                        sB[:, sec], sB[:, sec], tabS),
                        waits=[sw1, sw2], tick=True)
                    tok_o = P.do(VE, lambda e, dst=dst, sA=sA, sB=sB, sec=sec:
                                 e.tensor_add(dst[:, sec], sA[:, sec], sB[:, sec]),
                                 waits=[tm1, tm2], tick=True)
                    sfreeA[par][tb] = tok_o
                    st["o"].append(tok_o)
                st["tok_o"] = st["o"][-1]

            def post_C(st):
                par, sB, dst, bound_idx = st["par"], st["sB"], st["dst"], st["bidx"]
                tmax = None
                for tb in range(NTB):
                    sec = slice(tb * 512, (tb + 1) * 512)
                    Q = 4 + tb % 2
                    vsq = P.do(SE, lambda e, dst=dst, sB=sB, sec=sec: e.activation(
                        out=sB[:, sec], in_=dst[:, sec], func=AF.Square),
                        waits=[st["o"][tb]], tick=True)
                    tmb2 = t_mm(psh[Q][0:1, :], ones_c[:], sB[:, sec],
                                start=True, stop=True,
                                waits=[vsq, bank_free[Q]], tick=True)
                    tred = P.do(VE, lambda e, Q=Q: e.reduce_max(
                        tiny[0:1, 32:33], psh[Q][0:1, :], axis=AX.X),
                        waits=[tmb2], tick=True)
                    bank_free[Q] = tred
                    sfreeB[par][tb] = tred
                    if tb == 0:
                        tmax = P.do(VE, lambda e, bi=bound_idx: e.tensor_copy(
                            tiny[0:1, bi:bi + 1], tiny[0:1, 32:33]), waits=[tred],
                            tick=True)
                    else:
                        tmax = P.do(VE, lambda e, bi=bound_idx: e.tensor_max(
                            tiny[0:1, bi:bi + 1], tiny[0:1, bi:bi + 1],
                            tiny[0:1, 32:33]), waits=[tred], tick=True)
                st["tmax"] = tmax

            slabs = []
            kst, qst = [], []
            for g in range(NKV):
                st = {"w_cols": wk_src[:, g * 128:(g + 1) * 128],
                      "dst": kT[:, g * 2048:(g + 1) * 2048],
                      "ln_w": knw_sb if is_b else None,
                      "ln_b": knb_sb if is_b else None, "bidx": 16 + g}
                slabs.append(st)
                kst.append(st)
            for h in range(NH):
                st = {"w_cols": wq_src[:, h * 128:(h + 1) * 128],
                      "dst": qhat[:, h * 2048:(h + 1) * 2048],
                      "ln_w": qnw_sb if is_b else None,
                      "ln_b": qnb_sb if is_b else None, "bidx": h}
                slabs.append(st)
                qst.append(st)
                if is_b:
                    slabs.append({"w_cols": wqkv_b[:, 2048 + h * 128: 2048 + (h + 1) * 128],
                                  "gate": h})
            stage_fns = [post_A, post_B, post_C] if is_b else [post_B, post_C]
            pend_stages = []
            point = [0]

            def pump():
                point[0] += 1
                emitted = 0
                for ent in pend_stages:
                    if emitted >= 3:
                        break
                    if ent[1] < len(stage_fns) and ent[2] < point[0]:
                        stage_fns[ent[1]](ent[0])
                        ent[1] += 1
                        ent[2] = point[0]
                        emitted += 1
                pend_stages[:] = [e for e in pend_stages if e[1] < len(stage_fns)]

            for st in slabs:
                emit_proj(st)
                pump()
                if st.get("gate") is None:
                    pend_stages.append([st, 0, point[0]])
            while pend_stages:
                pump()
            ktoks = [(st["tok_o"], st["tmax"]) for st in kst]
            qtoks = [(st["tok_o"], st["tmax"]) for st in qst]
            if _DBG == "k":
                dbg_dump(kT, nslab=4)
                return
            if _DBG == "q":
                dbg_dump(qhat)
                return

            gate_toks = [gstage_free[0], gstage_free[1]]

            bound_toks = [t for (_, t) in qtoks] + [t for (_, t) in ktoks]
            nkv_view = tiny[0:1, 16:20].unsqueeze(2).broadcast_to([1, 4, 4])
            nq_view = tiny[0:1, 0:16].rearrange("a (b c) -> a b c", b=4)
            out_view = tiny[0:1, 40:56].rearrange("a (b c) -> a b c", b=4)
            t_b1 = P.do(VE, lambda e: e.tensor_mul(out_view, nq_view, nkv_view),
                        waits=bound_toks, tick=True)
            t_b2 = P.do(SE, lambda e: e.activation(out=tiny[0:1, 40:56],
                                                   in_=tiny[0:1, 40:56], func=AF.Sqrt),
                        waits=[t_b1], tick=True)
            t_b3 = P.do(VE, lambda e: e.tensor_scalar_mul(tiny[0:1, 40:56],
                                                          tiny[0:1, 40:56], -SCALE),
                        waits=[t_b2], tick=True)
            t_b4 = t_mm(psh[2][:, 0:16], onesf_r[:], tiny[0:1, 40:56],
                        start=True, stop=True, waits=[t_b3, bank_free[2]], tick=True)
            tok_negB = P.do(SE, lambda e: e.activation(out=negB[:], in_=psh[2][:, 0:16],
                                                       func=AF.Copy),
                            waits=[t_b4], tick=True)
            bank_free[2] = tok_negB

            P.barrier()
            wo_toks = []
            for s4 in range(4):
                wo_toks.append(s_cast.dma(
                    wo_sb[:, s4 * 8192:(s4 + 1) * 8192].rearrange("p (s d) -> p s d", s=4),
                    wo_src[s4 * 512:(s4 + 1) * 512, :].rearrange("(s p) d -> p s d", p=128)))
            tok_wo = ("MULTI", wo_toks)

            plan = [kv_plan(qb, windowed=(not is_b)) for qb in range(NTB)]
            gate_ld_free = [None, None]
            ostage_free = [None, None]
            qb_list = [1] if _DBG == "attnout" else list(range(NTB))
            phat_free = [None, None, None]
            z16_free = [None]
            for qb in qb_list:
                att_toks = []
                np_count = 0

                def head_main(h):
                    nonlocal np_count
                    g = h // 4
                    av_bank = 4 + (h % 2)
                    zb = 6 if h % 2 == 0 else 3
                    ents = plan[qb]
                    nkv = len(ents)

                    def stage_a(kvi):
                        jt, a_lo, a_hi, mks = ents[kvi]
                        nonlocal np_count
                        cols = slice(a_lo * 128, (a_hi + 1) * 128)
                        sbank = kvi % 3
                        tok_s = t_mm(psh[sbank][:, cols],
                                     kT[:, g * 2048 + jt * 128: g * 2048 + (jt + 1) * 128],
                                     qhat[:, h * 2048 + qb * 512 + a_lo * 128:
                                          h * 2048 + qb * 512 + (a_hi + 1) * 128],
                                     start=True, stop=True,
                                     waits=[qtoks[h][0], ktoks[g][0], bank_free[sbank]],
                                     tick=True)
                        for (a, mid) in mks:
                            mc = slice((mid - 1) * 128, mid * 128)
                            tok_s = P.do(VE, lambda e, sbank=sbank, a=a, mc=mc: e.tensor_add(
                                psh[sbank][:, a * 128:(a + 1) * 128],
                                psh[sbank][:, a * 128:(a + 1) * 128], masks_sb[:, mc]),
                                waits=[tok_s, t_consts], tick=True)
                        pi = np_count % 3
                        np_count += 1
                        pt = phat[pi]
                        tok_e = P.do(SE, lambda e, sbank=sbank, pt=pt, cols=cols, h=h:
                                     e.activation(out=pt[:, cols], in_=psh[sbank][:, cols],
                                                  func=AF.Exp, scale=SCALE,
                                                  bias=negB[:, h:h + 1]),
                                     waits=[tok_s, tok_negB, phat_free[pi]], tick=True)
                        bank_free[sbank] = tok_e
                        return (kvi, pi, pt, cols, tok_e)

                    def stage_b(sa):
                        kvi, pi, pt, cols, tok_e = sa
                        jt = ents[kvi][0]
                        first = kvi == 0
                        is_last = kvi == nkv - 1
                        t_mm(psh[zb][0:1, cols], ones_c[:], pt[:, cols],
                             start=first, stop=is_last,
                             waits=[tok_e] + ([bank_free[zb]] if first else []), tick=False)
                        tok_av = t_mm(psh[av_bank][:, cols],
                                      vT[:, jt * 512 + g * 128: jt * 512 + (g + 1) * 128],
                                      pt[:, cols],
                                      start=first, stop=is_last,
                                      waits=[tok_e, tok_v] +
                                            ([bank_free[av_bank]] if first else []),
                                      tick=True)
                        phat_free[pi] = tok_av

                    from collections import deque
                    pend = deque()
                    for kvi in range(nkv):
                        pend.append(stage_a(kvi))
                        if len(pend) > 2:
                            stage_b(pend.popleft())
                    while pend:
                        stage_b(pend.popleft())
                    return (h, av_bank, zb, P.now(TE))

                def head_epi(hs):
                    h, av_bank, zb, tok_avdone = hs
                    tok_ln = P.do(SE, lambda e, zb=zb: e.activation(
                        out=z_row[:], in_=psh[zb][0:1, :], func=AF.Ln),
                        waits=[tok_avdone, z16_free[0]], tick=True)
                    bank_free[zb] = tok_ln
                    tok_zr = P.do(SE, lambda e: e.activation(
                        out=z16[:], in_=z_row[:], func=AF.Exp, scale=-1.0),
                        waits=[tok_ln], tick=True)
                    t_bz = t_mm(psh[7][:, :], ones_r[:], z16[:],
                                start=True, stop=True, waits=[tok_zr, bank_free[7]],
                                tick=True)
                    z16_free[0] = t_bz
                    rz = rzstage[h % 2]
                    tok_rz = P.do(SE, lambda e, rz=rz: e.activation(
                        out=rz, in_=psh[7][:, :], func=AF.Copy), waits=[t_bz], tick=True)
                    bank_free[7] = tok_rz
                    tok_ao = P.do(VE, lambda e, av_bank=av_bank, h=h, rz=rz: e.tensor_mul(
                        attnout[:, h * 512:(h + 1) * 512], psh[av_bank][:, :], rz),
                        waits=[tok_rz, tok_avdone], tick=True)
                    bank_free[av_bank] = tok_ao
                    att_toks.append(tok_ao)

                pend_h = None
                for h in range(NH):
                    hs = head_main(h)
                    if pend_h is not None:
                        head_epi(pend_h)
                    pend_h = hs
                head_epi(pend_h)

                if _DBG == "attnout":
                    dbg_dump(attnout, nslab=4)
                    return
                if is_b:
                    for s in range(NSLAB):
                        b = s % 2
                        tok_g = s_ld2.dma(gstage[b], gate_scr[s, :, qb, :],
                                          waits=[gate_ld_free[b]] + gate_toks)
                        tok_sg = P.do(SE, lambda e, b=b: e.activation(
                            out=gstage[b], in_=gstage[b], func=AF.Sigmoid),
                            waits=[tok_g, att_toks[s] if s < len(att_toks) else None],
                            tick=True)
                        tok_mulg = P.do(VE, lambda e, s=s, b=b: e.tensor_mul(
                            attnout[:, s * 512:(s + 1) * 512],
                            attnout[:, s * 512:(s + 1) * 512], gstage[b]),
                            waits=[tok_sg] + att_toks, tick=True)
                        gate_ld_free[b] = tok_mulg

                ao_ready = P.now(VE)
                for kd in range(NSLAB):
                    j = kd % 2
                    last_mm = None
                    for s in range(NSLAB):
                        last_mm = t_mm(psh[j][:, :],
                                       wo_sb[:, s * 2048 + kd * 128: s * 2048 + (kd + 1) * 128],
                                       attnout[:, s * 512:(s + 1) * 512],
                                       start=(s == 0), stop=(s == NSLAB - 1),
                                       waits=[tok_wo, ao_ready] +
                                             ([bank_free[j]] if s == 0 else []),
                                       tick=(s == NSLAB - 1))
                    ot = ostage[j]
                    tok_oe = P.do(SE, lambda e, j=j, ot=ot: e.activation(
                        out=ot, in_=psh[j][:, :], func=AF.Copy),
                        waits=[last_mm, ostage_free[j]], tick=True)
                    bank_free[j] = tok_oe
                    extra_acc = getattr(P, "first_acc_extra", None)
                    P.first_acc_extra = None
                    ostage_free[j] = s_acc.dma(
                        R_dm[kd * 128:(kd + 1) * 128, qb * 512:(qb + 1) * 512],
                        ot, waits=[tok_oe, extra_acc], accum_op=ALU.add)
            P.barrier()

        def emit_ffn(src_dm, extra_wait=()):
            tok_xn, xn_tb = emit_ln(src_dm, extra_wait=extra_wait)
            wgu_free = [None, None, None, None]
            wgu_u8_free = [tok_xn, tok_xn, tok_xn, tok_xn]
            h_free = [None, None, None]
            hb_pre = []
            nt = 0

            def emit_wgu(it):
                bg, bu = (it % 2) * 2, (it % 2) * 2 + 1
                tg8 = s_ld.dma(wgu_u8[bg][:, 0:2048], gw_blk[it].rearrange("p k i -> p (k i)"),
                               waits=[wgu_u8_free[bg]])
                tu8 = s_ld2.dma(wgu_u8[bu][:, 0:2048], uw_blk[it].rearrange("p k i -> p (k i)"),
                                waits=[wgu_u8_free[bu]])
                tok_wg = P.do(VE, lambda e, bg=bg: e.tensor_scalar_add(
                    wgu[bg], wgu_u8[bg][:, 0:2048], -128.0),
                    waits=[tg8, wgu_free[bg]], tick=True)
                wgu_u8_free[bg] = tok_wg
                tok_wu = P.do(VE, lambda e, bu=bu: e.tensor_scalar_add(
                    wgu[bu], wgu_u8[bu][:, 0:2048], -128.0),
                    waits=[tu8, wgu_free[bu]], tick=True)
                wgu_u8_free[bu] = tok_wu
                return tok_wg, tok_wu

            pend_gu = emit_wgu(0)
            for it in range(64):
                bg, bu = (it % 2) * 2, (it % 2) * 2 + 1
                tok_wg, tok_wu = pend_gu
                if it + 1 < 64:
                    pend_gu = emit_wgu(it + 1)
                for tb in range(NTB):
                    jg = tb % 2
                    ju = 2 + tb % 2
                    mg = mu = None
                    for k in range(NSLAB):
                        rhs = xn[:, k * 2048 + tb * 512: k * 2048 + (tb + 1) * 512]
                        mg = t_mm(psh[jg][:, :], wgu[bg][:, k * 128:(k + 1) * 128], rhs,
                                  start=(k == 0), stop=(k == NSLAB - 1),
                                  waits=[tok_wg, xn_tb[tb]] + ([bank_free[jg]] if k == 0 else []),
                                  tick=(k == NSLAB - 1))
                    for k in range(NSLAB):
                        rhs = xn[:, k * 2048 + tb * 512: k * 2048 + (tb + 1) * 512]
                        mu = t_mm(psh[ju][:, :], wgu[bu][:, k * 128:(k + 1) * 128], rhs,
                                  start=(k == 0), stop=(k == NSLAB - 1),
                                  waits=[tok_wu, xn_tb[tb]] + ([bank_free[ju]] if k == 0 else []),
                                  tick=(k == NSLAB - 1))
                    if tb == NTB - 1:
                        wgu_free[bg] = mg
                        wgu_free[bu] = mu
                    fs = fstage[tb % 2]
                    tok_sg = P.do(SE, lambda e, jg=jg, fs=fs, it=it: e.activation(
                        out=fs, in_=psh[jg][:, :], func=AF.Silu,
                        scale=gsc_sb[:, it:it + 1]),
                        waits=[mg, t_consts], tick=True)
                    bank_free[jg] = tok_sg
                    ht = hstage[nt % 3]
                    tok_h = P.do(VE, lambda e, ju=ju, fs=fs, ht=ht, it=it:
                                 e.scalar_tensor_tensor(ht, psh[ju][:, :],
                                                        usc_sb[:, it:it + 1], fs,
                                                        ALU.mult, ALU.mult),
                                 waits=[tok_sg, mu, h_free[nt % 3]], tick=True)
                    bank_free[ju] = tok_h
                    h_free[nt % 3] = s_st.dma(h_scr[it, :, tb, :], ht, waits=[tok_h])
                    nt += 1
                if it % 16 == 15:
                    q4 = it // 16
                    hb_pre.append(s_ld2.dma(
                        h_sb[:, q4 * 8192:(q4 + 1) * 8192].rearrange("p (it t) -> p it t", it=16),
                        h_scr[q4 * 16:(q4 + 1) * 16, :, 0, :].rearrange("it p t -> p it t"),
                        waits=[s_st.wait_all()]))
            P.barrier()
            if _DBG == "ffng":
                pass
            fstage_free = [None, None]
            wd_free = [None, None]
            wd_u8_free = [None, None]
            hbufs = [h_sb, h_sb2]
            hb_tok = {0: hb_pre}
            h_guard = [None, None]

            def emit_wd(g):
                kd = g % 16
                b = g % 2
                t8 = s_ld.dma(wd_u8[b][:, 0:8192],
                              dw_blk[kd].rearrange("p it d -> p (it d)"),
                              waits=[wd_u8_free[b]])
                tok_wd = P.do(VE, lambda e, b=b: e.tensor_scalar_add(
                    wd_buf[b], wd_u8[b][:, 0:8192], -128.0),
                    waits=[t8, wd_free[b]], tick=True)
                wd_u8_free[b] = tok_wd
                return tok_wd

            pend_wd = emit_wd(0)
            for g in range(4 * 16):
                tb, kd = divmod(g, 16)
                if kd == 0:
                    if tb + 1 < NTB:
                        nb = hbufs[(tb + 1) % 2]
                        toks = []
                        for q4 in range(4):
                            toks.append(s_ld2.dma(
                                nb[:, q4 * 8192:(q4 + 1) * 8192].rearrange("p (it t) -> p it t", it=16),
                                h_scr[q4 * 16:(q4 + 1) * 16, :, tb + 1, :].rearrange("it p t -> p it t"),
                                waits=[h_guard[(tb + 1) % 2]]))
                        hb_tok[tb + 1] = toks
                    hb_toks = hb_tok[tb]
                    hbuf = hbufs[tb % 2]
                    if _DBG == "ffnh" and tb == 0:
                        dbg_dump(h_sb)
                        return
                b = g % 2
                j = g % 2
                tok_wd = pend_wd
                if g + 1 < 64:
                    pend_wd = emit_wd(g + 1)
                last_mm = None
                for it in range(64):
                    last_mm = t_mm(psh[j][:, :],
                                   wd_buf[b][:, it * 128:(it + 1) * 128],
                                   hbuf[:, it * 512:(it + 1) * 512],
                                   start=(it == 0), stop=(it == 63),
                                   waits=[tok_wd, hb_toks[it // 16]] +
                                         ([bank_free[j]] if it == 0 else []),
                                   tick=(it == 63))
                wd_free[b] = last_mm
                ot = ostage[j]
                tok_oe = P.do(SE, lambda e, j=j, ot=ot, kd=kd: e.activation(
                    out=ot, in_=psh[j][:, :], func=AF.Copy,
                    scale=dsc_sb[:, kd:kd + 1]),
                    waits=[last_mm, fstage_free[j]], tick=True)
                bank_free[j] = tok_oe
                fstage_free[j] = s_acc.dma(
                    R_dm[kd * 128:(kd + 1) * 128, tb * 512:(tb + 1) * 512],
                    ot, waits=[tok_oe], accum_op=ALU.add)
                if kd == 15:
                    h_guard[tb % 2] = last_mm
            P.barrier()

        P.first_acc_extra = t_Rinit
        stage_list = _STAGE_LIST if _STAGE_LIST is not None else list(range(1, _STAGES + 1))
        for si, s in enumerate(stage_list):
            src0 = x_dm if si == 0 else R_dm
            ew = [] if si == 0 else [s_acc.wait_all()]
            if s == 1:
                emit_attn("a", src0, extra_wait=ew)
            elif s == 2:
                emit_ffn(src0, extra_wait=ew)
            elif s == 3:
                emit_attn("b", src0, extra_wait=ew)
            elif s == 4:
                emit_ffn(src0, extra_wait=ew)
            if _DBG is not None:
                break

        P.barrier()
        for e in ENGS:
            P.do(e, lambda ee: None, waits=[s_acc.wait_all(), s_st.wait_all(),
                                            s_cast.wait_all()])

        block = es.enter_context(nc.Block())
        P.replay(block)
    return nc


def kernel(**inputs):
    g = host_prep(inputs)
    x = np.asarray(inputs["L0_attn_a_x"], dtype=np.float32)
    nc = build_program()
    in_maps = []
    for c in range(8):
        m = dict(g)
        m["x_dm"] = np.ascontiguousarray(x[c].T)
        in_maps.append(m)
    res = run_bass_kernel_spmd(nc, in_maps, core_ids=list(range(8)))
    out = np.stack([np.asarray(r["out_dm"]).T for r in res.results], axis=0)
    return out
